# version 03
# baseline (speedup 1.0000x reference)
from contextlib import ExitStack

import numpy as np
import ml_dtypes

import concourse.bass as bass
import concourse.mybir as mybir
from concourse.bass_utils import run_bass_kernel_spmd

F32 = mybir.dt.float32
BF16 = mybir.dt.bfloat16
AF = mybir.ActivationFunctionType
ALU = mybir.AluOpType
AX = mybir.AxisListType

B, S, D, H, DH, M, DFF, I = 4, 4096, 1024, 16, 64, 128, 4096, 2816
TOK = 2048
NT = 16
NI = 22
ND = 8
NF = 32
NH = 16
NP = 8
QMAX = 127.0
MAGIC = 12582912.0
LN_EPS = 1e-5
LN1E6 = 1e6
FEAT_EPS = 1e-6
SQM = float(np.sqrt(M))
SCALE_QK = float(DH ** -0.25)

N_PHASES = 99
DEBUG_TAP = None
ENGINES = ("tensor", "vector", "scalar", "gpsimd", "sync")


class Prog:

    def __init__(self, nc, ctx):
        self.nc = nc
        self.recs = {e: [] for e in ENGINES}
        self.cnt = {}
        self.h = {}
        self.ctx = ctx
        self.dma_sems = set()

    def sem(self, name):
        if name not in self.h:
            self.h[name] = self.ctx.enter_context(self.nc.semaphore(name))
            self.cnt[name] = 0
        return self.h[name]

    def val(self, s):
        self.sem(s)
        return self.cnt.get(s, 0)

    def op(self, eng, meth, *a, w=(), inc=None, **k):
        for (s, v) in w:
            assert v <= self.cnt.get(s, 0), f"wait-on-future {s}:{v}>{self.cnt.get(s, 0)}"
            if s in self.dma_sems:
                assert v == self.cnt[s], f"partial dma wait {s}:{v}!={self.cnt[s]}"
            self.sem(s)
        if inc is not None:
            s, d = inc
            self.sem(s)
            if d == 16:
                self.dma_sems.add(s)
            self.cnt[s] = self.cnt.get(s, 0) + d
        self.recs[eng].append((meth, a, k, tuple(w), inc))

    def dma(self, eng, semname, out, in_, w=(), **kw):
        self.op(eng, "dma_start", out=out, in_=in_, w=w, inc=(semname, 16), **kw)
        return (semname, self.cnt[semname])

    def wdma(self, semname):
        return (semname, self.val(semname))

    def flush(self):
        recs, self.recs = self.recs, {e: [] for e in ENGINES}
        if not any(recs.values()):
            return

        def emitter(rs):
            def f(e):
                for (meth, a, k, w, inc) in rs:
                    for (s, v) in w:
                        e.wait_ge(self.h[s], v)
                    ins = getattr(e, meth)(*a, **k)
                    if inc is not None:
                        s, d = inc
                        ins.then_inc(self.h[s], d)
            return f

        with self.nc.Block() as blk:
            for eng in ENGINES:
                if recs[eng]:
                    getattr(blk, eng)(emitter(recs[eng]))


def build_nc():
    nc = bass.Bass("TRN2", target_bir_lowering=False)
    ctx = ExitStack()
    nc._keepalive_ctx = ctx
    p = Prog(nc, ctx)

    def din(name, shape):
        return nc.dram_tensor(name, list(shape), F32, kind="ExternalInput").ap()

    x_hbm = din("x0", (TOK, D))
    wq_h, wk_h, wv_h = din("Wq", (D, D)), din("Wk", (D, D)), din("Wv", (D, D))
    wo_h, omg_h = din("Wo", (D, D)), din("omega", (M, DH))
    w1_h, b1_h = din("W1", (D, DFF)), din("b1", (32, 128))
    w2_h, b2_h = din("W2", (DFF, D)), din("b2", (1, D))
    gam_h, bet_h = din("gamma", (1, D)), din("beta", (1, D))
    clx_h, cld_h = din("clip_x", (1, 1)), din("clip_down", (1, 1))
    wg_h, wu_h, wd_h = din("w_gate", (I, D)), din("w_up", (I, D)), din("w_down", (D, I))
    hm_h, cm_h = din("hmask", (1, 1)), din("cmask", (1, 1))

    out_hbm = nc.dram_tensor("out", [TOK, D], F32, kind="ExternalOutput").ap()
    dbg_hbm = nc.dram_tensor("dbg", [128, NT * D], BF16, kind="ExternalOutput").ap()
    dbg2_hbm = nc.dram_tensor("dbg2", [128, 192], F32, kind="ExternalOutput").ap()

    sc_wgt = nc.dram_tensor("sc_wgt", [NI, 128, ND * 128], BF16).ap()
    sc_wut = nc.dram_tensor("sc_wut", [NI, 128, ND * 128], BF16).ap()
    sc_wdt = nc.dram_tensor("sc_wdt", [NI, 128, ND * 128], BF16).ap()
    sc_h = nc.dram_tensor("sc_h", [NI, 128, TOK], F32).ap()
    sc_x1 = nc.dram_tensor("sc_x1", [NT, 128, D], F32).ap()
    sc_a = nc.dram_tensor("sc_a", [NT, 128, D], F32).ap()
    sc_x2 = nc.dram_tensor("sc_x2", [NT, 128, D], F32).ap()
    sc_x2ln = nc.dram_tensor("sc_x2ln", [NT, 128, D], F32).ap()
    sc_row = nc.dram_tensor("sc_row", [8, 128], F32).ap()
    cc_in = nc.dram_tensor("cc_in", [1, 8], F32).ap()
    cc_out = nc.dram_tensor("cc_out", [1, 8], F32).ap()
    ccs_in = nc.dram_tensor("ccs_in", [128, NH * 65], BF16).ap()
    ccs_out = nc.dram_tensor("ccs_out", [128, NH * 65], BF16).ap()

    offA = nc.bump_sbuf(65536)[0]
    offB = nc.bump_sbuf(65536)[0]
    offC = nc.bump_sbuf(65536)[0]

    def at(name, base, off_b, shape, dt):
        return nc.alloc_sbuf_tensor_at(name, list(shape), dt, offset=base + off_b)

    x_sb = at("x_sb", offA, 0, [128, NT * D], F32)
    kpfm = at("kpfm", offA, 0, [128, NH * TOK], BF16)
    w2h_sb = at("w2h_b", offA, 0, [128, 2, 16, D], BF16)

    vtm = at("vtm", offB, 0, [128, NT * 1024], BF16)
    wv_sb = at("wv_sb", offB, 32768, [128, ND, 1024], BF16)
    s_sb = at("s_sb", offB, 32768, [128, NH * 15 * 65], BF16)
    hq_sb = at("hq_sb", offB, 0, [128, NI * 1024], BF16)
    hrd_sb = at("hrd_sb", offB, 45056, [128, 2, 1024], F32)
    dn_tmp = at("dn_tmp", offB, 53248, [128, 2, 512], F32)
    xlrd_sb = at("xlrd_sb", offB, 8192, [128, 2, 1024], F32)
    b1row = at("b1row", offB, 40960, [32, 128], F32)
    w1c_sb = at("w1c_b", offB, 0, [128, 2, ND * 128], BF16)
    pacc_sb = at("pacc_sb", offB, 4096, [128, 4, D], F32)
    x1t_sb = at("x1t_sb", offB, 20480, [128, D], F32)
    x2t_sb = at("x2t_sb", offB, 24576, [128, D], F32)
    xlnt_sb = at("xlnt_sb", offB, 28672, [128, D], F32)
    silu_sb = at("silu_sb", offB, 0, [128, TOK], F32)
    hcmb_sb = at("hcmb_sb", offB, 8192, [128, 2, TOK], F32)
    a_tm = at("a_tm", offB, 0, [128, NT * D], F32)
    wo_sb = at("wo_sb", offB, 0, [128, ND, 1024], BF16)
    wo_a = at("wo_a", offB, 16384, [128, 2, D], F32)
    h2tm = at("h2tm", offB, 24576, [128, NT * D], BF16)
    t2_sb = at("t2_sb", offB, 57344, [128, D], F32)
    sqd_sb = at("sqd_sb", offB, 61440, [128, D], F32)

    xqtm = at("xqtm", offC, 0, [128, NT * D], BF16)
    xqfm = at("xqfm", offC, 32768, [128, ND * TOK], BF16)
    u1_sb = at("u1_sb", offB, 0, [128, 2, D], F32)
    wgub = at("wgub", offC, 0, [128, 2, 2, ND * 128], BF16)
    wdt_sb = at("wdt_sb", offC, 0, [128, NI * 1024], BF16)
    afm = at("afm", offC, 0, [128, ND * TOK], BF16)
    abf_sb = at("abf_sb", offC, 32768, [128, NT * D], BF16)
    kfmp = at("kfmp", offC, 32768, [128, TOK], BF16)
    e_pair = at("e_pair", offC, 36864, [128, NT * 256], BF16)
    qktm = at("qktm", offC, 45056, [128, 2 * TOK], BF16)
    qpfm = at("qpfm", offC, 53248, [128, 2 * TOK], BF16)
    h2fm = at("h2fm", offC, 0, [128, ND * TOK], BF16)
    rfm = at("rfm", offC, 32768, [128, NF * 512], BF16)
    asb = at("asb", offC, 63520, [128, 2, 128], BF16)
    ctrb = at("ctrb", offC, 61440, [128, NH * 65], BF16)
    scar = at("scar", offC, 61440, [128, NH * 65], BF16)
    wraw = at("wraw", offC, 0, [128, 2, I], F32)
    wqi = at("wqi", offC, 22528, [128, 2, I], F32)
    wtev = at("wtev", offC, 45056, [128, 2, I], BF16)

    def sb(name, shape, dt):
        return ctx.enter_context(nc.sbuf_tensor(name, list(shape), dt))

    gam_b = sb("gam_b", [128, D], BF16)
    bet_b = sb("bet_b", [128, D], BF16)
    b2_b = sb("b2_b", [128, D], BF16)
    swd_b = sb("swd_b", [128, D], BF16)
    b1_pp = sb("b1_pp", [128, 32], F32)
    swg_pp = sb("swg_pp", [128, NI], F32)
    swu_pp = sb("swu_pp", [128, NI], F32)
    tri_f = sb("tri_f", [128, 128], BF16)
    idf = sb("idf", [128, 128], F32)
    omgbd = sb("omgbd", [128, 256], BF16)
    ones_r = sb("ones_r", [1, 128], F32)
    ones_c = sb("ones_c", [128, 1], BF16)
    scal = sb("scal", [128, 16], F32)
    stats = sb("stats", [128, 136], F32)
    phst = sb("phst", [128, 64], F32)

    wkp_sb = sb("wkp_sb", [128, 2, ND, 128], BF16)
    tiny = sb("tiny", [128, 16], F32)
    nmag_pp = sb("nmag_pp", [128, 1], F32)
    pmag_pp = sb("pmag_pp", [128, 1], F32)
    eps5_pp = sb("eps5_pp", [128, 1], F32)
    eps6_pp = sb("eps6_pp", [128, 1], F32)
    feps_pp = sb("feps_pp", [128, 1], F32)
    zero_pp = sb("zero_pp", [128, 1], F32)
    onep = sb("onep", [1, 16], F32)


    SCL_SX, SCL_RX, SCL_RH, SCL_SH05, SCL_HM, SCL_CM = 0, 1, 2, 3, 4, 5
    SCL_GMX, SCL_SGX, SCL_SGD = 6, 8, 9
    ST_SUM, ST_SSQ, ST_NMU, ST_RSQ, ST_SPP, ST_M2, ST_RMX, ST_RMN = (
        0, 16, 32, 48, 64, 80, 96, 112)
    ST_T1, ST_T2 = 128, 132

    pb = [ctx.enter_context(nc.psum_tensor(f"pb{i}", [128, 512], F32)) for i in range(8)]

    def bcast_from_p0(src_ap, dst_ap, w=()):
        p.op("tensor", "matmul", pb[7][:, 0:1], ones_r[:, :], src_ap,
             start=True, stop=True, w=w + (("ae", p.val("ae")),), inc=("pe", 1))
        p.op("scalar", "copy", dst_ap, pb[7][:, 0:1],
             w=(("pe", p.val("pe")),), inc=("ae", 1))

    def part_max_to_p0(src_col_ap, dst_ap, w=()):
        p.op("tensor", "transpose", pb[7][0:1, 0:128], src_col_ap, idf[:, :],
             w=w + (("ve", p.val("ve")), ("ge", p.val("ge")), ("ae", p.val("ae"))),
             inc=("pe", 1))
        p.op("vector", "reduce_max", dst_ap, pb[7][0:1, 0:128], AX.X,
             w=(("pe", p.val("pe")),), inc=("ve", 1))

    def global_absmax_chain(run_col_ap, sig_col, sx_col, rx_col, sh05_col=None, w=()):
        part_max_to_p0(run_col_ap, onep[0:1, 0:1], w=w)
        p.op("vector", "tensor_copy", onep[0:1, 1:8], onep[0:1, 0:1].broadcast_to((1, 7)),
             inc=("ve", 1))
        p.dma("sync", "so", out=cc_in, in_=onep[0:1, 0:8], w=(("ve", p.val("ve")),))
        p.op("gpsimd", "collective_compute", "AllReduce", ALU.max,
             [list(range(8))], [cc_in], [cc_out], w=(p.wdma("so"),), inc=("cc", 1))
        p.dma("sync", "c0", out=onep[0:1, 8:16], in_=cc_out, w=(("cc", p.val("cc")),))
        bcast_from_p0(onep[0:1, 8:9], scal[:, SCL_GMX:SCL_GMX + 1], w=(p.wdma("c0"),))
        p.op("vector", "tensor_scalar", scal[:, 10:11],
             scal[:, sig_col:sig_col + 1], scal[:, SCL_GMX:SCL_GMX + 1], 1.0 / QMAX,
             ALU.mult, ALU.mult, w=(("ae", p.val("ae")),), inc=("ve", 1))
        p.op("scalar", "copy", scal[:, sx_col:sx_col + 1], scal[:, 10:11],
             w=(("ve", p.val("ve")),), inc=("ae", 1))
        p.op("scalar", "activation", scal[:, rx_col:rx_col + 1],
             scal[:, 10:11], AF.Ln, inc=("ae", 1))
        p.op("scalar", "activation", scal[:, rx_col:rx_col + 1],
             scal[:, rx_col:rx_col + 1], AF.Exp, scale=-1.0, inc=("ae", 1))
        if sh05_col is not None:
            p.op("scalar", "activation", scal[:, sh05_col:sh05_col + 1],
                 scal[:, 10:11], AF.Copy, scale=0.5, inc=("ae", 1))
        p.op("vector", "nop", w=(("ae", p.val("ae")),), inc=("ve", 1))

    def ln_stats_tiles(src_ap_fn, eps, with_minmax, w0=()):
        for t in range(NT):
            xt = src_ap_fn(t)
            ww = w0 if t == 0 else ()
            p.op("vector", "reduce_sum", stats[:, ST_SUM + t:ST_SUM + t + 1], xt, AX.X,
                 w=ww, inc=("ve", 1))
            p.op("scalar", "activation", sqd_sb[:, :], xt, AF.Square,
                 accum_out=stats[:, ST_SSQ + t:ST_SSQ + t + 1], w=ww, inc=("ae", 1))
        W = (("ve", p.val("ve")), ("ae", p.val("ae")), ("ge", p.val("ge")))
        sU, sQ = stats[:, ST_SUM:ST_SUM + 16], stats[:, ST_SSQ:ST_SSQ + 16]
        nmu, rsq = stats[:, ST_NMU:ST_NMU + 16], stats[:, ST_RSQ:ST_RSQ + 16]
        t1 = stats[:, ST_SPP:ST_SPP + 16]
        t2 = stats[:, ST_M2:ST_M2 + 16]
        p.op("gpsimd", "tensor_scalar", nmu, sU, -1.0 / D, None, ALU.mult, w=W, inc=("ge", 1))
        p.op("gpsimd", "tensor_scalar", t1, sQ, 1.0 / D, eps, ALU.mult, ALU.add, inc=("ge", 1))
        p.op("scalar", "activation", t2, nmu, AF.Square, w=(("ge", p.val("ge")),),
             inc=("ae", 1))
        p.op("vector", "tensor_sub", t1, t1, t2,
             w=(("ge", p.val("ge")), ("ae", p.val("ae"))), inc=("ve", 1))
        p.op("scalar", "activation", t2, t1, AF.Ln, w=(("ve", p.val("ve")),), inc=("ae", 1))
        p.op("scalar", "activation", rsq, t2, AF.Exp, scale=-0.5, inc=("ae", 1))
        if with_minmax:
            for t in range(NT):
                buf = t % 2
                p.op("vector", "tensor_scalar", u1_sb[:, buf, :], src_ap_fn(t),
                     stats[:, ST_NMU + t:ST_NMU + t + 1],
                     stats[:, ST_RSQ + t:ST_RSQ + t + 1], ALU.add, ALU.mult,
                     w=(("ae", p.val("ae")), ("ge", p.val("ge"))) if t == 0 else (),
                     inc=("ve", 1))
                p.op("vector", "tensor_reduce", stats[:, ST_RMX + t:ST_RMX + t + 1],
                     u1_sb[:, buf, :], AX.X, ALU.max, apply_absolute_value=True,
                     inc=("ve", 1))
            p.op("scalar", "copy", stats[:, ST_RMN:ST_RMN + 16],
                 stats[:, ST_RMX:ST_RMX + 16], w=(("ve", p.val("ve")),), inc=("ae", 1))
            p.op("vector", "reduce_max", tiny[:, 12:13], stats[:, ST_RMN:ST_RMN + 16],
                 AX.X, w=(("ae", p.val("ae")),), inc=("ve", 1))

    def dma_transpose_cmaj(src, dst, nseg, chunk=32):
        for s0 in range(0, nseg, chunk):
            n = min(chunk, nseg - s0)
            p.op("scalar", "dma_start_transpose",
                 dst[:, s0 * 128:(s0 + n) * 128].rearrange("p (s f) -> p s f", f=128),
                 src[:, s0 * 128:(s0 + n) * 128].rearrange("p (s f) -> p s f", f=128),
                 w=(("ve", p.val("ve")), ("ge", p.val("ge")), ("ae", p.val("ae")),
                    ("pe", p.val("pe"))),
                 inc=("t0", 16))

    def xtile(t):
        return x_sb[:, t * D:(t + 1) * D]

    def finish_dump_stream(dbg_src=None):
        if dbg_src is not None:
            p.dma("sync", "so", out=dbg_hbm[:, 0:dbg_src.shape[1]], in_=dbg_src[:, :],
                  w=(("ve", p.val("ve")), ("ge", p.val("ge")), ("ae", p.val("ae")),
                     ("pe", p.val("pe")), p.wdma("t0")))
            p.dma("sync", "so", out=dbg2_hbm[:, 0:16], in_=scal[:, :])
            p.dma("sync", "so", out=dbg2_hbm[:, 16:152], in_=stats[:, :])
            p.dma("sync", "so", out=dbg2_hbm[:, 152:168], in_=tiny[:, :])
            p.dma("sync", "so", out=dbg2_hbm[0:1, 168:184], in_=onep[:, :])
        for t in range(NT):
            p.dma("sync", "so", out=out_hbm.rearrange("(tt p) d -> p tt d", p=128)[:, t, :],
                  in_=xtile(t),
                  w=(("ve", p.val("ve")), ("ge", p.val("ge")), ("ae", p.val("ae")),
                     ("pe", p.val("pe"))))
        p.op("sync", "nop", w=(p.wdma("so"),))
        p.flush()

    p.dma("sync", "c0", out=x_sb[:, :].rearrange("p (t d) -> p t d", d=D),
          in_=x_hbm.rearrange("(t p) d -> p t d", p=128))
    p.dma("gpsimd", "g0", out=gam_b[:, :], in_=gam_h.partition_broadcast(128))
    p.dma("gpsimd", "g0", out=bet_b[:, :], in_=bet_h.partition_broadcast(128))
    p.dma("gpsimd", "g0", out=b2_b[:, :], in_=b2_h.partition_broadcast(128))
    p.dma("sync", "c1", out=tiny[0:1, 0:1], in_=clx_h)
    p.dma("sync", "c1", out=tiny[0:1, 1:2], in_=cld_h)
    p.dma("sync", "c1", out=tiny[0:1, 2:3], in_=hm_h)
    p.dma("sync", "c1", out=tiny[0:1, 3:4], in_=cm_h)
    p.dma("sync", "c1", out=b1row[:, :], in_=b1_h)
    p.dma("sync", "c1", out=wraw[0:128, 0, 0:64], in_=omg_h)

    p.op("gpsimd", "memset", idf[:, :], 1.0)
    p.op("gpsimd", "affine_select", idf[:, :], idf[:, :], [[-1, 128]], ALU.is_equal,
         0.0, base=0, channel_multiplier=1)
    p.op("gpsimd", "memset", tri_f[:, :], 1.0)
    p.op("gpsimd", "affine_select", tri_f[:, :], tri_f[:, :], [[1, 128]], ALU.is_ge,
         0.0, base=0, channel_multiplier=-1)
    p.op("gpsimd", "memset", ones_r[:, :], 1.0)
    p.op("gpsimd", "memset", ones_c[:, :], 1.0)
    p.op("gpsimd", "memset", omgbd[:, :], 0.0)
    p.op("gpsimd", "memset", onep[:, :], 0.0)
    p.op("gpsimd", "memset", nmag_pp[:, :], -MAGIC)
    p.op("gpsimd", "memset", pmag_pp[:, :], MAGIC)
    p.op("gpsimd", "memset", eps5_pp[:, :], LN_EPS)
    p.op("gpsimd", "memset", eps6_pp[:, :], LN1E6)
    p.op("gpsimd", "memset", feps_pp[:, :], FEAT_EPS)
    p.op("gpsimd", "memset", zero_pp[:, :], 0.0)
    p.op("gpsimd", "memset", scal[:, :], 0.0, inc=("ge", 1))

    GE0 = (("ge", p.val("ge")),)
    WC1 = (p.wdma("c1"),)

    bcast_from_p0(tiny[0:1, 2:3], scal[:, SCL_HM:SCL_HM + 1], w=WC1 + GE0)
    bcast_from_p0(tiny[0:1, 3:4], scal[:, SCL_CM:SCL_CM + 1])
    for (cidx, dcol) in ((0, SCL_SGX), (1, SCL_SGD)):
        bcast_from_p0(tiny[0:1, cidx:cidx + 1], scal[:, dcol:dcol + 1])
        p.op("scalar", "activation", scal[:, dcol:dcol + 1], scal[:, dcol:dcol + 1],
             AF.Exp, scale=-1.0, w=(("ve", p.val("ve")),), inc=("ae", 1))
        p.op("gpsimd", "tensor_scalar", scal[:, dcol:dcol + 1], scal[:, dcol:dcol + 1],
             1.0, None, ALU.add, w=(("ae", p.val("ae")),), inc=("ge", 1))
        p.op("scalar", "activation", scal[:, dcol:dcol + 1], scal[:, dcol:dcol + 1],
             AF.Ln, w=(("ge", p.val("ge")),), inc=("ae", 1))
        p.op("scalar", "activation", scal[:, dcol:dcol + 1], scal[:, dcol:dcol + 1],
             AF.Exp, scale=-1.0, inc=("ae", 1))

    p.op("tensor", "transpose", pb[6][:, 0:32], b1row[:, :], idf[0:32, 0:32],
         w=WC1 + GE0 + (("ve", p.val("ve")),), inc=("pe", 1))
    p.op("vector", "tensor_copy", b1_pp[:, :], pb[6][:, 0:32],
         w=(("pe", p.val("pe")),), inc=("ve", 1))

    p.op("tensor", "transpose", pb[5][0:64, 0:128], wraw[0:128, 0, 0:64], idf[:, :],
         w=WC1 + GE0, inc=("pe", 1))
    p.op("vector", "tensor_scalar", omgbd[0:64, 0:128], pb[5][0:64, 0:128], SCALE_QK,
         None, ALU.mult, w=(("pe", p.val("pe")),), inc=("ve", 1))
    p.op("vector", "tensor_copy", omgbd[64:128, 128:256], omgbd[0:64, 0:128],
         inc=("ve", 1))

    def quant_and_transpose(src_hbm, scr, sw_col_ap, ncols, dtile):
        buf = dtile % 2
        wv = (("ve", p.val("ve")), ("ae", p.val("ae")), ("ge", p.val("ge")),
              ("pe", p.val("pe")))
        tok_ld = p.dma("sync", "c" + str(buf), out=wraw[:, buf, 0:ncols],
                       in_=src_hbm, w=wv)
        WL = (p.wdma("c" + str(buf)),)
        p.op("vector", "tensor_reduce", tiny[:, 4 + buf:5 + buf], wraw[:, buf, 0:ncols],
             AX.X, ALU.max, apply_absolute_value=True, w=WL, inc=("ve", 1))
        p.op("gpsimd", "tensor_scalar", sw_col_ap, tiny[:, 4 + buf:5 + buf], 1.0 / QMAX,
             None, ALU.mult, w=(("ve", p.val("ve")),), inc=("ge", 1))
        p.op("scalar", "activation", tiny[:, 6 + buf:7 + buf], tiny[:, 4 + buf:5 + buf],
             AF.Ln, w=(("ve", p.val("ve")),), inc=("ae", 1))
        p.op("scalar", "activation", tiny[:, 6 + buf:7 + buf], tiny[:, 6 + buf:7 + buf],
             AF.Exp, scale=-1.0, inc=("ae", 1))
        p.op("vector", "tensor_scalar", wraw[:, buf, 0:ncols], wraw[:, buf, 0:ncols],
             tiny[:, 6 + buf:7 + buf], None, ALU.mult, w=(("ae", p.val("ae")),),
             inc=("ve", 1))
        p.op("vector", "tensor_scalar", wraw[:, buf, 0:ncols], wraw[:, buf, 0:ncols],
             QMAX, MAGIC, ALU.mult, ALU.add, inc=("ve", 1))
        p.op("scalar", "add", wraw[:, buf, 0:ncols], wraw[:, buf, 0:ncols], nmag_pp[:, :],
             w=(("ve", p.val("ve")),), inc=("ae", 1))
        p.op("vector", "tensor_scalar", wqi[:, buf, 0:ncols], wraw[:, buf, 0:ncols],
             -QMAX - 1.0, QMAX, ALU.max, ALU.min, w=(("ae", p.val("ae")),),
             inc=("ve", 1))
        vq = p.val("ve")
        nblk = ncols // 128
        aeq0 = p.val("ae")
        for k in range(nblk):
            bank = k % 4
            p.op("tensor", "transpose", pb[bank][:, 0:128],
                 wqi[:, buf, k * 128:(k + 1) * 128], idf[:, :],
                 w=(("ve", vq), ("ae", aeq0)), inc=("pe", 1))
            p.op("scalar", "copy", wtev[:, buf, k * 128:(k + 1) * 128], pb[bank][:, 0:128],
                 w=(("pe", p.val("pe")),), inc=("ae", 1))
        p.dma("sync", "so", out=scr, in_=wtev[:, buf, 0:nblk * 128],
              w=(("ae", p.val("ae")),))

    for i in range(NI):
        quant_and_transpose(wg_h[i * 128:(i + 1) * 128, :], sc_wgt[i],
                            swg_pp[:, i:i + 1], 1024, i)
    for i in range(NI):
        quant_and_transpose(wu_h[i * 128:(i + 1) * 128, :], sc_wut[i],
                            swu_pp[:, i:i + 1], 1024, i)
    for dt in range(ND):
        buf = dt % 2
        wv = (("ve", p.val("ve")), ("ae", p.val("ae")), ("ge", p.val("ge")),
              ("pe", p.val("pe")))
        p.dma("sync", "c" + str(buf), out=wraw[:, buf, :],
              in_=wd_h[dt * 128:(dt + 1) * 128, :], w=wv)
        WL = (p.wdma("c" + str(buf)),)
        p.op("vector", "tensor_reduce", tiny[:, 4 + buf:5 + buf], wraw[:, buf, :],
             AX.X, ALU.max, apply_absolute_value=True, w=WL, inc=("ve", 1))
        p.op("gpsimd", "tensor_scalar", stats[:, ST_RMX + dt:ST_RMX + dt + 1],
             tiny[:, 4 + buf:5 + buf], 1.0 / QMAX, None, ALU.mult,
             w=(("ve", p.val("ve")),), inc=("ge", 1))
        p.op("scalar", "activation", tiny[:, 6 + buf:7 + buf], tiny[:, 4 + buf:5 + buf],
             AF.Ln, w=(("ve", p.val("ve")),), inc=("ae", 1))
        p.op("scalar", "activation", tiny[:, 6 + buf:7 + buf], tiny[:, 6 + buf:7 + buf],
             AF.Exp, scale=-1.0, inc=("ae", 1))
        p.op("vector", "tensor_scalar", wraw[:, buf, :], wraw[:, buf, :],
             tiny[:, 6 + buf:7 + buf], None, ALU.mult, w=(("ae", p.val("ae")),),
             inc=("ve", 1))
        p.op("vector", "tensor_scalar", wraw[:, buf, :], wraw[:, buf, :],
             QMAX, MAGIC, ALU.mult, ALU.add, inc=("ve", 1))
        p.op("scalar", "add", wraw[:, buf, :], wraw[:, buf, :], nmag_pp[:, :],
             w=(("ve", p.val("ve")),), inc=("ae", 1))
        p.op("vector", "tensor_scalar", wqi[:, buf, :], wraw[:, buf, :],
             -QMAX - 1.0, QMAX, ALU.max, ALU.min, w=(("ae", p.val("ae")),),
             inc=("ve", 1))
        vq = p.val("ve")
        aeq0 = p.val("ae")
        for i in range(NI):
            bank = i % 4
            p.op("tensor", "transpose", pb[bank][:, 0:128],
                 wqi[:, buf, i * 128:(i + 1) * 128], idf[:, :],
                 w=(("ve", vq), ("ae", aeq0)), inc=("pe", 1))
            p.op("scalar", "copy", wtev[:, buf, i * 128:(i + 1) * 128], pb[bank][:, 0:128],
                 w=(("pe", p.val("pe")),), inc=("ae", 1))
        p.dma("sync", "so",
              out=sc_wdt[:, :, dt * 128:(dt + 1) * 128].transpose([1, 0, 2]),
              in_=wtev[:, buf, 0:NI * 128].rearrange("p (i f) -> p i f", f=128),
              w=(("ae", p.val("ae")),))

    p.op("tensor", "transpose", pb[6][0:8, 0:128], stats[:, ST_RMX:ST_RMX + 8],
         idf[:, :], w=(("ve", p.val("ve")),), inc=("pe", 1))
    p.op("vector", "tensor_copy", hrd_sb[0:8, 0, 0:128], pb[6][0:8, 0:128],
         w=(("pe", p.val("pe")),), inc=("ve", 1))
    p.dma("sync", "so", out=sc_row, in_=hrd_sb[0:8, 0, 0:128],
          w=(("ve", p.val("ve")),))
    p.dma("gpsimd", "g0", out=swd_b[:, :],
          in_=sc_row.rearrange("p f -> (p f)").unsqueeze(0).partition_broadcast(128),
          w=(p.wdma("so"),))
    if N_PHASES <= 0:
        p.flush(); return nc
    p.flush()

    ln_stats_tiles(lambda t: xtile(t), LN_EPS, with_minmax=True, w0=(p.wdma("c0"),))
    global_absmax_chain(tiny[:, 12:13], SCL_SGX, SCL_SX, SCL_RX)
    p.op("vector", "tensor_scalar", stats[:, ST_RMN:ST_RMN + 16],
         stats[:, ST_RSQ:ST_RSQ + 16], scal[:, SCL_RX:SCL_RX + 1], None, ALU.mult,
         w=(("ae", p.val("ae")),), inc=("ve", 1))
    p.op("scalar", "copy", stats[:, ST_SPP:ST_SPP + 16], stats[:, ST_RMN:ST_RMN + 16],
         w=(("ve", p.val("ve")),), inc=("ae", 1))

    def quant_tile_to_cmaj(src_ap, t, nmu_ap, spp_ap, dst_tm):
        buf = t % 2
        p.op("vector", "tensor_scalar", u1_sb[:, buf, :], src_ap, nmu_ap, spp_ap,
             ALU.add, ALU.mult, w=(("ae", p.val("ae")), ("ge", p.val("ge"))),
             inc=("ve", 1))
        p.op("scalar", "add", u1_sb[:, buf, :], u1_sb[:, buf, :], pmag_pp[:, :],
             w=(("ve", p.val("ve")),), inc=("ae", 1))
        p.op("vector", "tensor_scalar", u1_sb[:, buf, :], u1_sb[:, buf, :],
             -MAGIC, -QMAX - 1.0, ALU.add, ALU.max, w=(("ae", p.val("ae")),),
             inc=("ve", 1))
        p.op("vector", "tensor_scalar",
             dst_tm[:, :].rearrange("p (c f) -> p c f", f=TOK)[:, :, t * 128:(t + 1) * 128],
             u1_sb[:, buf, :].rearrange("p (c f) -> p c f", f=128),
             QMAX, None, ALU.min, inc=("ve", 1))

    for t in range(NT):
        quant_tile_to_cmaj(xtile(t), t, stats[:, ST_NMU + t:ST_NMU + t + 1],
                           stats[:, ST_SPP + t:ST_SPP + t + 1], xqtm)
    dma_transpose_cmaj(xqtm, xqfm, NT * ND)
    if N_PHASES <= 1:
        finish_dump_stream(dbg_src=xqfm if DEBUG_TAP == "xqfm" else xqtm)
        return nc
    p.flush()

    def ffn_front(call):
        WT = (p.wdma("t0"),)
        for i in range(2):
            p.dma("sync", "c" + str(i % 2), out=wgub[:, i % 2, 0, :], in_=sc_wgt[i])
            p.dma("sync", "c" + str(i % 2), out=wgub[:, i % 2, 1, :], in_=sc_wut[i])
        bank_free = [("ae", p.val("ae")), ("ae", p.val("ae")),
                     ("ve", p.val("ve")), ("ve", p.val("ve"))]
        for i in range(NI):
            bi = i % 2
            p.op("vector", "tensor_scalar", tiny[:, 10:11], swg_pp[:, i:i + 1],
                 scal[:, SCL_SX:SCL_SX + 1], None, ALU.mult,
                 w=(("ae", p.val("ae")), ("ge", p.val("ge"))), inc=("ve", 1))
            p.op("vector", "tensor_scalar", tiny[:, 13:14], swu_pp[:, i:i + 1],
                 scal[:, SCL_SX:SCL_SX + 1], None, ALU.mult, inc=("ve", 1))
            p.op("gpsimd", "tensor_copy", tiny[:, 11:12], tiny[:, 13:14],
                 w=(("ve", p.val("ve")),), inc=("ge", 1))
            sc_ve = ("ve", p.val("ve"))
            WLD = (p.wdma("c" + str(bi)),)
            for n in range(4):
                bg, bu = n % 2, 2 + n % 2
                for k in range(ND):
                    ww = (WT + WLD + (bank_free[bg],)) if k == 0 else ()
                    p.op("tensor", "matmul", pb[bg][:, :], wgub[:, bi, 0, k * 128:(k + 1) * 128],
                         xqfm[:, k * TOK + n * 512:k * TOK + (n + 1) * 512],
                         start=(k == 0), stop=(k == ND - 1), w=ww,
                         inc=("pe", 1) if k == ND - 1 else None)
                p.op("scalar", "activation", silu_sb[:, n * 512:(n + 1) * 512], pb[bg][:, :],
                     AF.Silu, scale=tiny[:, 10:11], w=(("pe", p.val("pe")), sc_ve),
                     inc=("ae", 1))
                bank_free[bg] = ("ae", p.val("ae"))
                for k in range(ND):
                    ww = (bank_free[bu],) if k == 0 else ()
                    p.op("tensor", "matmul", pb[bu][:, :], wgub[:, bi, 1, k * 128:(k + 1) * 128],
                         xqfm[:, k * TOK + n * 512:k * TOK + (n + 1) * 512],
                         start=(k == 0), stop=(k == ND - 1), w=ww,
                         inc=("pe", 1) if k == ND - 1 else None)
                wwh = (("pe", p.val("pe")), ("ae", p.val("ae")), ("ge", p.val("ge")))
                if i >= 2 and n == 0:
                    wwh = wwh + (p.wdma("so"),)
                p.op("vector", "scalar_tensor_tensor", hcmb_sb[:, bi, n * 512:(n + 1) * 512],
                     pb[bu][:, :], tiny[:, 11:12], silu_sb[:, n * 512:(n + 1) * 512],
                     ALU.mult, ALU.mult, w=wwh, inc=("ve", 1))
                bank_free[bu] = ("ve", p.val("ve"))
            p.op("vector", "tensor_reduce", stats[:, i:i + 1], hcmb_sb[:, bi, :], AX.X,
                 ALU.max, apply_absolute_value=True, inc=("ve", 1))
            p.dma("sync", "so", out=sc_h[i], in_=hcmb_sb[:, bi, :],
                  w=(("ve", p.val("ve")),))
            if i + 2 < NI:
                p.dma("sync", "c" + str(i % 2), out=wgub[:, i % 2, 0, :], in_=sc_wgt[i + 2],
                      w=(("pe", p.val("pe")),))
                p.dma("sync", "c" + str(i % 2), out=wgub[:, i % 2, 1, :], in_=sc_wut[i + 2])
        p.op("scalar", "copy", stats[:, 32:32 + NI], stats[:, 0:NI],
             w=(("ve", p.val("ve")),), inc=("ae", 1))
        p.op("vector", "reduce_max", tiny[:, 8:9], stats[:, 32:32 + NI], AX.X,
             w=(("ae", p.val("ae")),), inc=("ve", 1))
        global_absmax_chain(tiny[:, 8:9], SCL_SGD, 7, SCL_RH, sh05_col=SCL_SH05)

    def ffn_down(call):
        for i in range(NI):
            p.dma("sync", "c" + str(i % 2), out=wdt_sb[:, i * 1024:(i + 1) * 1024],
                  in_=sc_wdt[i], w=(("pe", p.val("pe")),) if i < 2 else ())
        for half in range(2):
            for i in range(NI):
                bi = i % 2
                ww = (("ve", p.val("ve")),)
                if half == 0 and i < 2:
                    ww = (p.wdma("so"),)
                p.dma("sync", "c" + str(bi), out=hrd_sb[:, bi, :],
                      in_=sc_h[i][:, half * 1024:(half + 1) * 1024], w=ww)
                p.op("vector", "tensor_scalar", hrd_sb[:, bi, :], hrd_sb[:, bi, :],
                     scal[:, SCL_RH:SCL_RH + 1], MAGIC, ALU.mult, ALU.add,
                     w=(p.wdma("c" + str(bi)),), inc=("ve", 1))
                p.op("scalar", "add", hrd_sb[:, bi, :], hrd_sb[:, bi, :], nmag_pp[:, :],
                     w=(("ve", p.val("ve")),), inc=("ae", 1))
                p.op("vector", "tensor_scalar", hq_sb[:, i * 1024:(i + 1) * 1024],
                     hrd_sb[:, bi, :], -QMAX - 1.0, QMAX, ALU.max, ALU.min,
                     w=(("ae", p.val("ae")),), inc=("ve", 1))
            VQ = (("ve", p.val("ve")),)
            for t8 in range(8):
                t = half * 8 + t8
                for n in range(2):
                    bank = (t8 * 2 + n) % 4
                    for i in range(NI):
                        ww = VQ if i == 0 else ()
                        p.op("tensor", "matmul", pb[bank][:, :],
                             hq_sb[:, i * 1024 + t8 * 128:i * 1024 + (t8 + 1) * 128],
                             wdt_sb[:, i * 1024 + n * 512:i * 1024 + (n + 1) * 512],
                             start=(i == 0), stop=(i == NI - 1), w=ww,
                             inc=("pe", 1) if i == NI - 1 else None)
                    bt = bank % 2
                    p.op("vector", "scalar_tensor_tensor", dn_tmp[:, bt, :], pb[bank][:, :],
                         scal[:, SCL_SH05:SCL_SH05 + 1], swd_b[:, n * 512:(n + 1) * 512],
                         ALU.mult, ALU.mult, w=(("pe", p.val("pe")),), inc=("ve", 1))
                    p.op("vector", "tensor_add",
                         x_sb[:, t * D + n * 512:t * D + (n + 1) * 512],
                         x_sb[:, t * D + n * 512:t * D + (n + 1) * 512],
                         dn_tmp[:, bt, :], inc=("ve", 1))
                if call == 1:
                    p.dma("sync", "so", out=out_hbm.rearrange("(tt p) d -> p tt d", p=128)[:, t, :],
                          in_=xtile(t), w=(("ve", p.val("ve")),))

    ffn_front(0)
    if N_PHASES <= 2:
        finish_dump_stream(); return nc
    p.flush()
    ffn_down(0)
    if N_PHASES <= 3:
        finish_dump_stream(); return nc
    p.flush()

    ln_stats_tiles(lambda t: xtile(t), LN_EPS, with_minmax=False)
    for t in range(NT):
        p.op("vector", "tensor_scalar", a_tm[:, t * D:(t + 1) * D], xtile(t),
             stats[:, ST_NMU + t:ST_NMU + t + 1], stats[:, ST_RSQ + t:ST_RSQ + t + 1],
             ALU.add, ALU.mult,
             w=(("ae", p.val("ae")), ("ge", p.val("ge"))) if t == 0 else (),
             inc=("ve", 1))
        p.op("vector", "tensor_scalar",
             abf_sb[:, :].rearrange("p (c f) -> p c f", f=TOK)[:, :, t * 128:(t + 1) * 128],
             xtile(t).rearrange("p (c f) -> p c f", f=128),
             stats[:, ST_NMU + t:ST_NMU + t + 1], stats[:, ST_RSQ + t:ST_RSQ + t + 1],
             ALU.add, ALU.mult, inc=("ve", 1))
    p.dma("sync", "so", out=sc_a.transpose([1, 0, 2]),
          in_=a_tm[:, :].rearrange("p (t d) -> p t d", d=D), w=(("ve", p.val("ve")),))
    p.dma("sync", "so", out=sc_x1.transpose([1, 0, 2]),
          in_=x_sb[:, :].rearrange("p (t d) -> p t d", d=D))
    dma_transpose_cmaj(abf_sb, afm, NT * ND)
    if N_PHASES <= 4:
        finish_dump_stream(dbg_src=afm)
        return nc
    p.flush()

    p.dma("gpsimd", "g0", out=wv_sb[:, :, :],
          in_=wv_h.rearrange("(k pp) j -> pp k j", pp=128))

    def load_wpair(w_h, pr, ww=()):
        p.dma("gpsimd", "g1", out=wkp_sb[:, pr % 2, :, :],
              in_=w_h.rearrange("(k pp) (pr j) -> pp k pr j", pp=128, j=128)[:, :, pr, :],
              w=ww)
    load_wpair(wk_h, 0)
    load_wpair(wk_h, 1)
    WT0 = (p.wdma("t0"),)
    for t in range(NT):
        for n in range(2):
            bank = (2 * t + n) % 4
            for k in range(ND):
                ww = (WT0 + (p.wdma("g0"), ("ae", p.val("ae")))) if k == 0 else ()
                p.op("tensor", "matmul", pb[bank][:, :],
                     afm[:, k * TOK + t * 128:k * TOK + (t + 1) * 128],
                     wv_sb[:, k, n * 512:(n + 1) * 512],
                     start=(k == 0), stop=(k == ND - 1), w=ww,
                     inc=("pe", 1) if k == ND - 1 else None)
            p.op("scalar", "copy", vtm[:, t * 1024 + n * 512:t * 1024 + (n + 1) * 512],
                 pb[bank][:, :], w=(("pe", p.val("pe")),), inc=("ae", 1))

    def qk_phi(pr, wsem):
        for n in range(4):
            bank = n % 2
            for k in range(ND):
                ww = ((p.wdma(wsem), ("ae", p.val("ae")), ("ve", p.val("ve")))
                      if k == 0 else ())
                p.op("tensor", "matmul", pb[bank][:, :],
                     wkp_sb[:, pr % 2, k, :],
                     afm[:, k * TOK + n * 512:k * TOK + (n + 1) * 512],
                     start=(k == 0), stop=(k == ND - 1), w=ww,
                     inc=("pe", 1) if k == ND - 1 else None)
            p.op("scalar", "copy", kfmp[:, n * 512:(n + 1) * 512], pb[bank][:, :],
                 w=(("pe", p.val("pe")),), inc=("ae", 1))
        AEK = (("ae", p.val("ae")),)
        for t in range(NT):
            bank = 4 + t % 2
            p.op("tensor", "matmul", pb[bank][:, 0:256],
                 kfmp[:, t * 128:(t + 1) * 128], omgbd[:, :], start=True, stop=True,
                 w=AEK + (("ae", p.val("ae")),), inc=("pe", 1))
            wwx = (("pe", p.val("pe")),)
            if t == 0:
                wwx = wwx + (("ve", p.val("ve")),)
            p.op("scalar", "activation", e_pair[:, t * 256:(t + 1) * 256],
                 pb[bank][:, 0:256], AF.Exp, w=wwx, inc=("ae", 1))
        AEE = (("ae", p.val("ae")),)
        for h01 in range(2):
            p.op("vector", "tensor_reduce", phst[:, h01 * 16:(h01 + 1) * 16],
                 e_pair[:, :].rearrange("p (t f) -> p t f", f=256)
                 [:, :, h01 * 128:(h01 + 1) * 128],
                 AX.X, ALU.max, w=AEE, inc=("ve", 1))
        p.op("scalar", "activation", phst[:, 32:64], phst[:, 0:32], AF.Ln,
             scale=SQM, w=(("ve", p.val("ve")),), inc=("ae", 1))
        p.op("scalar", "activation", phst[:, 32:64], phst[:, 32:64], AF.Exp,
             scale=-1.0, inc=("ae", 1))
        AEF = (("ae", p.val("ae")),)
        for h01 in range(2):
            for t in range(NT):
                wwq = ()
                if h01 == 0 and t == 0:
                    wwq = AEF + (("pe", p.val("pe")), p.wdma("t0"))
                p.op("vector", "tensor_scalar",
                     qktm[:, h01 * TOK + t * 128:h01 * TOK + (t + 1) * 128],
                     e_pair[:, t * 256 + h01 * 128:t * 256 + (h01 + 1) * 128],
                     phst[:, 32 + h01 * 16 + t:33 + h01 * 16 + t], FEAT_EPS,
                     ALU.mult, ALU.add, w=wwq, inc=("ve", 1))

    for pr in range(NP):
        qk_phi(pr, "g1")
        if pr + 2 < NP:
            load_wpair(wk_h, pr + 2, ww=(("pe", p.val("pe")),))
        dma_transpose_cmaj(qktm, kpfm[:, 2 * pr * TOK:(2 * pr + 2) * TOK], 32)
        sbank_free = [("ae", p.val("ae")), ("ae", p.val("ae"))]
        for c in range(NT):
            for h01 in range(2):
                h = 2 * pr + h01
                pbS = pb[6 + h01]
                ww = (sbank_free[h01],) + ((("ve", p.val("ve")),) if c == 0 else ())
                p.op("tensor", "matmul", pbS[:, 0:64],
                     qktm[:, h01 * TOK + c * 128:h01 * TOK + (c + 1) * 128],
                     vtm[:, c * 1024 + h * 64:c * 1024 + (h + 1) * 64],
                     start=(c == 0), stop=False, w=ww)
                p.op("tensor", "matmul", pbS[:, 64:65],
                     qktm[:, h01 * TOK + c * 128:h01 * TOK + (c + 1) * 128],
                     ones_c[:, :], start=False, stop=True, inc=("pe", 1))
            for h01 in range(2):
                h = 2 * pr + h01
                pbS = pb[6 + h01]
                if c < 15:
                    p.op("scalar", "copy", s_sb[:, (h * 15 + c) * 65:(h * 15 + c + 1) * 65],
                         pbS[:, 0:65], w=(("pe", p.val("pe")),), inc=("ae", 1))
                else:
                    p.op("scalar", "activation", ctrb[:, h * 65:(h + 1) * 65],
                         pbS[:, 0:65], AF.Copy, scale=scal[:, SCL_CM:SCL_CM + 1],
                         w=(("pe", p.val("pe")),), inc=("ae", 1))
                sbank_free[h01] = ("ae", p.val("ae"))
    p.dma("sync", "so", out=ccs_in, in_=ctrb[:, :], w=(("ae", p.val("ae")),))
    p.op("gpsimd", "collective_compute", "AllReduce", ALU.add,
         [[0, 1], [2, 3], [4, 5], [6, 7]], [ccs_in], [ccs_out],
         w=(p.wdma("so"),), inc=("cc", 1))
    p.dma("sync", "c0", out=scar[:, :], in_=ccs_out, w=(("cc", p.val("cc")),))
    p.op("vector", "tensor_scalar", scar[:, :], scar[:, :],
         scal[:, SCL_HM:SCL_HM + 1], None, ALU.mult, w=(p.wdma("c0"),), inc=("ve", 1))
    if N_PHASES <= 5:
        finish_dump_stream(dbg_src=s_sb[:, 0:15600] if DEBUG_TAP == "s_sb"
                           else kpfm[:, 0:NT * D])
        return nc
    p.flush()

    load_wpair(wq_h, 0)
    load_wpair(wq_h, 1)
    for pr in range(NP):
        qk_phi(pr, "g1")
        if pr + 2 < NP:
            load_wpair(wq_h, pr + 2, ww=(("pe", p.val("pe")),))
        dma_transpose_cmaj(qktm, qpfm, 32)
        WTQ = (p.wdma("t0"),)
        abank_free = [("ve", p.val("ve")), ("ve", p.val("ve"))]
        nbank_free = [("ve", p.val("ve")), ("ve", p.val("ve"))]
        for c in range(NT):
            for h01 in range(2):
                h = 2 * pr + h01
                p.op("tensor", "matmul", pb[h01][:, 0:128],
                     kpfm[:, h * TOK + c * 128:h * TOK + (c + 1) * 128],
                     qpfm[:, h01 * TOK + c * 128:h01 * TOK + (c + 1) * 128],
                     start=True, stop=True,
                     w=(abank_free[h01],) + (WTQ if c == 0 else ()), inc=("pe", 1))
                p.op("vector", "tensor_tensor", asb[:, h01, :], pb[h01][:, 0:128],
                     tri_f[:, :], ALU.mult, w=(("pe", p.val("pe")),), inc=("ve", 1))
                abank_free[h01] = ("ve", p.val("ve"))
            for h01 in range(2):
                h = 2 * pr + h01
                pbN = pb[2 + h01]
                p.op("tensor", "matmul", pbN[:, 0:64], asb[:, h01, :],
                     vtm[:, c * 1024 + h * 64:c * 1024 + (h + 1) * 64],
                     start=True, stop=False,
                     w=(nbank_free[h01], ("ve", p.val("ve"))))
                p.op("tensor", "matmul", pbN[:, 64:65], asb[:, h01, :], ones_c[:, :],
                     start=False, stop=False)
                if c > 0:
                    p.op("tensor", "matmul", pbN[:, 0:65],
                         qpfm[:, h01 * TOK + c * 128:h01 * TOK + (c + 1) * 128],
                         s_sb[:, (h * 15 + c - 1) * 65:(h * 15 + c) * 65],
                         start=False, stop=False)
                p.op("tensor", "matmul", pbN[:, 0:65],
                     qpfm[:, h01 * TOK + c * 128:h01 * TOK + (c + 1) * 128],
                     scar[:, h * 65:(h + 1) * 65], start=False, stop=True,
                     inc=("pe", 1))
                p.op("scalar", "activation", tiny[:, 14 + h01:15 + h01], pbN[:, 64:65],
                     AF.Ln, bias=feps_pp[:, :], w=(("pe", p.val("pe")),), inc=("ae", 1))
                p.op("scalar", "activation", tiny[:, 14 + h01:15 + h01], tiny[:, 14 + h01:15 + h01],
                     AF.Exp, scale=-1.0, inc=("ae", 1))
                p.op("vector", "tensor_scalar",
                     kpfm[:, 2 * pr * TOK + c * 128 + h01 * 64:
                          2 * pr * TOK + c * 128 + (h01 + 1) * 64],
                     pbN[:, 0:64], tiny[:, 14 + h01:15 + h01], None, ALU.mult,
                     w=(("ae", p.val("ae")),), inc=("ve", 1))
                nbank_free[h01] = ("ve", p.val("ve"))
    if N_PHASES <= 6:
        finish_dump_stream(dbg_src=kpfm[:, 0:NT * D])
        return nc
    p.flush()

    for pr in range(NP):
        dma_transpose_cmaj(kpfm[:, 2 * pr * TOK:(2 * pr + 1) * TOK],
                           afm[:, pr * TOK:(pr + 1) * TOK], 16)
    p.dma("gpsimd", "g0", out=wo_sb[:, :, :],
          in_=wo_h.rearrange("(k pp) j -> pp k j", pp=128))
    WT7 = (p.wdma("t0"), p.wdma("g0"))
    if N_PHASES == 65:
        finish_dump_stream(dbg_src=afm)
        return nc
    for t in range(NT):
        p.dma("sync", "c" + str(t % 2), out=wo_a[:, t % 2, :],
              in_=sc_a[t], w=(("ve", p.val("ve")),))
        for n in range(2):
            bank = (2 * t + n) % 4
            for k in range(ND):
                ww = (WT7 + (("ve", p.val("ve")),)) if k == 0 else ()
                p.op("tensor", "matmul", pb[bank][:, :],
                     afm[:, k * TOK + t * 128:k * TOK + (t + 1) * 128],
                     wo_sb[:, k, n * 512:(n + 1) * 512],
                     start=(k == 0), stop=(k == ND - 1), w=ww,
                     inc=("pe", 1) if k == ND - 1 else None)
            p.op("vector", "tensor_add", t2_sb[:, n * 512:(n + 1) * 512],
                 pb[bank][:, :], wo_a[:, t % 2, n * 512:(n + 1) * 512],
                 w=(("pe", p.val("pe")), p.wdma("c" + str(t % 2))), inc=("ve", 1))
        p.op("vector", "reduce_sum", tiny[:, 4:5], t2_sb[:, :], AX.X, inc=("ve", 1))
        p.op("scalar", "activation", sqd_sb[:, :], t2_sb[:, :], AF.Square,
             accum_out=tiny[:, 5:6], w=(("ve", p.val("ve")),), inc=("ae", 1))
        p.op("gpsimd", "tensor_scalar", tiny[:, 6:7], tiny[:, 4:5], -1.0 / D, None,
             ALU.mult, w=(("ve", p.val("ve")),), inc=("ge", 1))
        p.op("gpsimd", "tensor_scalar", tiny[:, 7:8], tiny[:, 5:6], 1.0 / D, LN1E6,
             ALU.mult, ALU.add, w=(("ae", p.val("ae")),), inc=("ge", 1))
        p.op("scalar", "activation", tiny[:, 8:9], tiny[:, 6:7], AF.Square,
             w=(("ge", p.val("ge")),), inc=("ae", 1))
        p.op("vector", "tensor_sub", tiny[:, 9:10], tiny[:, 7:8], tiny[:, 8:9],
             w=(("ge", p.val("ge")), ("ae", p.val("ae"))), inc=("ve", 1))
        p.op("scalar", "activation", tiny[:, 10:11], tiny[:, 9:10], AF.Ln,
             w=(("ve", p.val("ve")),), inc=("ae", 1))
        p.op("scalar", "activation", tiny[:, 10:11], tiny[:, 10:11], AF.Exp,
             scale=-0.5, inc=("ae", 1))
        p.op("vector", "tensor_scalar", sqd_sb[:, :], t2_sb[:, :], tiny[:, 6:7],
             tiny[:, 10:11], ALU.add, ALU.mult,
             w=(("ge", p.val("ge")), ("ae", p.val("ae"))), inc=("ve", 1))
        p.op("vector", "tensor_tensor", t2_sb[:, :], sqd_sb[:, :], gam_b[:, :],
             ALU.mult, inc=("ve", 1))
        p.op("vector", "tensor_tensor",
             h2tm[:, :].rearrange("p (c f) -> p c f", f=TOK)[:, :, t * 128:(t + 1) * 128],
             t2_sb[:, :].rearrange("p (c f) -> p c f", f=128),
             bet_b[:, :].rearrange("p (c f) -> p c f", f=128), ALU.add, inc=("ve", 1))
    dma_transpose_cmaj(h2tm, h2fm, NT * ND)
    if N_PHASES <= 7:
        finish_dump_stream(dbg_src=afm if DEBUG_TAP == "afm" else h2fm)
        return nc
    p.flush()

    def w1_load(q4, f):
        gs = "g" + str(2 + f % 2)
        p.dma("gpsimd", gs, out=w1c_sb[:, f % 2, :],
              in_=w1_h.rearrange("(k pp) (ff c) -> pp k ff c", pp=128, c=128)
              [:, :, f, :], w=(("pe", p.val("pe")),))

    def w2_load(q4, fh):
        for fj in range(16):
            f = fh * 16 + fj
            p.dma("gpsimd", "g" + str(fh % 2),
                  out=w2h_sb[:, fh % 2, fj, :], in_=w2_h[f * 128:(f + 1) * 128, :],
                  w=(("pe", p.val("pe")),) if fj == 0 else ())

    w2_load(0, 0)
    for q4 in range(4):
        w1_load(q4, 0)
        w1_load(q4, 1)
        for f in range(NF):
            gs = "g" + str(2 + f % 2)
            WL1 = (p.wdma(gs), ("ae", p.val("ae")), p.wdma("t0"), ("ve", p.val("ve")))
            for k in range(ND):
                p.op("tensor", "matmul", pb[f % 4][:, :],
                     w1c_sb[:, f % 2, k * 128:(k + 1) * 128],
                     h2fm[:, k * TOK + q4 * 512:k * TOK + (q4 + 1) * 512],
                     start=(k == 0), stop=(k == ND - 1), w=WL1 if k == 0 else (),
                     inc=("pe", 1) if k == ND - 1 else None)
            if f + 2 < NF:
                w1_load(q4, f + 2)
            p.op("scalar", "activation", rfm[:, f * 512:(f + 1) * 512], pb[f % 4][:, :],
                 AF.Relu, bias=b1_pp[:, f:f + 1], w=(("pe", p.val("pe")),), inc=("ae", 1))
        for fh in range(2):
            if fh == 0:
                w2_load(q4, 1)
            elif q4 < 3:
                w2_load(q4 + 1, 0)
            WL2 = (p.wdma("g" + str(fh % 2)), ("ae", p.val("ae")), ("ve", p.val("ve")))
            for tq4 in range(4):
                for n in range(2):
                    for fj in range(16):
                        f = fh * 16 + fj
                        p.op("tensor", "matmul", pb[tq4 * 2 + n][:, :],
                             rfm[:, f * 512 + tq4 * 128:f * 512 + (tq4 + 1) * 128],
                             w2h_sb[:, fh % 2, fj, n * 512:(n + 1) * 512],
                             start=(fj == 0), stop=(fj == 15), w=WL2 if fj == 0 else (),
                             inc=("pe", 1) if fj == 15 else None)
                    if fh == 0:
                        p.op("vector", "tensor_copy", pacc_sb[:, tq4, n * 512:(n + 1) * 512],
                             pb[tq4 * 2 + n][:, :], w=(("pe", p.val("pe")),),
                             inc=("ve", 1))
        PEQ = (("pe", p.val("pe")),)
        for tq4 in range(4):
            t = 4 * q4 + tq4
            p.dma("sync", "c" + str(t % 2), out=x1t_sb[:, :], in_=sc_x1[t],
                  w=(("ve", p.val("ve")),))
            for n in range(2):
                wwe = PEQ
                if n == 0:
                    wwe = wwe + (p.wdma("c" + str(t % 2)), p.wdma("so"))
                p.op("vector", "tensor_add", x2t_sb[:, n * 512:(n + 1) * 512],
                     pb[tq4 * 2 + n][:, :], pacc_sb[:, tq4, n * 512:(n + 1) * 512],
                     w=wwe, inc=("ve", 1))
            p.op("vector", "tensor_tensor", x2t_sb[:, :], x2t_sb[:, :], b2_b[:, :],
                 ALU.add, inc=("ve", 1))
            p.op("vector", "tensor_add", x2t_sb[:, :], x2t_sb[:, :],
                 x1t_sb[:, :], inc=("ve", 1))
            p.op("vector", "reduce_sum", stats[:, ST_SUM + t:ST_SUM + t + 1],
                 x2t_sb[:, :], AX.X, inc=("ve", 1))
            p.op("scalar", "activation", sqd_sb[:, :], x2t_sb[:, :], AF.Square,
                 accum_out=stats[:, ST_SSQ + t:ST_SSQ + t + 1],
                 w=(("ve", p.val("ve")),), inc=("ae", 1))
            p.dma("sync", "so", out=sc_x2[t], in_=x2t_sb[:, :],
                  w=(("ae", p.val("ae")),))
    W3 = (("ve", p.val("ve")), ("ae", p.val("ae")), ("ge", p.val("ge")))
    sU3, sQ3 = stats[:, ST_SUM:ST_SUM + 16], stats[:, ST_SSQ:ST_SSQ + 16]
    nmu3, rsq3 = stats[:, ST_NMU:ST_NMU + 16], stats[:, ST_RSQ:ST_RSQ + 16]
    t13 = stats[:, ST_SPP:ST_SPP + 16]
    t23 = stats[:, ST_M2:ST_M2 + 16]
    p.op("gpsimd", "tensor_scalar", nmu3, sU3, -1.0 / D, None, ALU.mult, w=W3, inc=("ge", 1))
    p.op("gpsimd", "tensor_scalar", t13, sQ3, 1.0 / D, LN_EPS, ALU.mult, ALU.add, inc=("ge", 1))
    p.op("scalar", "activation", t23, nmu3, AF.Square, w=(("ge", p.val("ge")),), inc=("ae", 1))
    p.op("vector", "tensor_sub", t13, t13, t23,
         w=(("ge", p.val("ge")), ("ae", p.val("ae"))), inc=("ve", 1))
    p.op("scalar", "activation", t23, t13, AF.Ln, w=(("ve", p.val("ve")),), inc=("ae", 1))
    p.op("scalar", "activation", rsq3, t23, AF.Exp, scale=-0.5, inc=("ae", 1))
    for t in range(NT):
        p.dma("sync", "c" + str(t % 2), out=x2t_sb[:, :] if t % 2 == 0 else x1t_sb[:, :],
              in_=sc_x2[t],
              w=(p.wdma("so"), ("ve", p.val("ve"))) if t == 0 else (("ve", p.val("ve")),))
        src = x2t_sb if t % 2 == 0 else x1t_sb
        dstl = xlnt_sb if t % 2 == 0 else sqd_sb
        p.op("vector", "tensor_scalar", dstl[:, :], src[:, :],
             stats[:, ST_NMU + t:ST_NMU + t + 1], stats[:, ST_RSQ + t:ST_RSQ + t + 1],
             ALU.add, ALU.mult,
             w=(p.wdma("c" + str(t % 2)), p.wdma("so")) +
               ((("ae", p.val("ae")), ("ge", p.val("ge"))) if t == 0 else ()),
             inc=("ve", 1))
        p.op("vector", "tensor_reduce", stats[:, t:t + 1], dstl[:, :],
             AX.X, ALU.max, apply_absolute_value=True, inc=("ve", 1))
        p.dma("sync", "so", out=sc_x2ln[t], in_=dstl[:, :],
              w=(("ve", p.val("ve")),))
    p.op("scalar", "copy", stats[:, 32:48], stats[:, 0:16],
         w=(("ve", p.val("ve")),), inc=("ae", 1))
    p.op("vector", "reduce_max", tiny[:, 12:13], stats[:, 32:48], AX.X,
         w=(("ae", p.val("ae")),), inc=("ve", 1))
    global_absmax_chain(tiny[:, 12:13], SCL_SGX, SCL_SX, SCL_RX)
    if N_PHASES <= 8:
        finish_dump_stream(dbg_src=rfm if DEBUG_TAP == "rfm" else h2fm)
        return nc
    p.flush()

    p.dma("sync", "c0", out=x_sb[:, :].rearrange("p (t d) -> p t d", d=D),
          in_=sc_x2.transpose([1, 0, 2]), w=(p.wdma("so"),))
    for t in range(NT):
        bi = t % 2
        p.dma("sync", "c1", out=xlrd_sb[:, bi, :], in_=sc_x2ln[t],
              w=(("ve", p.val("ve")),) if t >= 2 else ())
        WRD = (p.wdma("c1"),)
        buf = t % 2
        p.op("vector", "tensor_scalar", u1_sb[:, buf, :], xlrd_sb[:, bi, :],
             scal[:, SCL_RX:SCL_RX + 1], None, ALU.mult,
             w=WRD + ((("ae", p.val("ae")), ("ge", p.val("ge"))) if t == 0 else ()),
             inc=("ve", 1))
        p.op("scalar", "add", u1_sb[:, buf, :], u1_sb[:, buf, :], pmag_pp[:, :],
             w=(("ve", p.val("ve")),), inc=("ae", 1))
        p.op("vector", "tensor_scalar", u1_sb[:, buf, :], u1_sb[:, buf, :],
             -MAGIC, -QMAX - 1.0, ALU.add, ALU.max, w=(("ae", p.val("ae")),),
             inc=("ve", 1))
        p.op("vector", "tensor_scalar",
             xqtm[:, :].rearrange("p (c f) -> p c f", f=TOK)[:, :, t * 128:(t + 1) * 128],
             u1_sb[:, buf, :].rearrange("p (c f) -> p c f", f=128),
             QMAX, None, ALU.min, inc=("ve", 1))
    dma_transpose_cmaj(xqtm, xqfm, NT * ND)
    if N_PHASES <= 9:
        finish_dump_stream(dbg_src=xqfm)
        return nc
    p.flush()

    ffn_front(1)
    if N_PHASES <= 10:
        finish_dump_stream(); return nc
    p.flush()
    ffn_down(1)
    p.flush()

    p.op("sync", "nop", w=(p.wdma("so"),))
    p.flush()
    return nc


_CACHE = {}


def kernel(**inputs):
    key = N_PHASES
    if key not in _CACHE:
        _CACHE[key] = build_nc()
    nc = _CACHE[key]

    f32 = lambda a: np.ascontiguousarray(np.asarray(a, dtype=np.float32))
    shared = {
        "Wq": f32(inputs["L0_attn_Wq"]), "Wk": f32(inputs["L0_attn_Wk"]),
        "Wv": f32(inputs["L0_attn_Wv"]), "Wo": f32(inputs["L0_attn_Wo"]),
        "omega": f32(inputs["L0_attn_omega"]),
        "W1": f32(inputs["L0_attn_W1"]),
        "b1": f32(inputs["L0_attn_b1"]).reshape(32, 128),
        "W2": f32(inputs["L0_attn_W2"]),
        "b2": f32(inputs["L0_attn_b2"]).reshape(1, D),
        "gamma": f32(inputs["L0_attn_gamma"]).reshape(1, D),
        "beta": f32(inputs["L0_attn_beta"]).reshape(1, D),
        "clip_x": f32(inputs["L0_ffn_clip_x"]).reshape(1, 1),
        "clip_down": f32(inputs["L0_ffn_clip_down"]).reshape(1, 1),
        "w_gate": f32(inputs["L0_ffn_w_gate"]),
        "w_up": f32(inputs["L0_ffn_w_up"]),
        "w_down": f32(inputs["L0_ffn_w_down"]),
    }
    xb = f32(inputs["L0_attn_batch"])
    in_maps = []
    for c in range(8):
        b, half = c // 2, c % 2
        m = dict(shared)
        m["x0"] = np.ascontiguousarray(xb[b, half * TOK:(half + 1) * TOK, :])
        m["hmask"] = np.full((1, 1), float(half == 1), np.float32)
        m["cmask"] = np.full((1, 1), float(half == 0), np.float32)
        in_maps.append(m)

    res = run_bass_kernel_spmd(nc, in_maps, core_ids=list(range(8)))
    kernel._LAST = res
    out = np.empty((B, S, D), np.float32)
    for c in range(8):
        b, half = c // 2, c % 2
        out[b, half * TOK:(half + 1) * TOK, :] = res.results[c]["out"]
    return out


# version 09 (final)
# speedup vs baseline: 1.0651x; 1.0651x over previous
from contextlib import ExitStack

import numpy as np
import ml_dtypes

import concourse.bass as bass
import concourse.mybir as mybir
from concourse.bass_utils import run_bass_kernel_spmd

F32 = mybir.dt.float32
BF16 = mybir.dt.bfloat16
AF = mybir.ActivationFunctionType
ALU = mybir.AluOpType
AX = mybir.AxisListType

B, S, D, H, DH, M, DFF, I = 4, 4096, 1024, 16, 64, 128, 4096, 2816
TOK = 2048
NT = 16
NI = 22
ND = 8
NF = 32
NH = 16
NP = 8
QMAX = 127.0
MAGIC = 12582912.0
LN_EPS = 1e-5
LN1E6 = 1e6
FEAT_EPS = 1e-6
SQM = float(np.sqrt(M))
SCALE_QK = float(DH ** -0.25)

N_PHASES = 99
DEBUG_TAP = None
ENGINES = ("tensor", "vector", "scalar", "gpsimd", "sync")


class Prog:

    def __init__(self, nc, ctx):
        self.nc = nc
        self.recs = {e: [] for e in ENGINES}
        self.cnt = {}
        self.h = {}
        self.ctx = ctx
        self.dma_sems = set()

    def sem(self, name):
        if name not in self.h:
            self.h[name] = self.ctx.enter_context(self.nc.semaphore(name))
            self.cnt[name] = 0
        return self.h[name]

    def val(self, s):
        self.sem(s)
        return self.cnt.get(s, 0)

    def op(self, eng, meth, *a, w=(), inc=None, **k):
        for (s, v) in w:
            assert v <= self.cnt.get(s, 0), f"wait-on-future {s}:{v}>{self.cnt.get(s, 0)}"
            if s in self.dma_sems:
                assert v == self.cnt[s], f"partial dma wait {s}:{v}!={self.cnt[s]}"
            self.sem(s)
        if inc is not None:
            s, d = inc
            self.sem(s)
            if d == 16:
                self.dma_sems.add(s)
            self.cnt[s] = self.cnt.get(s, 0) + d
        self.recs[eng].append((meth, a, k, tuple(w), inc))

    def dma(self, eng, semname, out, in_, w=(), **kw):
        self.op(eng, "dma_start", out=out, in_=in_, w=w, inc=(semname, 16), **kw)
        return (semname, self.cnt[semname])

    def wdma(self, semname):
        return (semname, self.val(semname))

    def flush(self):
        recs, self.recs = self.recs, {e: [] for e in ENGINES}
        if not any(recs.values()):
            return

        def emitter(rs):
            def f(e):
                for (meth, a, k, w, inc) in rs:
                    for (s, v) in w:
                        e.wait_ge(self.h[s], v)
                    ins = getattr(e, meth)(*a, **k)
                    if inc is not None:
                        s, d = inc
                        ins.then_inc(self.h[s], d)
            return f

        with self.nc.Block() as blk:
            for eng in ENGINES:
                if recs[eng]:
                    getattr(blk, eng)(emitter(recs[eng]))


def build_nc():
    nc = bass.Bass("TRN2", target_bir_lowering=False)
    ctx = ExitStack()
    nc._keepalive_ctx = ctx
    p = Prog(nc, ctx)

    def din(name, shape):
        return nc.dram_tensor(name, list(shape), F32, kind="ExternalInput").ap()

    x_hbm = din("x0", (TOK, D))
    wq_h, wk_h, wv_h = din("Wq", (D, D)), din("Wk", (D, D)), din("Wv", (D, D))
    wo_h, omg_h = din("Wo", (D, D)), din("omega", (M, DH))
    w1_h, b1_h = din("W1", (D, DFF)), din("b1", (32, 128))
    w2_h, b2_h = din("W2", (DFF, D)), din("b2", (1, D))
    gam_h, bet_h = din("gamma", (1, D)), din("beta", (1, D))
    clx_h, cld_h = din("clip_x", (1, 1)), din("clip_down", (1, 1))
    wg_h, wu_h, wd_h = din("w_gate", (I, D)), din("w_up", (I, D)), din("w_down", (D, I))
    hm_h, cm_h = din("hmask", (1, 1)), din("cmask", (1, 1))

    out_hbm = nc.dram_tensor("out", [TOK, D], F32, kind="ExternalOutput").ap()
    dbg_hbm = nc.dram_tensor("dbg", [128, NT * D], BF16, kind="ExternalOutput").ap()
    dbg2_hbm = nc.dram_tensor("dbg2", [128, 192], F32, kind="ExternalOutput").ap()

    sc_wgt = nc.dram_tensor("sc_wgt", [NI, 128, ND * 128], BF16).ap()
    sc_wut = nc.dram_tensor("sc_wut", [NI, 128, ND * 128], BF16).ap()
    sc_wdt = nc.dram_tensor("sc_wdt", [NI, 128, ND * 128], BF16).ap()
    sc_h = nc.dram_tensor("sc_h", [NI, 128, TOK], F32).ap()
    sc_x1 = nc.dram_tensor("sc_x1", [NT, 128, D], F32).ap()
    sc_a = nc.dram_tensor("sc_a", [NT, 128, D], F32).ap()
    sc_x2 = nc.dram_tensor("sc_x2", [NT, 128, D], F32).ap()
    sc_x2ln = nc.dram_tensor("sc_x2ln", [NT, 128, D], F32).ap()
    sc_row = nc.dram_tensor("sc_row", [8, 128], F32).ap()
    cc_in = nc.dram_tensor("cc_in", [1, 8], F32).ap()
    cc_out = nc.dram_tensor("cc_out", [1, 8], F32).ap()
    ccs_in = nc.dram_tensor("ccs_in", [128, NH * 65], BF16).ap()
    ccs_out = nc.dram_tensor("ccs_out", [128, NH * 65], BF16).ap()

    offA = nc.bump_sbuf(65536)[0]
    offB = nc.bump_sbuf(65536)[0]
    offC = nc.bump_sbuf(65536)[0]

    def at(name, base, off_b, shape, dt):
        return nc.alloc_sbuf_tensor_at(name, list(shape), dt, offset=base + off_b)

    x_sb = at("x_sb", offA, 0, [128, NT * D], F32)
    kpfm = at("kpfm", offA, 0, [128, NH * TOK], BF16)
    w2h_sb = at("w2h_b", offA, 0, [128, 2, 16, D], BF16)

    vtm = at("vtm", offB, 0, [128, NT * 1024], BF16)
    wv_sb = at("wv_sb", offB, 32768, [128, ND, 1024], BF16)
    s_sb = at("s_sb", offB, 32768, [128, NH * 15 * 65], BF16)
    hq_sb = at("hq_sb", offB, 0, [128, NI * 1024], BF16)
    hrd_sb = at("hrd_sb", offB, 45056, [128, 4, 1024], F32)
    dn_tmp = at("dn_tmp", offB, 61440, [128, 2, 512], F32)
    xlrd_sb = at("xlrd_sb", offB, 8192, [128, 4, 1024], F32)
    b1row = at("b1row", offB, 40960, [32, 128], F32)
    w1c_sb = at("w1c_b", offB, 0, [128, 2, ND * 128], BF16)
    pacc_sb = at("pacc_sb", offB, 4096, [128, 4, D], F32)
    x1t_sb = at("x1t_sb", offB, 20480, [128, D], F32)
    x2t_sb = at("x2t_sb", offB, 24576, [128, D], F32)
    xlnt_sb = at("xlnt_sb", offB, 28672, [128, D], F32)
    silu_sb = at("silu_sb", offB, 0, [128, TOK], F32)
    hcmb_sb = at("hcmb_sb", offB, 8192, [128, 2, TOK], F32)
    a_tm = at("a_tm", offB, 0, [128, NT * D], F32)
    wo_sb = at("wo_sb", offB, 0, [128, ND, 1024], BF16)
    wo_a = at("wo_a", offB, 16384, [128, 2, D], F32)
    h2tm = at("h2tm", offB, 24576, [128, NT * D], BF16)
    t2_sb = at("t2_sb", offB, 57344, [128, D], F32)
    sqd_sb = at("sqd_sb", offB, 61440, [128, D], F32)

    xqtm = at("xqtm", offC, 0, [128, NT * D], BF16)
    xqfm = at("xqfm", offC, 32768, [128, ND * TOK], BF16)
    u1_sb = at("u1_sb", offB, 0, [128, 2, D], F32)
    wgub = at("wgub", offC, 0, [128, 2, 2, ND * 128], BF16)
    wdt_sb = at("wdt_sb", offC, 0, [128, NI * 1024], BF16)
    afm = at("afm", offC, 0, [128, ND * TOK], BF16)
    abf_sb = at("abf_sb", offC, 32768, [128, NT * D], BF16)
    kfmp = at("kfmp", offC, 32768, [128, TOK], BF16)
    e_pair = at("e_pair", offC, 36864, [128, NT * 256], BF16)
    qktm = at("qktm", offC, 45056, [128, 2 * TOK], BF16)
    qpfm = at("qpfm", offC, 53248, [128, 2 * TOK], BF16)
    h2fm = at("h2fm", offC, 0, [128, ND * TOK], BF16)
    rfm = at("rfm", offC, 32768, [128, NF * 512], BF16)
    asb = at("asb", offC, 63520, [128, 2, 128], BF16)
    ctrb = at("ctrb", offC, 61440, [128, NH * 65], BF16)
    scar = at("scar", offC, 61440, [128, NH * 65], BF16)
    wraw = at("wraw", offC, 0, [128, 2, I], F32)
    wqi = at("wqi", offC, 22528, [128, 2, I], F32)
    wtev = at("wtev", offC, 45056, [128, 2, I], BF16)

    def sb(name, shape, dt):
        return ctx.enter_context(nc.sbuf_tensor(name, list(shape), dt))

    gam_b = sb("gam_b", [128, D], BF16)
    bet_b = sb("bet_b", [128, D], BF16)
    b2_b = sb("b2_b", [128, D], BF16)
    swd_b = sb("swd_b", [128, D], BF16)
    b1_pp = sb("b1_pp", [128, 32], F32)
    swg_pp = sb("swg_pp", [128, NI], F32)
    swu_pp = sb("swu_pp", [128, NI], F32)
    tri_f = sb("tri_f", [128, 128], BF16)
    idf = sb("idf", [128, 128], F32)
    omgbd = sb("omgbd", [128, 256], BF16)
    ones_r = sb("ones_r", [1, 128], F32)
    ones_c = sb("ones_c", [128, 1], BF16)
    scal = sb("scal", [128, 16], F32)
    stats = sb("stats", [128, 136], F32)
    phst = sb("phst", [128, 64], F32)

    wkp_sb = sb("wkp_sb", [128, 2, ND, 128], BF16)
    tiny = sb("tiny", [128, 16], F32)
    nmag_pp = sb("nmag_pp", [128, 1], F32)
    pmag_pp = sb("pmag_pp", [128, 1], F32)
    eps5_pp = sb("eps5_pp", [128, 1], F32)
    eps6_pp = sb("eps6_pp", [128, 1], F32)
    feps_pp = sb("feps_pp", [128, 1], F32)
    zero_pp = sb("zero_pp", [128, 1], F32)
    onep = sb("onep", [1, 16], F32)


    SCL_SX, SCL_RX, SCL_RH, SCL_SH05, SCL_HM, SCL_CM = 0, 1, 2, 3, 4, 5
    SCL_GMX, SCL_SGX, SCL_SGD = 6, 8, 9
    ST_SUM, ST_SSQ, ST_NMU, ST_RSQ, ST_SPP, ST_M2, ST_RMX, ST_RMN = (
        0, 16, 32, 48, 64, 80, 96, 112)
    ST_T1, ST_T2 = 128, 132

    pb = [ctx.enter_context(nc.psum_tensor(f"pb{i}", [128, 512], F32)) for i in range(8)]

    def bcast_from_p0(src_ap, dst_ap, w=()):
        p.op("tensor", "matmul", pb[7][:, 0:1], ones_r[:, :], src_ap,
             start=True, stop=True, w=w + (("ae", p.val("ae")),), inc=("pe", 1))
        p.op("scalar", "copy", dst_ap, pb[7][:, 0:1],
             w=(("pe", p.val("pe")),), inc=("ae", 1))

    def part_max_to_p0(src_col_ap, dst_ap, w=()):
        p.op("tensor", "transpose", pb[7][0:1, 0:128], src_col_ap, idf[:, :],
             w=w + (("ve", p.val("ve")), ("ge", p.val("ge")), ("ae", p.val("ae"))),
             inc=("pe", 1))
        p.op("vector", "reduce_max", dst_ap, pb[7][0:1, 0:128], AX.X,
             w=(("pe", p.val("pe")),), inc=("ve", 1))

    def global_absmax_chain(run_col_ap, sig_col, sx_col, rx_col, sh05_col=None, w=()):
        part_max_to_p0(run_col_ap, onep[0:1, 0:1], w=w)
        p.op("vector", "tensor_copy", onep[0:1, 1:8], onep[0:1, 0:1].broadcast_to((1, 7)),
             inc=("ve", 1))
        p.dma("sync", "so", out=cc_in, in_=onep[0:1, 0:8],
              w=(("ve", p.val("ve")), ("cc", p.val("cc"))))
        p.op("gpsimd", "collective_compute", "AllReduce", ALU.max,
             [list(range(8))], [cc_in], [cc_out],
             w=(p.wdma("so"), p.wdma("c0")), inc=("cc", 1))
        p.dma("sync", "c0", out=onep[0:1, 8:16], in_=cc_out, w=(("cc", p.val("cc")),))
        bcast_from_p0(onep[0:1, 8:9], scal[:, SCL_GMX:SCL_GMX + 1], w=(p.wdma("c0"),))
        p.op("vector", "tensor_scalar", scal[:, 10:11],
             scal[:, sig_col:sig_col + 1], scal[:, SCL_GMX:SCL_GMX + 1], 1.0 / QMAX,
             ALU.mult, ALU.mult, w=(("ae", p.val("ae")),), inc=("ve", 1))
        p.op("scalar", "copy", scal[:, sx_col:sx_col + 1], scal[:, 10:11],
             w=(("ve", p.val("ve")),), inc=("ae", 1))
        p.op("scalar", "activation", scal[:, rx_col:rx_col + 1],
             scal[:, 10:11], AF.Ln, inc=("ae", 1))
        p.op("scalar", "activation", scal[:, rx_col:rx_col + 1],
             scal[:, rx_col:rx_col + 1], AF.Exp, scale=-1.0, inc=("ae", 1))
        if sh05_col is not None:
            p.op("scalar", "activation", scal[:, sh05_col:sh05_col + 1],
                 scal[:, 10:11], AF.Copy, scale=0.5, inc=("ae", 1))
        p.op("vector", "nop", w=(("ae", p.val("ae")),), inc=("ve", 1))

    def ln_stats_tiles(src_ap_fn, eps, with_minmax, w0=()):
        for t in range(NT):
            xt = src_ap_fn(t)
            ww = w0 if t == 0 else ()
            p.op("vector", "reduce_sum", stats[:, ST_SUM + t:ST_SUM + t + 1], xt, AX.X,
                 w=ww, inc=("ve", 1))
            p.op("scalar", "activation", sqd_sb[:, :], xt, AF.Square,
                 accum_out=stats[:, ST_SSQ + t:ST_SSQ + t + 1], w=ww, inc=("ae", 1))
        W = (("ve", p.val("ve")), ("ae", p.val("ae")), ("ge", p.val("ge")))
        sU, sQ = stats[:, ST_SUM:ST_SUM + 16], stats[:, ST_SSQ:ST_SSQ + 16]
        nmu, rsq = stats[:, ST_NMU:ST_NMU + 16], stats[:, ST_RSQ:ST_RSQ + 16]
        t1 = stats[:, ST_SPP:ST_SPP + 16]
        t2 = stats[:, ST_M2:ST_M2 + 16]
        p.op("gpsimd", "tensor_scalar", nmu, sU, -1.0 / D, None, ALU.mult, w=W, inc=("ge", 1))
        p.op("gpsimd", "tensor_scalar", t1, sQ, 1.0 / D, eps, ALU.mult, ALU.add, inc=("ge", 1))
        p.op("scalar", "activation", t2, nmu, AF.Square, w=(("ge", p.val("ge")),),
             inc=("ae", 1))
        p.op("vector", "tensor_sub", t1, t1, t2,
             w=(("ge", p.val("ge")), ("ae", p.val("ae"))), inc=("ve", 1))
        p.op("scalar", "activation", t2, t1, AF.Ln, w=(("ve", p.val("ve")),), inc=("ae", 1))
        p.op("scalar", "activation", rsq, t2, AF.Exp, scale=-0.5, inc=("ae", 1))
        if with_minmax:
            for t in range(NT):
                buf = t % 2
                p.op("vector", "tensor_scalar", u1_sb[:, buf, :], src_ap_fn(t),
                     stats[:, ST_NMU + t:ST_NMU + t + 1],
                     stats[:, ST_RSQ + t:ST_RSQ + t + 1], ALU.add, ALU.mult,
                     w=(("ae", p.val("ae")), ("ge", p.val("ge"))) if t == 0 else (),
                     inc=("ve", 1))
                p.op("vector", "tensor_reduce", stats[:, ST_RMX + t:ST_RMX + t + 1],
                     u1_sb[:, buf, :], AX.X, ALU.max, apply_absolute_value=True,
                     inc=("ve", 1))
            p.op("scalar", "copy", stats[:, ST_RMN:ST_RMN + 16],
                 stats[:, ST_RMX:ST_RMX + 16], w=(("ve", p.val("ve")),), inc=("ae", 1))
            p.op("vector", "reduce_max", tiny[:, 12:13], stats[:, ST_RMN:ST_RMN + 16],
                 AX.X, w=(("ae", p.val("ae")),), inc=("ve", 1))

    def dma_transpose_cmaj(src, dst, nseg, chunk=32):
        for s0 in range(0, nseg, chunk):
            n = min(chunk, nseg - s0)
            p.op("scalar", "dma_start_transpose",
                 dst[:, s0 * 128:(s0 + n) * 128].rearrange("p (s f) -> p s f", f=128),
                 src[:, s0 * 128:(s0 + n) * 128].rearrange("p (s f) -> p s f", f=128),
                 w=(("ve", p.val("ve")), ("ge", p.val("ge")), ("ae", p.val("ae")),
                    ("pe", p.val("pe"))),
                 inc=("t0", 16))

    def xtile(t):
        return x_sb[:, t * D:(t + 1) * D]

    def finish_dump_stream(dbg_src=None):
        if dbg_src is not None:
            p.dma("sync", "so", out=dbg_hbm[:, 0:dbg_src.shape[1]], in_=dbg_src[:, :],
                  w=(("ve", p.val("ve")), ("ge", p.val("ge")), ("ae", p.val("ae")),
                     ("pe", p.val("pe")), p.wdma("t0")))
            p.dma("sync", "so", out=dbg2_hbm[:, 0:16], in_=scal[:, :])
            p.dma("sync", "so", out=dbg2_hbm[:, 16:152], in_=stats[:, :])
            p.dma("sync", "so", out=dbg2_hbm[:, 152:168], in_=tiny[:, :])
            p.dma("sync", "so", out=dbg2_hbm[0:1, 168:184], in_=onep[:, :])
        for t in range(NT):
            p.dma("sync", "so", out=out_hbm.rearrange("(tt p) d -> p tt d", p=128)[:, t, :],
                  in_=xtile(t),
                  w=(("ve", p.val("ve")), ("ge", p.val("ge")), ("ae", p.val("ae")),
                     ("pe", p.val("pe"))))
        p.op("sync", "nop", w=(p.wdma("so"),))
        p.flush()

    p.dma("sync", "c0", out=x_sb[:, :].rearrange("p (t d) -> p t d", d=D),
          in_=x_hbm.rearrange("(t p) d -> p t d", p=128))
    p.dma("gpsimd", "g0", out=gam_b[:, :], in_=gam_h.partition_broadcast(128))
    p.dma("gpsimd", "g0", out=bet_b[:, :], in_=bet_h.partition_broadcast(128))
    p.dma("gpsimd", "g0", out=b2_b[:, :], in_=b2_h.partition_broadcast(128))
    p.dma("sync", "c1", out=tiny[0:1, 0:1], in_=clx_h)
    p.dma("sync", "c1", out=tiny[0:1, 1:2], in_=cld_h)
    p.dma("sync", "c1", out=tiny[0:1, 2:3], in_=hm_h)
    p.dma("sync", "c1", out=tiny[0:1, 3:4], in_=cm_h)
    p.dma("sync", "c1", out=b1row[:, :], in_=b1_h)
    p.dma("sync", "c1", out=wraw[0:128, 0, 0:64], in_=omg_h)

    p.op("gpsimd", "memset", idf[:, :], 1.0)
    p.op("gpsimd", "affine_select", idf[:, :], idf[:, :], [[-1, 128]], ALU.is_equal,
         0.0, base=0, channel_multiplier=1)
    p.op("gpsimd", "memset", tri_f[:, :], 1.0)
    p.op("gpsimd", "affine_select", tri_f[:, :], tri_f[:, :], [[1, 128]], ALU.is_ge,
         0.0, base=0, channel_multiplier=-1)
    p.op("gpsimd", "memset", ones_r[:, :], 1.0)
    p.op("gpsimd", "memset", ones_c[:, :], 1.0)
    p.op("gpsimd", "memset", omgbd[:, :], 0.0)
    p.op("gpsimd", "memset", onep[:, :], 0.0)
    p.op("gpsimd", "memset", nmag_pp[:, :], -MAGIC)
    p.op("gpsimd", "memset", pmag_pp[:, :], MAGIC)
    p.op("gpsimd", "memset", eps5_pp[:, :], LN_EPS)
    p.op("gpsimd", "memset", eps6_pp[:, :], LN1E6)
    p.op("gpsimd", "memset", feps_pp[:, :], FEAT_EPS)
    p.op("gpsimd", "memset", zero_pp[:, :], 0.0)
    p.op("gpsimd", "memset", scal[:, :], 0.0, inc=("ge", 1))

    GE0 = (("ge", p.val("ge")),)
    WC1 = (p.wdma("c1"),)

    bcast_from_p0(tiny[0:1, 2:3], scal[:, SCL_HM:SCL_HM + 1], w=WC1 + GE0)
    bcast_from_p0(tiny[0:1, 3:4], scal[:, SCL_CM:SCL_CM + 1])
    for (cidx, dcol) in ((0, SCL_SGX), (1, SCL_SGD)):
        bcast_from_p0(tiny[0:1, cidx:cidx + 1], scal[:, dcol:dcol + 1])
        p.op("scalar", "activation", scal[:, dcol:dcol + 1], scal[:, dcol:dcol + 1],
             AF.Exp, scale=-1.0, w=(("ve", p.val("ve")),), inc=("ae", 1))
        p.op("gpsimd", "tensor_scalar", scal[:, dcol:dcol + 1], scal[:, dcol:dcol + 1],
             1.0, None, ALU.add, w=(("ae", p.val("ae")),), inc=("ge", 1))
        p.op("scalar", "activation", scal[:, dcol:dcol + 1], scal[:, dcol:dcol + 1],
             AF.Ln, w=(("ge", p.val("ge")),), inc=("ae", 1))
        p.op("scalar", "activation", scal[:, dcol:dcol + 1], scal[:, dcol:dcol + 1],
             AF.Exp, scale=-1.0, inc=("ae", 1))

    p.op("tensor", "transpose", pb[6][:, 0:32], b1row[:, :], idf[0:32, 0:32],
         w=WC1 + GE0 + (("ve", p.val("ve")),), inc=("pe", 1))
    p.op("vector", "tensor_copy", b1_pp[:, :], pb[6][:, 0:32],
         w=(("pe", p.val("pe")),), inc=("ve", 1))

    p.op("tensor", "transpose", pb[5][0:64, 0:128], wraw[0:128, 0, 0:64], idf[:, :],
         w=WC1 + GE0, inc=("pe", 1))
    p.op("vector", "tensor_scalar", omgbd[0:64, 0:128], pb[5][0:64, 0:128], SCALE_QK,
         None, ALU.mult, w=(("pe", p.val("pe")),), inc=("ve", 1))
    p.op("vector", "tensor_copy", omgbd[64:128, 128:256], omgbd[0:64, 0:128],
         inc=("ve", 1))

    def quant_and_transpose(src_hbm, scr, sw_col_ap, ncols, dtile):
        buf = dtile % 2
        wv = (("ve", p.val("ve")), ("ae", p.val("ae")), ("ge", p.val("ge")),
              ("pe", p.val("pe")))
        tok_ld = p.dma("sync", "c" + str(buf), out=wraw[:, buf, 0:ncols],
                       in_=src_hbm, w=wv)
        WL = (p.wdma("c" + str(buf)),)
        p.op("vector", "tensor_reduce", tiny[:, 4 + buf:5 + buf], wraw[:, buf, 0:ncols],
             AX.X, ALU.max, apply_absolute_value=True, w=WL, inc=("ve", 1))
        p.op("gpsimd", "tensor_scalar", sw_col_ap, tiny[:, 4 + buf:5 + buf], 1.0 / QMAX,
             None, ALU.mult, w=(("ve", p.val("ve")),), inc=("ge", 1))
        p.op("scalar", "activation", tiny[:, 6 + buf:7 + buf], tiny[:, 4 + buf:5 + buf],
             AF.Ln, w=(("ve", p.val("ve")),), inc=("ae", 1))
        p.op("scalar", "activation", tiny[:, 6 + buf:7 + buf], tiny[:, 6 + buf:7 + buf],
             AF.Exp, scale=-1.0, inc=("ae", 1))
        p.op("vector", "tensor_scalar", wraw[:, buf, 0:ncols], wraw[:, buf, 0:ncols],
             tiny[:, 6 + buf:7 + buf], None, ALU.mult, w=(("ae", p.val("ae")),),
             inc=("ve", 1))
        p.op("vector", "tensor_scalar", wraw[:, buf, 0:ncols], wraw[:, buf, 0:ncols],
             QMAX, MAGIC, ALU.mult, ALU.add, inc=("ve", 1))
        p.op("scalar", "add", wraw[:, buf, 0:ncols], wraw[:, buf, 0:ncols], nmag_pp[:, :],
             w=(("ve", p.val("ve")),), inc=("ae", 1))
        p.op("vector", "tensor_scalar", wqi[:, buf, 0:ncols], wraw[:, buf, 0:ncols],
             -QMAX - 1.0, QMAX, ALU.max, ALU.min,
             w=(("ae", p.val("ae")), ("pe", p.val("pe"))), inc=("ve", 1))
        vq = p.val("ve")
        nblk = ncols // 128
        aeq0 = p.val("ae")
        so_tok = p.wdma("so")
        for k in range(nblk):
            bank = k % 4
            p.op("tensor", "transpose", pb[bank][:, 0:128],
                 wqi[:, buf, k * 128:(k + 1) * 128], idf[:, :],
                 w=(("ve", vq), ("ae", aeq0)), inc=("pe", 1))
            p.op("scalar", "copy", wtev[:, buf, k * 128:(k + 1) * 128], pb[bank][:, 0:128],
                 w=(("pe", p.val("pe")),) + ((so_tok,) if k == 0 else ()), inc=("ae", 1))
        p.dma("sync", "so", out=scr, in_=wtev[:, buf, 0:nblk * 128],
              w=(("ae", p.val("ae")),))

    for i in range(NI):
        quant_and_transpose(wg_h[i * 128:(i + 1) * 128, :], sc_wgt[i],
                            swg_pp[:, i:i + 1], 1024, i)
    for i in range(NI):
        quant_and_transpose(wu_h[i * 128:(i + 1) * 128, :], sc_wut[i],
                            swu_pp[:, i:i + 1], 1024, i)
    for dt in range(ND):
        buf = dt % 2
        wv = (("ve", p.val("ve")), ("ae", p.val("ae")), ("ge", p.val("ge")),
              ("pe", p.val("pe")))
        p.dma("sync", "c" + str(buf), out=wraw[:, buf, :],
              in_=wd_h[dt * 128:(dt + 1) * 128, :], w=wv)
        WL = (p.wdma("c" + str(buf)),)
        p.op("vector", "tensor_reduce", tiny[:, 4 + buf:5 + buf], wraw[:, buf, :],
             AX.X, ALU.max, apply_absolute_value=True, w=WL, inc=("ve", 1))
        p.op("gpsimd", "tensor_scalar", stats[:, ST_RMX + dt:ST_RMX + dt + 1],
             tiny[:, 4 + buf:5 + buf], 1.0 / QMAX, None, ALU.mult,
             w=(("ve", p.val("ve")),), inc=("ge", 1))
        p.op("scalar", "activation", tiny[:, 6 + buf:7 + buf], tiny[:, 4 + buf:5 + buf],
             AF.Ln, w=(("ve", p.val("ve")),), inc=("ae", 1))
        p.op("scalar", "activation", tiny[:, 6 + buf:7 + buf], tiny[:, 6 + buf:7 + buf],
             AF.Exp, scale=-1.0, inc=("ae", 1))
        p.op("vector", "tensor_scalar", wraw[:, buf, :], wraw[:, buf, :],
             tiny[:, 6 + buf:7 + buf], None, ALU.mult, w=(("ae", p.val("ae")),),
             inc=("ve", 1))
        p.op("vector", "tensor_scalar", wraw[:, buf, :], wraw[:, buf, :],
             QMAX, MAGIC, ALU.mult, ALU.add, inc=("ve", 1))
        p.op("scalar", "add", wraw[:, buf, :], wraw[:, buf, :], nmag_pp[:, :],
             w=(("ve", p.val("ve")),), inc=("ae", 1))
        p.op("vector", "tensor_scalar", wqi[:, buf, :], wraw[:, buf, :],
             -QMAX - 1.0, QMAX, ALU.max, ALU.min,
             w=(("ae", p.val("ae")), ("pe", p.val("pe"))), inc=("ve", 1))
        vq = p.val("ve")
        aeq0 = p.val("ae")
        so_tok = p.wdma("so")
        for i in range(NI):
            bank = i % 4
            p.op("tensor", "transpose", pb[bank][:, 0:128],
                 wqi[:, buf, i * 128:(i + 1) * 128], idf[:, :],
                 w=(("ve", vq), ("ae", aeq0)), inc=("pe", 1))
            p.op("scalar", "copy", wtev[:, buf, i * 128:(i + 1) * 128], pb[bank][:, 0:128],
                 w=(("pe", p.val("pe")),) + ((so_tok,) if i == 0 else ()), inc=("ae", 1))
        p.dma("sync", "so",
              out=sc_wdt[:, :, dt * 128:(dt + 1) * 128].transpose([1, 0, 2]),
              in_=wtev[:, buf, 0:NI * 128].rearrange("p (i f) -> p i f", f=128),
              w=(("ae", p.val("ae")),))

    p.op("tensor", "transpose", pb[6][0:8, 0:128], stats[:, ST_RMX:ST_RMX + 8],
         idf[:, :], w=(("ve", p.val("ve")),), inc=("pe", 1))
    p.op("vector", "tensor_copy", hrd_sb[0:8, 0, 0:128], pb[6][0:8, 0:128],
         w=(("pe", p.val("pe")),), inc=("ve", 1))
    p.dma("sync", "so", out=sc_row, in_=hrd_sb[0:8, 0, 0:128],
          w=(("ve", p.val("ve")),))
    p.dma("gpsimd", "g0", out=swd_b[:, :],
          in_=sc_row.rearrange("p f -> (p f)").unsqueeze(0).partition_broadcast(128),
          w=(p.wdma("so"),))
    if N_PHASES <= 0:
        p.flush(); return nc
    p.flush()

    ln_stats_tiles(lambda t: xtile(t), LN_EPS, with_minmax=True, w0=(p.wdma("c0"),))
    global_absmax_chain(tiny[:, 12:13], SCL_SGX, SCL_SX, SCL_RX)
    p.op("vector", "tensor_scalar", stats[:, ST_RMN:ST_RMN + 16],
         stats[:, ST_RSQ:ST_RSQ + 16], scal[:, SCL_RX:SCL_RX + 1], None, ALU.mult,
         w=(("ae", p.val("ae")),), inc=("ve", 1))
    p.op("scalar", "copy", stats[:, ST_SPP:ST_SPP + 16], stats[:, ST_RMN:ST_RMN + 16],
         w=(("ve", p.val("ve")),), inc=("ae", 1))

    def quant_tile_to_cmaj(src_ap, t, nmu_ap, spp_ap, dst_tm):
        buf = t % 2
        p.op("vector", "tensor_scalar", u1_sb[:, buf, :], src_ap, nmu_ap, spp_ap,
             ALU.add, ALU.mult, w=(("ae", p.val("ae")), ("ge", p.val("ge"))),
             inc=("ve", 1))
        p.op("scalar", "add", u1_sb[:, buf, :], u1_sb[:, buf, :], pmag_pp[:, :],
             w=(("ve", p.val("ve")),), inc=("ae", 1))
        p.op("vector", "tensor_scalar", u1_sb[:, buf, :], u1_sb[:, buf, :],
             -MAGIC, -QMAX - 1.0, ALU.add, ALU.max, w=(("ae", p.val("ae")),),
             inc=("ve", 1))
        p.op("vector", "tensor_scalar",
             dst_tm[:, :].rearrange("p (c f) -> p c f", f=TOK)[:, :, t * 128:(t + 1) * 128],
             u1_sb[:, buf, :].rearrange("p (c f) -> p c f", f=128),
             QMAX, None, ALU.min, inc=("ve", 1))

    for t in range(NT):
        quant_tile_to_cmaj(xtile(t), t, stats[:, ST_NMU + t:ST_NMU + t + 1],
                           stats[:, ST_SPP + t:ST_SPP + t + 1], xqtm)
    dma_transpose_cmaj(xqtm, xqfm, NT * ND)
    if N_PHASES <= 1:
        finish_dump_stream(dbg_src=xqfm if DEBUG_TAP == "xqfm" else xqtm)
        return nc
    p.flush()

    def ffn_front(call):
        WT = (p.wdma("t0"),)
        for i in range(2):
            p.dma("sync", "c" + str(i % 2), out=wgub[:, i % 2, 0, :], in_=sc_wgt[i])
            p.dma("sync", "c" + str(i % 2), out=wgub[:, i % 2, 1, :], in_=sc_wut[i])
        bank_free = [("ae", p.val("ae")), ("ae", p.val("ae")),
                     ("ve", p.val("ve")), ("ve", p.val("ve"))]
        for i in range(NI):
            bi = i % 2
            p.op("vector", "tensor_scalar", tiny[:, 10:11], swg_pp[:, i:i + 1],
                 scal[:, SCL_SX:SCL_SX + 1], None, ALU.mult,
                 w=(("ae", p.val("ae")), ("ge", p.val("ge"))), inc=("ve", 1))
            p.op("vector", "tensor_scalar", tiny[:, 13:14], swu_pp[:, i:i + 1],
                 scal[:, SCL_SX:SCL_SX + 1], None, ALU.mult, inc=("ve", 1))
            p.op("gpsimd", "tensor_copy", tiny[:, 11:12], tiny[:, 13:14],
                 w=(("ve", p.val("ve")),), inc=("ge", 1))
            sc_ve = ("ve", p.val("ve"))
            WLD = (p.wdma("c" + str(bi)),)
            for n in range(4):
                bg, bu = n % 2, 2 + n % 2
                for k in range(ND):
                    ww = (WT + WLD + (bank_free[bg],)) if k == 0 else ()
                    p.op("tensor", "matmul", pb[bg][:, :], wgub[:, bi, 0, k * 128:(k + 1) * 128],
                         xqfm[:, k * TOK + n * 512:k * TOK + (n + 1) * 512],
                         start=(k == 0), stop=(k == ND - 1), w=ww,
                         inc=("pe", 1) if k == ND - 1 else None)
                p.op("scalar", "activation", silu_sb[:, n * 512:(n + 1) * 512], pb[bg][:, :],
                     AF.Silu, scale=tiny[:, 10:11], w=(("pe", p.val("pe")), sc_ve),
                     inc=("ae", 1))
                bank_free[bg] = ("ae", p.val("ae"))
                for k in range(ND):
                    ww = (bank_free[bu],) if k == 0 else ()
                    p.op("tensor", "matmul", pb[bu][:, :], wgub[:, bi, 1, k * 128:(k + 1) * 128],
                         xqfm[:, k * TOK + n * 512:k * TOK + (n + 1) * 512],
                         start=(k == 0), stop=(k == ND - 1), w=ww,
                         inc=("pe", 1) if k == ND - 1 else None)
                wwh = (("pe", p.val("pe")), ("ae", p.val("ae")), ("ge", p.val("ge")))
                if i >= 2 and n == 0:
                    wwh = wwh + (p.wdma("so"),)
                p.op("vector", "scalar_tensor_tensor", hcmb_sb[:, bi, n * 512:(n + 1) * 512],
                     pb[bu][:, :], tiny[:, 11:12], silu_sb[:, n * 512:(n + 1) * 512],
                     ALU.mult, ALU.mult, w=wwh, inc=("ve", 1))
                bank_free[bu] = ("ve", p.val("ve"))
            p.op("vector", "tensor_reduce", stats[:, i:i + 1], hcmb_sb[:, bi, :], AX.X,
                 ALU.max, apply_absolute_value=True, inc=("ve", 1))
            p.dma("sync", "so", out=sc_h[i], in_=hcmb_sb[:, bi, :],
                  w=(("ve", p.val("ve")),))
            if i + 2 < NI:
                p.dma("sync", "c" + str(i % 2), out=wgub[:, i % 2, 0, :], in_=sc_wgt[i + 2],
                      w=(("pe", p.val("pe")),))
                p.dma("sync", "c" + str(i % 2), out=wgub[:, i % 2, 1, :], in_=sc_wut[i + 2])
        p.op("scalar", "copy", stats[:, 32:32 + NI], stats[:, 0:NI],
             w=(("ve", p.val("ve")),), inc=("ae", 1))
        p.op("vector", "reduce_max", tiny[:, 8:9], stats[:, 32:32 + NI], AX.X,
             w=(("ae", p.val("ae")),), inc=("ve", 1))
        global_absmax_chain(tiny[:, 8:9], SCL_SGD, 7, SCL_RH, sh05_col=SCL_SH05)

    def ffn_down(call):
        for i in range(NI):
            p.dma("sync", "c" + str(i % 2), out=wdt_sb[:, i * 1024:(i + 1) * 1024],
                  in_=sc_wdt[i], w=(("pe", p.val("pe")),) if i < 2 else ())
        for half in range(2):
            def h_load(i):
                ww = (("ve", p.val("ve")),)
                if half == 0 and i < 4:
                    ww = (p.wdma("so"),)
                p.dma("sync", "h" + str(i % 4), out=hrd_sb[:, i % 4, :],
                      in_=sc_h[i][:, half * 1024:(half + 1) * 1024], w=ww)
            for i in range(4):
                h_load(i)
            for i in range(NI):
                bi = i % 4
                p.op("vector", "tensor_scalar", hrd_sb[:, bi, :], hrd_sb[:, bi, :],
                     scal[:, SCL_RH:SCL_RH + 1], MAGIC, ALU.mult, ALU.add,
                     w=(p.wdma("h" + str(i % 4)),), inc=("ve", 1))
                p.op("scalar", "add", hrd_sb[:, bi, :], hrd_sb[:, bi, :], nmag_pp[:, :],
                     w=(("ve", p.val("ve")),), inc=("ae", 1))
                p.op("vector", "tensor_scalar", hq_sb[:, i * 1024:(i + 1) * 1024],
                     hrd_sb[:, bi, :], -QMAX - 1.0, QMAX, ALU.max, ALU.min,
                     w=(("ae", p.val("ae")),), inc=("ve", 1))
                if i + 4 < NI:
                    h_load(i + 4)
            VQ = (("ve", p.val("ve")),)
            for t8 in range(8):
                t = half * 8 + t8
                for n in range(2):
                    bank = (t8 * 2 + n) % 4
                    for i in range(NI):
                        ww = VQ if i == 0 else ()
                        p.op("tensor", "matmul", pb[bank][:, :],
                             hq_sb[:, i * 1024 + t8 * 128:i * 1024 + (t8 + 1) * 128],
                             wdt_sb[:, i * 1024 + n * 512:i * 1024 + (n + 1) * 512],
                             start=(i == 0), stop=(i == NI - 1), w=ww,
                             inc=("pe", 1) if i == NI - 1 else None)
                    bt = bank % 2
                    p.op("vector", "scalar_tensor_tensor", dn_tmp[:, bt, :], pb[bank][:, :],
                         scal[:, SCL_SH05:SCL_SH05 + 1], swd_b[:, n * 512:(n + 1) * 512],
                         ALU.mult, ALU.mult, w=(("pe", p.val("pe")),), inc=("ve", 1))
                    p.op("vector", "tensor_add",
                         x_sb[:, t * D + n * 512:t * D + (n + 1) * 512],
                         x_sb[:, t * D + n * 512:t * D + (n + 1) * 512],
                         dn_tmp[:, bt, :], inc=("ve", 1))
                if call == 1:
                    p.dma("sync", "so", out=out_hbm.rearrange("(tt p) d -> p tt d", p=128)[:, t, :],
                          in_=xtile(t), w=(("ve", p.val("ve")),))

    ffn_front(0)
    if N_PHASES <= 2:
        finish_dump_stream(); return nc
    p.flush()
    ffn_down(0)
    if N_PHASES <= 3:
        finish_dump_stream(); return nc
    p.flush()

    ln_stats_tiles(lambda t: xtile(t), LN_EPS, with_minmax=False)
    for t in range(NT):
        p.op("vector", "tensor_scalar", a_tm[:, t * D:(t + 1) * D], xtile(t),
             stats[:, ST_NMU + t:ST_NMU + t + 1], stats[:, ST_RSQ + t:ST_RSQ + t + 1],
             ALU.add, ALU.mult,
             w=(("ae", p.val("ae")), ("ge", p.val("ge"))) if t == 0 else (),
             inc=("ve", 1))
        p.op("vector", "tensor_scalar",
             abf_sb[:, :].rearrange("p (c f) -> p c f", f=TOK)[:, :, t * 128:(t + 1) * 128],
             xtile(t).rearrange("p (c f) -> p c f", f=128),
             stats[:, ST_NMU + t:ST_NMU + t + 1], stats[:, ST_RSQ + t:ST_RSQ + t + 1],
             ALU.add, ALU.mult, inc=("ve", 1))
    p.dma("sync", "so", out=sc_a.transpose([1, 0, 2]),
          in_=a_tm[:, :].rearrange("p (t d) -> p t d", d=D), w=(("ve", p.val("ve")),))
    p.dma("sync", "so", out=sc_x1.transpose([1, 0, 2]),
          in_=x_sb[:, :].rearrange("p (t d) -> p t d", d=D))
    dma_transpose_cmaj(abf_sb, afm, NT * ND)
    if N_PHASES <= 4:
        finish_dump_stream(dbg_src=afm)
        return nc
    p.flush()

    p.dma("gpsimd", "g0", out=wv_sb[:, :, :],
          in_=wv_h.rearrange("(k pp) j -> pp k j", pp=128))

    def load_wpair(w_h, pr, ww=()):
        p.dma("gpsimd", "g1", out=wkp_sb[:, pr % 2, :, :],
              in_=w_h.rearrange("(k pp) (pr j) -> pp k pr j", pp=128, j=128)[:, :, pr, :],
              w=ww)
    load_wpair(wk_h, 0)
    load_wpair(wk_h, 1)
    WT0 = (p.wdma("t0"),)
    for t in range(NT):
        for n in range(2):
            bank = (2 * t + n) % 4
            for k in range(ND):
                ww = (WT0 + (p.wdma("g0"), ("ae", p.val("ae")))) if k == 0 else ()
                p.op("tensor", "matmul", pb[bank][:, :],
                     afm[:, k * TOK + t * 128:k * TOK + (t + 1) * 128],
                     wv_sb[:, k, n * 512:(n + 1) * 512],
                     start=(k == 0), stop=(k == ND - 1), w=ww,
                     inc=("pe", 1) if k == ND - 1 else None)
            p.op("scalar", "copy", vtm[:, t * 1024 + n * 512:t * 1024 + (n + 1) * 512],
                 pb[bank][:, :], w=(("pe", p.val("pe")),), inc=("ae", 1))

    def qk_phi(pr, wsem):
        for n in range(4):
            bank = n % 2
            for k in range(ND):
                ww = ((p.wdma(wsem), ("ae", p.val("ae")), ("ve", p.val("ve")))
                      if k == 0 else ())
                p.op("tensor", "matmul", pb[bank][:, :],
                     wkp_sb[:, pr % 2, k, :],
                     afm[:, k * TOK + n * 512:k * TOK + (n + 1) * 512],
                     start=(k == 0), stop=(k == ND - 1), w=ww,
                     inc=("pe", 1) if k == ND - 1 else None)
            p.op("scalar", "copy", kfmp[:, n * 512:(n + 1) * 512], pb[bank][:, :],
                 w=(("pe", p.val("pe")),), inc=("ae", 1))
        AEK = (("ae", p.val("ae")),)
        for t in range(NT):
            bank = 4 + t % 2
            p.op("tensor", "matmul", pb[bank][:, 0:256],
                 kfmp[:, t * 128:(t + 1) * 128], omgbd[:, :], start=True, stop=True,
                 w=AEK + (("ae", p.val("ae")),), inc=("pe", 1))
            wwx = (("pe", p.val("pe")),)
            if t == 0:
                wwx = wwx + (("ve", p.val("ve")),)
            p.op("scalar", "activation", e_pair[:, t * 256:(t + 1) * 256],
                 pb[bank][:, 0:256], AF.Exp, w=wwx, inc=("ae", 1))
        AEE = (("ae", p.val("ae")),)
        for h01 in range(2):
            p.op("vector", "tensor_reduce", phst[:, h01 * 16:(h01 + 1) * 16],
                 e_pair[:, :].rearrange("p (t f) -> p t f", f=256)
                 [:, :, h01 * 128:(h01 + 1) * 128],
                 AX.X, ALU.max, w=AEE, inc=("ve", 1))
        p.op("scalar", "activation", phst[:, 32:64], phst[:, 0:32], AF.Ln,
             scale=SQM, w=(("ve", p.val("ve")),), inc=("ae", 1))
        p.op("scalar", "activation", phst[:, 32:64], phst[:, 32:64], AF.Exp,
             scale=-1.0, inc=("ae", 1))
        AEF = (("ae", p.val("ae")),)
        for h01 in range(2):
            for t in range(NT):
                wwq = ()
                if h01 == 0 and t == 0:
                    wwq = AEF + (("pe", p.val("pe")), p.wdma("t0"))
                p.op("vector", "tensor_scalar",
                     qktm[:, h01 * TOK + t * 128:h01 * TOK + (t + 1) * 128],
                     e_pair[:, t * 256 + h01 * 128:t * 256 + (h01 + 1) * 128],
                     phst[:, 32 + h01 * 16 + t:33 + h01 * 16 + t], FEAT_EPS,
                     ALU.mult, ALU.add, w=wwq, inc=("ve", 1))

    for pr in range(NP):
        qk_phi(pr, "g1")
        if pr + 2 < NP:
            load_wpair(wk_h, pr + 2, ww=(("pe", p.val("pe")),))
        dma_transpose_cmaj(qktm, kpfm[:, 2 * pr * TOK:(2 * pr + 2) * TOK], 32)
        sbank_free = [("ae", p.val("ae")), ("ae", p.val("ae"))]
        for c in range(NT):
            for h01 in range(2):
                h = 2 * pr + h01
                pbS = pb[6 + h01]
                ww = (sbank_free[h01],) + ((("ve", p.val("ve")),) if c == 0 else ())
                p.op("tensor", "matmul", pbS[:, 0:64],
                     qktm[:, h01 * TOK + c * 128:h01 * TOK + (c + 1) * 128],
                     vtm[:, c * 1024 + h * 64:c * 1024 + (h + 1) * 64],
                     start=(c == 0), stop=False, w=ww)
                p.op("tensor", "matmul", pbS[:, 64:65],
                     qktm[:, h01 * TOK + c * 128:h01 * TOK + (c + 1) * 128],
                     ones_c[:, :], start=False, stop=True, inc=("pe", 1))
            for h01 in range(2):
                h = 2 * pr + h01
                pbS = pb[6 + h01]
                if c < 15:
                    p.op("scalar", "copy", s_sb[:, (h * 15 + c) * 65:(h * 15 + c + 1) * 65],
                         pbS[:, 0:65], w=(("pe", p.val("pe")),), inc=("ae", 1))
                else:
                    p.op("scalar", "activation", ctrb[:, h * 65:(h + 1) * 65],
                         pbS[:, 0:65], AF.Copy, scale=scal[:, SCL_CM:SCL_CM + 1],
                         w=(("pe", p.val("pe")),), inc=("ae", 1))
                sbank_free[h01] = ("ae", p.val("ae"))
    p.dma("sync", "so", out=ccs_in, in_=ctrb[:, :],
          w=(("ae", p.val("ae")), ("cc", p.val("cc"))))
    p.op("gpsimd", "collective_compute", "AllReduce", ALU.add,
         [[0, 1], [2, 3], [4, 5], [6, 7]], [ccs_in], [ccs_out],
         w=(p.wdma("so"), p.wdma("c0")), inc=("cc", 1))
    p.dma("sync", "c0", out=scar[:, :], in_=ccs_out, w=(("cc", p.val("cc")),))
    p.op("vector", "tensor_scalar", scar[:, :], scar[:, :],
         scal[:, SCL_HM:SCL_HM + 1], None, ALU.mult, w=(p.wdma("c0"),), inc=("ve", 1))
    if N_PHASES <= 5:
        finish_dump_stream(dbg_src=s_sb[:, 0:15600] if DEBUG_TAP == "s_sb"
                           else kpfm[:, 0:NT * D])
        return nc
    p.flush()

    load_wpair(wq_h, 0)
    load_wpair(wq_h, 1)
    for pr in range(NP):
        qk_phi(pr, "g1")
        if pr + 2 < NP:
            load_wpair(wq_h, pr + 2, ww=(("pe", p.val("pe")),))
        dma_transpose_cmaj(qktm, qpfm, 32)
        WTQ = (p.wdma("t0"),)
        abank_free = [("ve", p.val("ve")), ("ve", p.val("ve"))]
        nbank_free = [("ve", p.val("ve")), ("ve", p.val("ve"))]
        for c in range(NT):
            for h01 in range(2):
                h = 2 * pr + h01
                p.op("tensor", "matmul", pb[h01][:, 0:128],
                     kpfm[:, h * TOK + c * 128:h * TOK + (c + 1) * 128],
                     qpfm[:, h01 * TOK + c * 128:h01 * TOK + (c + 1) * 128],
                     start=True, stop=True,
                     w=(abank_free[h01],) + (WTQ if c == 0 else ()), inc=("pe", 1))
                p.op("vector", "tensor_tensor", asb[:, h01, :], pb[h01][:, 0:128],
                     tri_f[:, :], ALU.mult, w=(("pe", p.val("pe")),), inc=("ve", 1))
                abank_free[h01] = ("ve", p.val("ve"))
            for h01 in range(2):
                h = 2 * pr + h01
                pbN = pb[2 + h01]
                p.op("tensor", "matmul", pbN[:, 0:64], asb[:, h01, :],
                     vtm[:, c * 1024 + h * 64:c * 1024 + (h + 1) * 64],
                     start=True, stop=False,
                     w=(nbank_free[h01], ("ve", p.val("ve"))))
                p.op("tensor", "matmul", pbN[:, 64:65], asb[:, h01, :], ones_c[:, :],
                     start=False, stop=False)
                if c > 0:
                    p.op("tensor", "matmul", pbN[:, 0:65],
                         qpfm[:, h01 * TOK + c * 128:h01 * TOK + (c + 1) * 128],
                         s_sb[:, (h * 15 + c - 1) * 65:(h * 15 + c) * 65],
                         start=False, stop=False)
                p.op("tensor", "matmul", pbN[:, 0:65],
                     qpfm[:, h01 * TOK + c * 128:h01 * TOK + (c + 1) * 128],
                     scar[:, h * 65:(h + 1) * 65], start=False, stop=True,
                     inc=("pe", 1))
                p.op("scalar", "activation", tiny[:, 14 + h01:15 + h01], pbN[:, 64:65],
                     AF.Ln, bias=feps_pp[:, :], w=(("pe", p.val("pe")),), inc=("ae", 1))
                p.op("scalar", "activation", tiny[:, 14 + h01:15 + h01], tiny[:, 14 + h01:15 + h01],
                     AF.Exp, scale=-1.0, inc=("ae", 1))
                p.op("vector", "tensor_scalar",
                     kpfm[:, 2 * pr * TOK + c * 128 + h01 * 64:
                          2 * pr * TOK + c * 128 + (h01 + 1) * 64],
                     pbN[:, 0:64], tiny[:, 14 + h01:15 + h01], None, ALU.mult,
                     w=(("ae", p.val("ae")),), inc=("ve", 1))
                nbank_free[h01] = ("ve", p.val("ve"))
    if N_PHASES <= 6:
        finish_dump_stream(dbg_src=kpfm[:, 0:NT * D])
        return nc
    p.flush()

    for pr in range(NP):
        dma_transpose_cmaj(kpfm[:, 2 * pr * TOK:(2 * pr + 1) * TOK],
                           afm[:, pr * TOK:(pr + 1) * TOK], 16)
    p.dma("gpsimd", "g0", out=wo_sb[:, :, :],
          in_=wo_h.rearrange("(k pp) j -> pp k j", pp=128))
    WT7 = (p.wdma("t0"), p.wdma("g0"))
    if N_PHASES == 65:
        finish_dump_stream(dbg_src=afm)
        return nc
    for t in range(NT):
        p.dma("sync", "c" + str(t % 2), out=wo_a[:, t % 2, :],
              in_=sc_a[t], w=(("ve", p.val("ve")),))
        for n in range(2):
            bank = (2 * t + n) % 4
            for k in range(ND):
                ww = (WT7 + (("ve", p.val("ve")),)) if k == 0 else ()
                p.op("tensor", "matmul", pb[bank][:, :],
                     afm[:, k * TOK + t * 128:k * TOK + (t + 1) * 128],
                     wo_sb[:, k, n * 512:(n + 1) * 512],
                     start=(k == 0), stop=(k == ND - 1), w=ww,
                     inc=("pe", 1) if k == ND - 1 else None)
            p.op("vector", "tensor_add", t2_sb[:, n * 512:(n + 1) * 512],
                 pb[bank][:, :], wo_a[:, t % 2, n * 512:(n + 1) * 512],
                 w=(("pe", p.val("pe")), p.wdma("c" + str(t % 2))), inc=("ve", 1))
        p.op("vector", "reduce_sum", tiny[:, 4:5], t2_sb[:, :], AX.X, inc=("ve", 1))
        p.op("scalar", "activation", sqd_sb[:, :], t2_sb[:, :], AF.Square,
             accum_out=tiny[:, 5:6], w=(("ve", p.val("ve")),), inc=("ae", 1))
        p.op("gpsimd", "tensor_scalar", tiny[:, 6:7], tiny[:, 4:5], -1.0 / D, None,
             ALU.mult, w=(("ve", p.val("ve")),), inc=("ge", 1))
        p.op("gpsimd", "tensor_scalar", tiny[:, 7:8], tiny[:, 5:6], 1.0 / D, LN1E6,
             ALU.mult, ALU.add, w=(("ae", p.val("ae")),), inc=("ge", 1))
        p.op("scalar", "activation", tiny[:, 8:9], tiny[:, 6:7], AF.Square,
             w=(("ge", p.val("ge")),), inc=("ae", 1))
        p.op("vector", "tensor_sub", tiny[:, 9:10], tiny[:, 7:8], tiny[:, 8:9],
             w=(("ge", p.val("ge")), ("ae", p.val("ae"))), inc=("ve", 1))
        p.op("scalar", "activation", tiny[:, 10:11], tiny[:, 9:10], AF.Ln,
             w=(("ve", p.val("ve")),), inc=("ae", 1))
        p.op("scalar", "activation", tiny[:, 10:11], tiny[:, 10:11], AF.Exp,
             scale=-0.5, inc=("ae", 1))
        p.op("vector", "tensor_scalar", sqd_sb[:, :], t2_sb[:, :], tiny[:, 6:7],
             tiny[:, 10:11], ALU.add, ALU.mult,
             w=(("ge", p.val("ge")), ("ae", p.val("ae"))), inc=("ve", 1))
        p.op("vector", "tensor_tensor", t2_sb[:, :], sqd_sb[:, :], gam_b[:, :],
             ALU.mult, inc=("ve", 1))
        p.op("vector", "tensor_tensor",
             h2tm[:, :].rearrange("p (c f) -> p c f", f=TOK)[:, :, t * 128:(t + 1) * 128],
             t2_sb[:, :].rearrange("p (c f) -> p c f", f=128),
             bet_b[:, :].rearrange("p (c f) -> p c f", f=128), ALU.add, inc=("ve", 1))
    dma_transpose_cmaj(h2tm, h2fm, NT * ND)
    if N_PHASES <= 7:
        finish_dump_stream(dbg_src=afm if DEBUG_TAP == "afm" else h2fm)
        return nc
    p.flush()

    def w1_load(q4, f):
        gs = "g" + str(2 + f % 2)
        p.dma("gpsimd", gs, out=w1c_sb[:, f % 2, :],
              in_=w1_h.rearrange("(k pp) (ff c) -> pp k ff c", pp=128, c=128)
              [:, :, f, :], w=(("pe", p.val("pe")),))

    def w2_load(q4, fh):
        for fj in range(16):
            f = fh * 16 + fj
            p.dma("gpsimd", "g" + str(fh % 2),
                  out=w2h_sb[:, fh % 2, fj, :], in_=w2_h[f * 128:(f + 1) * 128, :],
                  w=(("pe", p.val("pe")),) if fj == 0 else ())

    w2_load(0, 0)
    for q4 in range(4):
        w1_load(q4, 0)
        w1_load(q4, 1)
        for f in range(NF):
            gs = "g" + str(2 + f % 2)
            WL1 = (p.wdma(gs), ("ae", p.val("ae")), p.wdma("t0"), ("ve", p.val("ve")))
            for k in range(ND):
                p.op("tensor", "matmul", pb[f % 4][:, :],
                     w1c_sb[:, f % 2, k * 128:(k + 1) * 128],
                     h2fm[:, k * TOK + q4 * 512:k * TOK + (q4 + 1) * 512],
                     start=(k == 0), stop=(k == ND - 1), w=WL1 if k == 0 else (),
                     inc=("pe", 1) if k == ND - 1 else None)
            if f + 2 < NF:
                w1_load(q4, f + 2)
            p.op("scalar", "activation", rfm[:, f * 512:(f + 1) * 512], pb[f % 4][:, :],
                 AF.Relu, bias=b1_pp[:, f:f + 1], w=(("pe", p.val("pe")),), inc=("ae", 1))
        for fh in range(2):
            if fh == 0:
                w2_load(q4, 1)
            elif q4 < 3:
                w2_load(q4 + 1, 0)
            WL2 = (p.wdma("g" + str(fh % 2)), ("ae", p.val("ae")), ("ve", p.val("ve")))
            for tq4 in range(4):
                for n in range(2):
                    for fj in range(16):
                        f = fh * 16 + fj
                        p.op("tensor", "matmul", pb[tq4 * 2 + n][:, :],
                             rfm[:, f * 512 + tq4 * 128:f * 512 + (tq4 + 1) * 128],
                             w2h_sb[:, fh % 2, fj, n * 512:(n + 1) * 512],
                             start=(fj == 0), stop=(fj == 15), w=WL2 if fj == 0 else (),
                             inc=("pe", 1) if fj == 15 else None)
                    if fh == 0:
                        p.op("vector", "tensor_copy", pacc_sb[:, tq4, n * 512:(n + 1) * 512],
                             pb[tq4 * 2 + n][:, :], w=(("pe", p.val("pe")),),
                             inc=("ve", 1))
        PEQ = (("pe", p.val("pe")),)
        for tq4 in range(4):
            t = 4 * q4 + tq4
            p.dma("sync", "c" + str(t % 2), out=x1t_sb[:, :], in_=sc_x1[t],
                  w=(("ve", p.val("ve")),))
            for n in range(2):
                wwe = PEQ
                if n == 0:
                    wwe = wwe + (p.wdma("c" + str(t % 2)), p.wdma("so"))
                p.op("vector", "tensor_add", x2t_sb[:, n * 512:(n + 1) * 512],
                     pb[tq4 * 2 + n][:, :], pacc_sb[:, tq4, n * 512:(n + 1) * 512],
                     w=wwe, inc=("ve", 1))
            p.op("vector", "tensor_tensor", x2t_sb[:, :], x2t_sb[:, :], b2_b[:, :],
                 ALU.add, inc=("ve", 1))
            p.op("vector", "tensor_add", x2t_sb[:, :], x2t_sb[:, :],
                 x1t_sb[:, :], inc=("ve", 1))
            p.op("vector", "reduce_sum", stats[:, ST_SUM + t:ST_SUM + t + 1],
                 x2t_sb[:, :], AX.X, inc=("ve", 1))
            p.op("scalar", "activation", sqd_sb[:, :], x2t_sb[:, :], AF.Square,
                 accum_out=stats[:, ST_SSQ + t:ST_SSQ + t + 1],
                 w=(("ve", p.val("ve")),), inc=("ae", 1))
            p.dma("sync", "so", out=sc_x2[t], in_=x2t_sb[:, :],
                  w=(("ae", p.val("ae")),))
    W3 = (("ve", p.val("ve")), ("ae", p.val("ae")), ("ge", p.val("ge")))
    sU3, sQ3 = stats[:, ST_SUM:ST_SUM + 16], stats[:, ST_SSQ:ST_SSQ + 16]
    nmu3, rsq3 = stats[:, ST_NMU:ST_NMU + 16], stats[:, ST_RSQ:ST_RSQ + 16]
    t13 = stats[:, ST_SPP:ST_SPP + 16]
    t23 = stats[:, ST_M2:ST_M2 + 16]
    p.op("gpsimd", "tensor_scalar", nmu3, sU3, -1.0 / D, None, ALU.mult, w=W3, inc=("ge", 1))
    p.op("gpsimd", "tensor_scalar", t13, sQ3, 1.0 / D, LN_EPS, ALU.mult, ALU.add, inc=("ge", 1))
    p.op("scalar", "activation", t23, nmu3, AF.Square, w=(("ge", p.val("ge")),), inc=("ae", 1))
    p.op("vector", "tensor_sub", t13, t13, t23,
         w=(("ge", p.val("ge")), ("ae", p.val("ae"))), inc=("ve", 1))
    p.op("scalar", "activation", t23, t13, AF.Ln, w=(("ve", p.val("ve")),), inc=("ae", 1))
    p.op("scalar", "activation", rsq3, t23, AF.Exp, scale=-0.5, inc=("ae", 1))
    for t in range(NT):
        p.dma("sync", "c" + str(t % 2), out=x2t_sb[:, :] if t % 2 == 0 else x1t_sb[:, :],
              in_=sc_x2[t],
              w=(p.wdma("so"), ("ve", p.val("ve"))) if t == 0 else (("ve", p.val("ve")),))
        src = x2t_sb if t % 2 == 0 else x1t_sb
        dstl = xlnt_sb if t % 2 == 0 else sqd_sb
        p.op("vector", "tensor_scalar", dstl[:, :], src[:, :],
             stats[:, ST_NMU + t:ST_NMU + t + 1], stats[:, ST_RSQ + t:ST_RSQ + t + 1],
             ALU.add, ALU.mult,
             w=(p.wdma("c" + str(t % 2)), p.wdma("so")) +
               ((("ae", p.val("ae")), ("ge", p.val("ge"))) if t == 0 else ()),
             inc=("ve", 1))
        p.op("vector", "tensor_reduce", stats[:, t:t + 1], dstl[:, :],
             AX.X, ALU.max, apply_absolute_value=True, inc=("ve", 1))
        p.dma("sync", "so", out=sc_x2ln[t], in_=dstl[:, :],
              w=(("ve", p.val("ve")),))
    p.op("scalar", "copy", stats[:, 32:48], stats[:, 0:16],
         w=(("ve", p.val("ve")),), inc=("ae", 1))
    p.op("vector", "reduce_max", tiny[:, 12:13], stats[:, 32:48], AX.X,
         w=(("ae", p.val("ae")),), inc=("ve", 1))
    global_absmax_chain(tiny[:, 12:13], SCL_SGX, SCL_SX, SCL_RX)
    if N_PHASES <= 8:
        finish_dump_stream(dbg_src=rfm if DEBUG_TAP == "rfm" else h2fm)
        return nc
    p.flush()

    p.dma("sync", "c0", out=x_sb[:, :].rearrange("p (t d) -> p t d", d=D),
          in_=sc_x2.transpose([1, 0, 2]), w=(p.wdma("so"),))
    def xl_load(t):
        p.dma("sync", "h" + str(t % 4), out=xlrd_sb[:, t % 4, :], in_=sc_x2ln[t],
              w=(("ve", p.val("ve")),) if t >= 4 else (p.wdma("so"),))
    for t in range(4):
        xl_load(t)
    for t in range(NT):
        bi = t % 4
        WRD = (p.wdma("h" + str(t % 4)),)
        buf = t % 2
        p.op("vector", "tensor_scalar", u1_sb[:, buf, :], xlrd_sb[:, bi, :],
             scal[:, SCL_RX:SCL_RX + 1], None, ALU.mult,
             w=WRD + ((("ae", p.val("ae")), ("ge", p.val("ge"))) if t == 0 else ()),
             inc=("ve", 1))
        p.op("scalar", "add", u1_sb[:, buf, :], u1_sb[:, buf, :], pmag_pp[:, :],
             w=(("ve", p.val("ve")),), inc=("ae", 1))
        p.op("vector", "tensor_scalar", u1_sb[:, buf, :], u1_sb[:, buf, :],
             -MAGIC, -QMAX - 1.0, ALU.add, ALU.max, w=(("ae", p.val("ae")),),
             inc=("ve", 1))
        p.op("vector", "tensor_scalar",
             xqtm[:, :].rearrange("p (c f) -> p c f", f=TOK)[:, :, t * 128:(t + 1) * 128],
             u1_sb[:, buf, :].rearrange("p (c f) -> p c f", f=128),
             QMAX, None, ALU.min, inc=("ve", 1))
        if t + 4 < NT:
            xl_load(t + 4)
    dma_transpose_cmaj(xqtm, xqfm, NT * ND)
    if N_PHASES <= 9:
        finish_dump_stream(dbg_src=xqfm)
        return nc
    p.flush()

    ffn_front(1)
    if N_PHASES <= 10:
        finish_dump_stream(); return nc
    p.flush()
    ffn_down(1)
    p.flush()

    p.op("sync", "nop", w=(p.wdma("so"),))
    p.flush()
    return nc


_CACHE = {}


def kernel(**inputs):
    key = N_PHASES
    if key not in _CACHE:
        _CACHE[key] = build_nc()
    nc = _CACHE[key]

    f32 = lambda a: np.ascontiguousarray(np.asarray(a, dtype=np.float32))
    shared = {
        "Wq": f32(inputs["L0_attn_Wq"]), "Wk": f32(inputs["L0_attn_Wk"]),
        "Wv": f32(inputs["L0_attn_Wv"]), "Wo": f32(inputs["L0_attn_Wo"]),
        "omega": f32(inputs["L0_attn_omega"]),
        "W1": f32(inputs["L0_attn_W1"]),
        "b1": f32(inputs["L0_attn_b1"]).reshape(32, 128),
        "W2": f32(inputs["L0_attn_W2"]),
        "b2": f32(inputs["L0_attn_b2"]).reshape(1, D),
        "gamma": f32(inputs["L0_attn_gamma"]).reshape(1, D),
        "beta": f32(inputs["L0_attn_beta"]).reshape(1, D),
        "clip_x": f32(inputs["L0_ffn_clip_x"]).reshape(1, 1),
        "clip_down": f32(inputs["L0_ffn_clip_down"]).reshape(1, 1),
        "w_gate": f32(inputs["L0_ffn_w_gate"]),
        "w_up": f32(inputs["L0_ffn_w_up"]),
        "w_down": f32(inputs["L0_ffn_w_down"]),
    }
    xb = f32(inputs["L0_attn_batch"])
    in_maps = []
    for c in range(8):
        b, half = c // 2, c % 2
        m = dict(shared)
        m["x0"] = np.ascontiguousarray(xb[b, half * TOK:(half + 1) * TOK, :])
        m["hmask"] = np.full((1, 1), float(half == 1), np.float32)
        m["cmask"] = np.full((1, 1), float(half == 0), np.float32)
        in_maps.append(m)

    res = run_bass_kernel_spmd(nc, in_maps, core_ids=list(range(8)))
    kernel._LAST = res
    out = np.empty((B, S, D), np.float32)
    for c in range(8):
        b, half = c // 2, c % 2
        out[b, half * TOK:(half + 1) * TOK, :] = res.results[c]["out"]
    return out


# version 10
# speedup vs baseline: 1.0703x; 1.0048x over previous
from contextlib import ExitStack

import numpy as np
import ml_dtypes

import concourse.bass as bass
import concourse.mybir as mybir
from concourse.bass_utils import run_bass_kernel_spmd

F32 = mybir.dt.float32
BF16 = mybir.dt.bfloat16
AF = mybir.ActivationFunctionType
ALU = mybir.AluOpType
AX = mybir.AxisListType

B, S, D, H, DH, M, DFF, I = 4, 4096, 1024, 16, 64, 128, 4096, 2816
TOK = 2048
NT = 16
NI = 22
ND = 8
NF = 32
NH = 16
NP = 8
QMAX = 127.0
MAGIC = 12582912.0
LN_EPS = 1e-5
LN1E6 = 1e6
FEAT_EPS = 1e-6
SQM = float(np.sqrt(M))
SCALE_QK = float(DH ** -0.25)

N_PHASES = 99
DEBUG_TAP = None
ENGINES = ("tensor", "vector", "scalar", "gpsimd", "sync")


class Prog:

    def __init__(self, nc, ctx):
        self.nc = nc
        self.recs = {e: [] for e in ENGINES}
        self.cnt = {}
        self.h = {}
        self.ctx = ctx
        self.dma_sems = set()

    def sem(self, name):
        if name not in self.h:
            self.h[name] = self.ctx.enter_context(self.nc.semaphore(name))
            self.cnt[name] = 0
        return self.h[name]

    def val(self, s):
        self.sem(s)
        return self.cnt.get(s, 0)

    def op(self, eng, meth, *a, w=(), inc=None, **k):
        for (s, v) in w:
            assert v <= self.cnt.get(s, 0), f"wait-on-future {s}:{v}>{self.cnt.get(s, 0)}"
            if s in self.dma_sems:
                assert v == self.cnt[s], f"partial dma wait {s}:{v}!={self.cnt[s]}"
            self.sem(s)
        if inc is not None:
            s, d = inc
            self.sem(s)
            if d == 16:
                self.dma_sems.add(s)
            self.cnt[s] = self.cnt.get(s, 0) + d
        self.recs[eng].append((meth, a, k, tuple(w), inc))

    def dma(self, eng, semname, out, in_, w=(), **kw):
        self.op(eng, "dma_start", out=out, in_=in_, w=w, inc=(semname, 16), **kw)
        return (semname, self.cnt[semname])

    def wdma(self, semname):
        return (semname, self.val(semname))

    def flush(self):
        recs, self.recs = self.recs, {e: [] for e in ENGINES}
        if not any(recs.values()):
            return

        def emitter(rs):
            def f(e):
                for (meth, a, k, w, inc) in rs:
                    for (s, v) in w:
                        e.wait_ge(self.h[s], v)
                    ins = getattr(e, meth)(*a, **k)
                    if inc is not None:
                        s, d = inc
                        ins.then_inc(self.h[s], d)
            return f

        with self.nc.Block() as blk:
            for eng in ENGINES:
                if recs[eng]:
                    getattr(blk, eng)(emitter(recs[eng]))


def build_nc():
    nc = bass.Bass("TRN2", target_bir_lowering=False)
    ctx = ExitStack()
    nc._keepalive_ctx = ctx
    p = Prog(nc, ctx)

    def din(name, shape):
        return nc.dram_tensor(name, list(shape), F32, kind="ExternalInput").ap()

    x_hbm = din("x0", (TOK, D))
    wq_h, wk_h, wv_h = din("Wq", (D, D)), din("Wk", (D, D)), din("Wv", (D, D))
    wo_h, omg_h = din("Wo", (D, D)), din("omega", (M, DH))
    w1_h, b1_h = din("W1", (D, DFF)), din("b1", (32, 128))
    w2_h, b2_h = din("W2", (DFF, D)), din("b2", (1, D))
    gam_h, bet_h = din("gamma", (1, D)), din("beta", (1, D))
    clx_h, cld_h = din("clip_x", (1, 1)), din("clip_down", (1, 1))
    wg_h, wu_h, wd_h = din("w_gate", (I, D)), din("w_up", (I, D)), din("w_down", (D, I))
    hm_h, cm_h = din("hmask", (1, 1)), din("cmask", (1, 1))

    out_hbm = nc.dram_tensor("out", [TOK, D], F32, kind="ExternalOutput").ap()
    dbg_hbm = nc.dram_tensor("dbg", [128, NT * D], BF16, kind="ExternalOutput").ap()
    dbg2_hbm = nc.dram_tensor("dbg2", [128, 192], F32, kind="ExternalOutput").ap()

    sc_wgt = nc.dram_tensor("sc_wgt", [NI, 128, ND * 128], BF16).ap()
    sc_wut = nc.dram_tensor("sc_wut", [NI, 128, ND * 128], BF16).ap()
    sc_wdt = nc.dram_tensor("sc_wdt", [NI, 128, ND * 128], BF16).ap()
    sc_h = nc.dram_tensor("sc_h", [NI, 128, TOK], F32).ap()
    sc_x1 = nc.dram_tensor("sc_x1", [NT, 128, D], F32).ap()
    sc_a = nc.dram_tensor("sc_a", [NT, 128, D], F32).ap()
    sc_x2 = nc.dram_tensor("sc_x2", [NT, 128, D], F32).ap()
    sc_x2ln = nc.dram_tensor("sc_x2ln", [NT, 128, D], F32).ap()
    sc_row = nc.dram_tensor("sc_row", [8, 128], F32).ap()
    cc_in = nc.dram_tensor("cc_in", [1, 8], F32).ap()
    cc_out = nc.dram_tensor("cc_out", [1, 8], F32).ap()
    ccs_in = nc.dram_tensor("ccs_in", [128, NH * 65], BF16).ap()
    ccs_out = nc.dram_tensor("ccs_out", [128, NH * 65], BF16).ap()

    offA = nc.bump_sbuf(65536)[0]
    offB = nc.bump_sbuf(65536)[0]
    offC = nc.bump_sbuf(65536)[0]

    def at(name, base, off_b, shape, dt):
        return nc.alloc_sbuf_tensor_at(name, list(shape), dt, offset=base + off_b)

    x_sb = at("x_sb", offA, 0, [128, NT * D], F32)
    kpfm = at("kpfm", offA, 0, [128, NH * TOK], BF16)
    w2h_sb = at("w2h_b", offA, 0, [128, 2, 16, D], BF16)

    vtm = at("vtm", offB, 0, [128, NT * 1024], BF16)
    wv_sb = at("wv_sb", offB, 32768, [128, ND, 1024], BF16)
    s_sb = at("s_sb", offB, 32768, [128, NH * 15 * 65], BF16)
    hq_sb = at("hq_sb", offB, 0, [128, NI * 1024], BF16)
    hrd_sb = at("hrd_sb", offB, 45056, [128, 4, 1024], F32)
    dn_tmp = at("dn_tmp", offB, 61440, [128, 2, 512], F32)
    xlrd_sb = at("xlrd_sb", offB, 16384, [128, 4, 1024], F32)
    b1row = at("b1row", offB, 40960, [32, 128], F32)
    w1c_sb = at("w1c_b", offB, 0, [128, 2, ND * 128], BF16)
    pacc_sb = at("pacc_sb", offB, 4096, [128, 4, D], F32)
    x1t_sb = at("x1t_sb", offB, 20480, [128, D], F32)
    x2t_sb = at("x2t_sb", offB, 24576, [128, D], F32)
    xlnt_sb = at("xlnt_sb", offB, 28672, [128, D], F32)
    silu_sb = at("silu_sb", offB, 0, [128, TOK], F32)
    hcmb_sb = at("hcmb_sb", offB, 8192, [128, 2, TOK], F32)
    a_tm = at("a_tm", offB, 0, [128, NT * D], F32)
    wo_sb = at("wo_sb", offB, 0, [128, ND, 1024], BF16)
    wo_a = at("wo_a", offB, 16384, [128, 2, D], F32)
    h2tm = at("h2tm", offB, 24576, [128, NT * D], BF16)
    t2_sb = at("t2_sb", offB, 57344, [128, D], F32)
    sqd_sb = at("sqd_sb", offB, 61440, [128, D], F32)

    xqtm = at("xqtm", offC, 0, [128, NT * D], BF16)
    xqfm = at("xqfm", offC, 32768, [128, ND * TOK], BF16)
    u1_sb = at("u1_sb", offB, 0, [128, 4, D], F32)
    wgub = at("wgub", offC, 0, [128, 2, 2, ND * 128], BF16)
    wdt_sb = at("wdt_sb", offC, 0, [128, NI * 1024], BF16)
    afm = at("afm", offC, 0, [128, ND * TOK], BF16)
    abf_sb = at("abf_sb", offC, 32768, [128, NT * D], BF16)
    kfmp = at("kfmp", offC, 32768, [128, TOK], BF16)
    e_pair = at("e_pair", offC, 36864, [128, NT * 256], BF16)
    qktm = at("qktm", offC, 45056, [128, 2 * TOK], BF16)
    qpfm = at("qpfm", offC, 53248, [128, 2 * TOK], BF16)
    h2fm = at("h2fm", offC, 0, [128, ND * TOK], BF16)
    rfm = at("rfm", offC, 32768, [128, NF * 512], BF16)
    asb = at("asb", offC, 63520, [128, 2, 128], BF16)
    ctrb = at("ctrb", offC, 61440, [128, NH * 65], BF16)
    scar = at("scar", offC, 61440, [128, NH * 65], BF16)
    wraw = at("wraw", offC, 0, [128, 2, I], F32)
    wqi = at("wqi", offC, 22528, [128, 2, I], F32)
    wtev = at("wtev", offC, 45056, [128, 2, I], BF16)

    def sb(name, shape, dt):
        return ctx.enter_context(nc.sbuf_tensor(name, list(shape), dt))

    gam_b = sb("gam_b", [128, D], BF16)
    bet_b = sb("bet_b", [128, D], BF16)
    b2_b = sb("b2_b", [128, D], BF16)
    swd_b = sb("swd_b", [128, D], BF16)
    b1_pp = sb("b1_pp", [128, 32], F32)
    swg_pp = sb("swg_pp", [128, NI], F32)
    swu_pp = sb("swu_pp", [128, NI], F32)
    tri_f = sb("tri_f", [128, 128], BF16)
    idf = sb("idf", [128, 128], F32)
    omgbd = sb("omgbd", [128, 256], BF16)
    ones_r = sb("ones_r", [1, 128], F32)
    ones_c = sb("ones_c", [128, 1], BF16)
    scal = sb("scal", [128, 16], F32)
    stats = sb("stats", [128, 136], F32)
    phst = sb("phst", [128, 64], F32)

    wkp_sb = sb("wkp_sb", [128, 2, ND, 128], BF16)
    tiny = sb("tiny", [128, 16], F32)
    nmag_pp = sb("nmag_pp", [128, 1], F32)
    pmag_pp = sb("pmag_pp", [128, 1], F32)
    eps5_pp = sb("eps5_pp", [128, 1], F32)
    eps6_pp = sb("eps6_pp", [128, 1], F32)
    feps_pp = sb("feps_pp", [128, 1], F32)
    zero_pp = sb("zero_pp", [128, 1], F32)
    onep = sb("onep", [1, 16], F32)


    SCL_SX, SCL_RX, SCL_RH, SCL_SH05, SCL_HM, SCL_CM = 0, 1, 2, 3, 4, 5
    SCL_GMX, SCL_SGX, SCL_SGD = 6, 8, 9
    ST_SUM, ST_SSQ, ST_NMU, ST_RSQ, ST_SPP, ST_M2, ST_RMX, ST_RMN = (
        0, 16, 32, 48, 64, 80, 96, 112)
    ST_T1, ST_T2 = 128, 132

    pb = [ctx.enter_context(nc.psum_tensor(f"pb{i}", [128, 512], F32)) for i in range(8)]

    def bcast_from_p0(src_ap, dst_ap, w=()):
        p.op("tensor", "matmul", pb[7][:, 0:1], ones_r[:, :], src_ap,
             start=True, stop=True, w=w + (("ae", p.val("ae")),), inc=("pe", 1))
        p.op("scalar", "copy", dst_ap, pb[7][:, 0:1],
             w=(("pe", p.val("pe")),), inc=("ae", 1))

    def part_max_to_p0(src_col_ap, dst_ap, w=()):
        p.op("tensor", "transpose", pb[7][0:1, 0:128], src_col_ap, idf[:, :],
             w=w + (("ve", p.val("ve")), ("ge", p.val("ge")), ("ae", p.val("ae"))),
             inc=("pe", 1))
        p.op("vector", "reduce_max", dst_ap, pb[7][0:1, 0:128], AX.X,
             w=(("pe", p.val("pe")),), inc=("ve", 1))

    def global_absmax_chain(run_col_ap, sig_col, sx_col, rx_col, sh05_col=None, w=()):
        part_max_to_p0(run_col_ap, onep[0:1, 0:1], w=w)
        p.op("vector", "tensor_copy", onep[0:1, 1:8], onep[0:1, 0:1].broadcast_to((1, 7)),
             inc=("ve", 1))
        p.dma("sync", "so", out=cc_in, in_=onep[0:1, 0:8],
              w=(("ve", p.val("ve")), ("cc", p.val("cc"))))
        p.op("gpsimd", "collective_compute", "AllReduce", ALU.max,
             [list(range(8))], [cc_in], [cc_out],
             w=(p.wdma("so"), p.wdma("c0")), inc=("cc", 1))
        p.dma("sync", "c0", out=onep[0:1, 8:16], in_=cc_out, w=(("cc", p.val("cc")),))
        bcast_from_p0(onep[0:1, 8:9], scal[:, SCL_GMX:SCL_GMX + 1], w=(p.wdma("c0"),))
        p.op("vector", "tensor_scalar", scal[:, 10:11],
             scal[:, sig_col:sig_col + 1], scal[:, SCL_GMX:SCL_GMX + 1], 1.0 / QMAX,
             ALU.mult, ALU.mult, w=(("ae", p.val("ae")),), inc=("ve", 1))
        p.op("scalar", "copy", scal[:, sx_col:sx_col + 1], scal[:, 10:11],
             w=(("ve", p.val("ve")),), inc=("ae", 1))
        p.op("scalar", "activation", scal[:, rx_col:rx_col + 1],
             scal[:, 10:11], AF.Ln, inc=("ae", 1))
        p.op("scalar", "activation", scal[:, rx_col:rx_col + 1],
             scal[:, rx_col:rx_col + 1], AF.Exp, scale=-1.0, inc=("ae", 1))
        if sh05_col is not None:
            p.op("scalar", "activation", scal[:, sh05_col:sh05_col + 1],
                 scal[:, 10:11], AF.Copy, scale=0.5, inc=("ae", 1))
        p.op("vector", "nop", w=(("ae", p.val("ae")),), inc=("ve", 1))

    def ln_stats_tiles(src_ap_fn, eps, with_minmax, w0=()):
        for t in range(NT):
            xt = src_ap_fn(t)
            ww = w0 if t == 0 else ()
            p.op("vector", "reduce_sum", stats[:, ST_SUM + t:ST_SUM + t + 1], xt, AX.X,
                 w=ww, inc=("ve", 1))
            p.op("scalar", "activation", sqd_sb[:, :], xt, AF.Square,
                 accum_out=stats[:, ST_SSQ + t:ST_SSQ + t + 1], w=ww, inc=("ae", 1))
        W = (("ve", p.val("ve")), ("ae", p.val("ae")), ("ge", p.val("ge")))
        sU, sQ = stats[:, ST_SUM:ST_SUM + 16], stats[:, ST_SSQ:ST_SSQ + 16]
        nmu, rsq = stats[:, ST_NMU:ST_NMU + 16], stats[:, ST_RSQ:ST_RSQ + 16]
        t1 = stats[:, ST_SPP:ST_SPP + 16]
        t2 = stats[:, ST_M2:ST_M2 + 16]
        p.op("gpsimd", "tensor_scalar", nmu, sU, -1.0 / D, None, ALU.mult, w=W, inc=("ge", 1))
        p.op("gpsimd", "tensor_scalar", t1, sQ, 1.0 / D, eps, ALU.mult, ALU.add, inc=("ge", 1))
        p.op("scalar", "activation", t2, nmu, AF.Square, w=(("ge", p.val("ge")),),
             inc=("ae", 1))
        p.op("vector", "tensor_sub", t1, t1, t2,
             w=(("ge", p.val("ge")), ("ae", p.val("ae"))), inc=("ve", 1))
        p.op("scalar", "activation", t2, t1, AF.Ln, w=(("ve", p.val("ve")),), inc=("ae", 1))
        p.op("scalar", "activation", rsq, t2, AF.Exp, scale=-0.5, inc=("ae", 1))
        if with_minmax:
            for t in range(NT):
                buf = t % 4
                p.op("vector", "tensor_scalar", u1_sb[:, buf, :], src_ap_fn(t),
                     stats[:, ST_NMU + t:ST_NMU + t + 1],
                     stats[:, ST_RSQ + t:ST_RSQ + t + 1], ALU.add, ALU.mult,
                     w=(("ae", p.val("ae")), ("ge", p.val("ge"))) if t == 0 else (),
                     inc=("ve", 1))
                p.op("vector", "tensor_reduce", stats[:, ST_RMX + t:ST_RMX + t + 1],
                     u1_sb[:, buf, :], AX.X, ALU.max, apply_absolute_value=True,
                     inc=("ve", 1))
            p.op("scalar", "copy", stats[:, ST_RMN:ST_RMN + 16],
                 stats[:, ST_RMX:ST_RMX + 16], w=(("ve", p.val("ve")),), inc=("ae", 1))
            p.op("vector", "reduce_max", tiny[:, 12:13], stats[:, ST_RMN:ST_RMN + 16],
                 AX.X, w=(("ae", p.val("ae")),), inc=("ve", 1))

    def dma_transpose_cmaj(src, dst, nseg, chunk=32):
        for s0 in range(0, nseg, chunk):
            n = min(chunk, nseg - s0)
            p.op("scalar", "dma_start_transpose",
                 dst[:, s0 * 128:(s0 + n) * 128].rearrange("p (s f) -> p s f", f=128),
                 src[:, s0 * 128:(s0 + n) * 128].rearrange("p (s f) -> p s f", f=128),
                 w=(("ve", p.val("ve")), ("ge", p.val("ge")), ("ae", p.val("ae")),
                    ("pe", p.val("pe"))),
                 inc=("t0", 16))

    def xtile(t):
        return x_sb[:, t * D:(t + 1) * D]

    def finish_dump_stream(dbg_src=None):
        if dbg_src is not None:
            p.dma("sync", "so", out=dbg_hbm[:, 0:dbg_src.shape[1]], in_=dbg_src[:, :],
                  w=(("ve", p.val("ve")), ("ge", p.val("ge")), ("ae", p.val("ae")),
                     ("pe", p.val("pe")), p.wdma("t0")))
            p.dma("sync", "so", out=dbg2_hbm[:, 0:16], in_=scal[:, :])
            p.dma("sync", "so", out=dbg2_hbm[:, 16:152], in_=stats[:, :])
            p.dma("sync", "so", out=dbg2_hbm[:, 152:168], in_=tiny[:, :])
            p.dma("sync", "so", out=dbg2_hbm[0:1, 168:184], in_=onep[:, :])
        for t in range(NT):
            p.dma("sync", "so", out=out_hbm.rearrange("(tt p) d -> p tt d", p=128)[:, t, :],
                  in_=xtile(t),
                  w=(("ve", p.val("ve")), ("ge", p.val("ge")), ("ae", p.val("ae")),
                     ("pe", p.val("pe"))))
        p.op("sync", "nop", w=(p.wdma("so"),))
        p.flush()

    p.dma("sync", "c0", out=x_sb[:, :].rearrange("p (t d) -> p t d", d=D),
          in_=x_hbm.rearrange("(t p) d -> p t d", p=128))
    p.dma("gpsimd", "g0", out=gam_b[:, :], in_=gam_h.partition_broadcast(128))
    p.dma("gpsimd", "g0", out=bet_b[:, :], in_=bet_h.partition_broadcast(128))
    p.dma("gpsimd", "g0", out=b2_b[:, :], in_=b2_h.partition_broadcast(128))
    p.dma("sync", "c1", out=tiny[0:1, 0:1], in_=clx_h)
    p.dma("sync", "c1", out=tiny[0:1, 1:2], in_=cld_h)
    p.dma("sync", "c1", out=tiny[0:1, 2:3], in_=hm_h)
    p.dma("sync", "c1", out=tiny[0:1, 3:4], in_=cm_h)
    p.dma("sync", "c1", out=b1row[:, :], in_=b1_h)
    p.dma("sync", "c1", out=wraw[0:128, 0, 0:64], in_=omg_h)

    p.op("gpsimd", "memset", idf[:, :], 1.0)
    p.op("gpsimd", "affine_select", idf[:, :], idf[:, :], [[-1, 128]], ALU.is_equal,
         0.0, base=0, channel_multiplier=1)
    p.op("gpsimd", "memset", tri_f[:, :], 1.0)
    p.op("gpsimd", "affine_select", tri_f[:, :], tri_f[:, :], [[1, 128]], ALU.is_ge,
         0.0, base=0, channel_multiplier=-1)
    p.op("gpsimd", "memset", ones_r[:, :], 1.0)
    p.op("gpsimd", "memset", ones_c[:, :], 1.0)
    p.op("gpsimd", "memset", omgbd[:, :], 0.0)
    p.op("gpsimd", "memset", onep[:, :], 0.0)
    p.op("gpsimd", "memset", nmag_pp[:, :], -MAGIC)
    p.op("gpsimd", "memset", pmag_pp[:, :], MAGIC)
    p.op("gpsimd", "memset", eps5_pp[:, :], LN_EPS)
    p.op("gpsimd", "memset", eps6_pp[:, :], LN1E6)
    p.op("gpsimd", "memset", feps_pp[:, :], FEAT_EPS)
    p.op("gpsimd", "memset", zero_pp[:, :], 0.0)
    p.op("gpsimd", "memset", scal[:, :], 0.0, inc=("ge", 1))

    GE0 = (("ge", p.val("ge")),)
    WC1 = (p.wdma("c1"),)

    bcast_from_p0(tiny[0:1, 2:3], scal[:, SCL_HM:SCL_HM + 1], w=WC1 + GE0)
    bcast_from_p0(tiny[0:1, 3:4], scal[:, SCL_CM:SCL_CM + 1])
    for (cidx, dcol) in ((0, SCL_SGX), (1, SCL_SGD)):
        bcast_from_p0(tiny[0:1, cidx:cidx + 1], scal[:, dcol:dcol + 1])
        p.op("scalar", "activation", scal[:, dcol:dcol + 1], scal[:, dcol:dcol + 1],
             AF.Exp, scale=-1.0, w=(("ve", p.val("ve")),), inc=("ae", 1))
        p.op("gpsimd", "tensor_scalar", scal[:, dcol:dcol + 1], scal[:, dcol:dcol + 1],
             1.0, None, ALU.add, w=(("ae", p.val("ae")),), inc=("ge", 1))
        p.op("scalar", "activation", scal[:, dcol:dcol + 1], scal[:, dcol:dcol + 1],
             AF.Ln, w=(("ge", p.val("ge")),), inc=("ae", 1))
        p.op("scalar", "activation", scal[:, dcol:dcol + 1], scal[:, dcol:dcol + 1],
             AF.Exp, scale=-1.0, inc=("ae", 1))

    p.op("tensor", "transpose", pb[6][:, 0:32], b1row[:, :], idf[0:32, 0:32],
         w=WC1 + GE0 + (("ve", p.val("ve")),), inc=("pe", 1))
    p.op("vector", "tensor_copy", b1_pp[:, :], pb[6][:, 0:32],
         w=(("pe", p.val("pe")),), inc=("ve", 1))

    p.op("tensor", "transpose", pb[5][0:64, 0:128], wraw[0:128, 0, 0:64], idf[:, :],
         w=WC1 + GE0, inc=("pe", 1))
    p.op("vector", "tensor_scalar", omgbd[0:64, 0:128], pb[5][0:64, 0:128], SCALE_QK,
         None, ALU.mult, w=(("pe", p.val("pe")),), inc=("ve", 1))
    p.op("vector", "tensor_copy", omgbd[64:128, 128:256], omgbd[0:64, 0:128],
         inc=("ve", 1))

    def quant_and_transpose(src_hbm, scr, sw_col_ap, ncols, dtile):
        buf = dtile % 2
        wv = (("ve", p.val("ve")), ("ae", p.val("ae")), ("ge", p.val("ge")),
              ("pe", p.val("pe")))
        tok_ld = p.dma("sync", "c" + str(buf), out=wraw[:, buf, 0:ncols],
                       in_=src_hbm, w=wv)
        WL = (p.wdma("c" + str(buf)),)
        p.op("vector", "tensor_reduce", tiny[:, 4 + buf:5 + buf], wraw[:, buf, 0:ncols],
             AX.X, ALU.max, apply_absolute_value=True, w=WL, inc=("ve", 1))
        p.op("gpsimd", "tensor_scalar", sw_col_ap, tiny[:, 4 + buf:5 + buf], 1.0 / QMAX,
             None, ALU.mult, w=(("ve", p.val("ve")),), inc=("ge", 1))
        p.op("scalar", "activation", tiny[:, 6 + buf:7 + buf], tiny[:, 4 + buf:5 + buf],
             AF.Ln, w=(("ve", p.val("ve")),), inc=("ae", 1))
        p.op("scalar", "activation", tiny[:, 6 + buf:7 + buf], tiny[:, 6 + buf:7 + buf],
             AF.Exp, scale=-1.0, inc=("ae", 1))
        p.op("vector", "tensor_scalar", wraw[:, buf, 0:ncols], wraw[:, buf, 0:ncols],
             tiny[:, 6 + buf:7 + buf], None, ALU.mult, w=(("ae", p.val("ae")),),
             inc=("ve", 1))
        p.op("vector", "tensor_scalar", wraw[:, buf, 0:ncols], wraw[:, buf, 0:ncols],
             QMAX, MAGIC, ALU.mult, ALU.add, inc=("ve", 1))
        p.op("scalar", "add", wraw[:, buf, 0:ncols], wraw[:, buf, 0:ncols], nmag_pp[:, :],
             w=(("ve", p.val("ve")),), inc=("ae", 1))
        p.op("vector", "tensor_scalar", wqi[:, buf, 0:ncols], wraw[:, buf, 0:ncols],
             -QMAX - 1.0, QMAX, ALU.max, ALU.min,
             w=(("ae", p.val("ae")), ("pe", p.val("pe"))), inc=("ve", 1))
        vq = p.val("ve")
        nblk = ncols // 128
        aeq0 = p.val("ae")
        so_tok = p.wdma("so")
        for k in range(nblk):
            bank = k % 4
            p.op("tensor", "transpose", pb[bank][:, 0:128],
                 wqi[:, buf, k * 128:(k + 1) * 128], idf[:, :],
                 w=(("ve", vq), ("ae", aeq0)), inc=("pe", 1))
            p.op("scalar", "copy", wtev[:, buf, k * 128:(k + 1) * 128], pb[bank][:, 0:128],
                 w=(("pe", p.val("pe")),) + ((so_tok,) if k == 0 else ()), inc=("ae", 1))
        p.dma("sync", "so", out=scr, in_=wtev[:, buf, 0:nblk * 128],
              w=(("ae", p.val("ae")),))

    for i in range(NI):
        quant_and_transpose(wg_h[i * 128:(i + 1) * 128, :], sc_wgt[i],
                            swg_pp[:, i:i + 1], 1024, i)
    for i in range(NI):
        quant_and_transpose(wu_h[i * 128:(i + 1) * 128, :], sc_wut[i],
                            swu_pp[:, i:i + 1], 1024, i)
    for dt in range(ND):
        buf = dt % 2
        wv = (("ve", p.val("ve")), ("ae", p.val("ae")), ("ge", p.val("ge")),
              ("pe", p.val("pe")))
        p.dma("sync", "c" + str(buf), out=wraw[:, buf, :],
              in_=wd_h[dt * 128:(dt + 1) * 128, :], w=wv)
        WL = (p.wdma("c" + str(buf)),)
        p.op("vector", "tensor_reduce", tiny[:, 4 + buf:5 + buf], wraw[:, buf, :],
             AX.X, ALU.max, apply_absolute_value=True, w=WL, inc=("ve", 1))
        p.op("gpsimd", "tensor_scalar", stats[:, ST_RMX + dt:ST_RMX + dt + 1],
             tiny[:, 4 + buf:5 + buf], 1.0 / QMAX, None, ALU.mult,
             w=(("ve", p.val("ve")),), inc=("ge", 1))
        p.op("scalar", "activation", tiny[:, 6 + buf:7 + buf], tiny[:, 4 + buf:5 + buf],
             AF.Ln, w=(("ve", p.val("ve")),), inc=("ae", 1))
        p.op("scalar", "activation", tiny[:, 6 + buf:7 + buf], tiny[:, 6 + buf:7 + buf],
             AF.Exp, scale=-1.0, inc=("ae", 1))
        p.op("vector", "tensor_scalar", wraw[:, buf, :], wraw[:, buf, :],
             tiny[:, 6 + buf:7 + buf], None, ALU.mult, w=(("ae", p.val("ae")),),
             inc=("ve", 1))
        p.op("vector", "tensor_scalar", wraw[:, buf, :], wraw[:, buf, :],
             QMAX, MAGIC, ALU.mult, ALU.add, inc=("ve", 1))
        p.op("scalar", "add", wraw[:, buf, :], wraw[:, buf, :], nmag_pp[:, :],
             w=(("ve", p.val("ve")),), inc=("ae", 1))
        p.op("vector", "tensor_scalar", wqi[:, buf, :], wraw[:, buf, :],
             -QMAX - 1.0, QMAX, ALU.max, ALU.min,
             w=(("ae", p.val("ae")), ("pe", p.val("pe"))), inc=("ve", 1))
        vq = p.val("ve")
        aeq0 = p.val("ae")
        so_tok = p.wdma("so")
        for i in range(NI):
            bank = i % 4
            p.op("tensor", "transpose", pb[bank][:, 0:128],
                 wqi[:, buf, i * 128:(i + 1) * 128], idf[:, :],
                 w=(("ve", vq), ("ae", aeq0)), inc=("pe", 1))
            p.op("scalar", "copy", wtev[:, buf, i * 128:(i + 1) * 128], pb[bank][:, 0:128],
                 w=(("pe", p.val("pe")),) + ((so_tok,) if i == 0 else ()), inc=("ae", 1))
        p.dma("sync", "so",
              out=sc_wdt[:, :, dt * 128:(dt + 1) * 128].transpose([1, 0, 2]),
              in_=wtev[:, buf, 0:NI * 128].rearrange("p (i f) -> p i f", f=128),
              w=(("ae", p.val("ae")),))

    p.op("tensor", "transpose", pb[6][0:8, 0:128], stats[:, ST_RMX:ST_RMX + 8],
         idf[:, :], w=(("ve", p.val("ve")),), inc=("pe", 1))
    p.op("vector", "tensor_copy", hrd_sb[0:8, 0, 0:128], pb[6][0:8, 0:128],
         w=(("pe", p.val("pe")),), inc=("ve", 1))
    p.dma("sync", "so", out=sc_row, in_=hrd_sb[0:8, 0, 0:128],
          w=(("ve", p.val("ve")),))
    p.dma("gpsimd", "g0", out=swd_b[:, :],
          in_=sc_row.rearrange("p f -> (p f)").unsqueeze(0).partition_broadcast(128),
          w=(p.wdma("so"),))
    if N_PHASES <= 0:
        p.flush(); return nc
    p.flush()

    ln_stats_tiles(lambda t: xtile(t), LN_EPS, with_minmax=True, w0=(p.wdma("c0"),))
    global_absmax_chain(tiny[:, 12:13], SCL_SGX, SCL_SX, SCL_RX)
    p.op("vector", "tensor_scalar", stats[:, ST_RMN:ST_RMN + 16],
         stats[:, ST_RSQ:ST_RSQ + 16], scal[:, SCL_RX:SCL_RX + 1], None, ALU.mult,
         w=(("ae", p.val("ae")),), inc=("ve", 1))
    p.op("scalar", "copy", stats[:, ST_SPP:ST_SPP + 16], stats[:, ST_RMN:ST_RMN + 16],
         w=(("ve", p.val("ve")),), inc=("ae", 1))

    def quant_tile_to_cmaj(src_ap, t, nmu_ap, spp_ap, dst_tm):
        buf = t % 4
        p.op("vector", "tensor_scalar", u1_sb[:, buf, :], src_ap, nmu_ap, spp_ap,
             ALU.add, ALU.mult, w=(("ae", p.val("ae")), ("ge", p.val("ge"))),
             inc=("ve", 1))
        p.op("scalar", "add", u1_sb[:, buf, :], u1_sb[:, buf, :], pmag_pp[:, :],
             w=(("ve", p.val("ve")),), inc=("ae", 1))
        p.op("vector", "tensor_scalar", u1_sb[:, buf, :], u1_sb[:, buf, :],
             -MAGIC, -QMAX - 1.0, ALU.add, ALU.max, w=(("ae", p.val("ae")),),
             inc=("ve", 1))
        p.op("vector", "tensor_scalar",
             dst_tm[:, :].rearrange("p (c f) -> p c f", f=TOK)[:, :, t * 128:(t + 1) * 128],
             u1_sb[:, buf, :].rearrange("p (c f) -> p c f", f=128),
             QMAX, None, ALU.min, inc=("ve", 1))

    for t in range(NT):
        quant_tile_to_cmaj(xtile(t), t, stats[:, ST_NMU + t:ST_NMU + t + 1],
                           stats[:, ST_SPP + t:ST_SPP + t + 1], xqtm)
    dma_transpose_cmaj(xqtm, xqfm, NT * ND)
    if N_PHASES <= 1:
        finish_dump_stream(dbg_src=xqfm if DEBUG_TAP == "xqfm" else xqtm)
        return nc
    p.flush()

    def ffn_front(call):
        WT = (p.wdma("t0"),)
        for i in range(2):
            p.dma("sync", "c" + str(i % 2), out=wgub[:, i % 2, 0, :], in_=sc_wgt[i])
            p.dma("sync", "c" + str(i % 2), out=wgub[:, i % 2, 1, :], in_=sc_wut[i])
        bank_free = [("ae", p.val("ae")), ("ae", p.val("ae")),
                     ("ve", p.val("ve")), ("ve", p.val("ve"))]
        for i in range(NI):
            bi = i % 2
            p.op("vector", "tensor_scalar", tiny[:, 10:11], swg_pp[:, i:i + 1],
                 scal[:, SCL_SX:SCL_SX + 1], None, ALU.mult,
                 w=(("ae", p.val("ae")), ("ge", p.val("ge"))), inc=("ve", 1))
            p.op("vector", "tensor_scalar", tiny[:, 13:14], swu_pp[:, i:i + 1],
                 scal[:, SCL_SX:SCL_SX + 1], None, ALU.mult, inc=("ve", 1))
            p.op("gpsimd", "tensor_copy", tiny[:, 11:12], tiny[:, 13:14],
                 w=(("ve", p.val("ve")),), inc=("ge", 1))
            sc_ve = ("ve", p.val("ve"))
            WLD = (p.wdma("c" + str(bi)),)
            for n in range(4):
                bg, bu = n % 2, 2 + n % 2
                for k in range(ND):
                    ww = (WT + WLD + (bank_free[bg],)) if k == 0 else ()
                    p.op("tensor", "matmul", pb[bg][:, :], wgub[:, bi, 0, k * 128:(k + 1) * 128],
                         xqfm[:, k * TOK + n * 512:k * TOK + (n + 1) * 512],
                         start=(k == 0), stop=(k == ND - 1), w=ww,
                         inc=("pe", 1) if k == ND - 1 else None)
                p.op("scalar", "activation", silu_sb[:, n * 512:(n + 1) * 512], pb[bg][:, :],
                     AF.Silu, scale=tiny[:, 10:11], w=(("pe", p.val("pe")), sc_ve),
                     inc=("ae", 1))
                bank_free[bg] = ("ae", p.val("ae"))
                for k in range(ND):
                    ww = (bank_free[bu],) if k == 0 else ()
                    p.op("tensor", "matmul", pb[bu][:, :], wgub[:, bi, 1, k * 128:(k + 1) * 128],
                         xqfm[:, k * TOK + n * 512:k * TOK + (n + 1) * 512],
                         start=(k == 0), stop=(k == ND - 1), w=ww,
                         inc=("pe", 1) if k == ND - 1 else None)
                wwh = (("pe", p.val("pe")), ("ae", p.val("ae")), ("ge", p.val("ge")))
                if i >= 2 and n == 0:
                    wwh = wwh + (p.wdma("so"),)
                p.op("vector", "scalar_tensor_tensor", hcmb_sb[:, bi, n * 512:(n + 1) * 512],
                     pb[bu][:, :], tiny[:, 11:12], silu_sb[:, n * 512:(n + 1) * 512],
                     ALU.mult, ALU.mult, w=wwh, inc=("ve", 1))
                bank_free[bu] = ("ve", p.val("ve"))
            p.op("vector", "tensor_reduce", stats[:, i:i + 1], hcmb_sb[:, bi, :], AX.X,
                 ALU.max, apply_absolute_value=True, inc=("ve", 1))
            p.dma("sync", "so", out=sc_h[i], in_=hcmb_sb[:, bi, :],
                  w=(("ve", p.val("ve")),))
            if i + 2 < NI:
                p.dma("sync", "c" + str(i % 2), out=wgub[:, i % 2, 0, :], in_=sc_wgt[i + 2],
                      w=(("pe", p.val("pe")),))
                p.dma("sync", "c" + str(i % 2), out=wgub[:, i % 2, 1, :], in_=sc_wut[i + 2])
        p.op("scalar", "copy", stats[:, 32:32 + NI], stats[:, 0:NI],
             w=(("ve", p.val("ve")),), inc=("ae", 1))
        p.op("vector", "reduce_max", tiny[:, 8:9], stats[:, 32:32 + NI], AX.X,
             w=(("ae", p.val("ae")),), inc=("ve", 1))
        global_absmax_chain(tiny[:, 8:9], SCL_SGD, 7, SCL_RH, sh05_col=SCL_SH05)

    def ffn_down(call):
        for i in range(NI):
            p.dma("sync", "c" + str(i % 2), out=wdt_sb[:, i * 1024:(i + 1) * 1024],
                  in_=sc_wdt[i], w=(("pe", p.val("pe")),) if i < 2 else ())
        for half in range(2):
            def h_load(i):
                ww = (("ve", p.val("ve")),)
                if half == 0 and i < 4:
                    ww = (p.wdma("so"),)
                p.dma("sync", "h" + str(i % 4), out=hrd_sb[:, i % 4, :],
                      in_=sc_h[i][:, half * 1024:(half + 1) * 1024], w=ww)
            for i in range(4):
                h_load(i)
            for i in range(NI):
                bi = i % 4
                p.op("vector", "tensor_scalar", hrd_sb[:, bi, :], hrd_sb[:, bi, :],
                     scal[:, SCL_RH:SCL_RH + 1], MAGIC, ALU.mult, ALU.add,
                     w=(p.wdma("h" + str(i % 4)),), inc=("ve", 1))
                p.op("scalar", "add", hrd_sb[:, bi, :], hrd_sb[:, bi, :], nmag_pp[:, :],
                     w=(("ve", p.val("ve")),), inc=("ae", 1))
                p.op("vector", "tensor_scalar", hq_sb[:, i * 1024:(i + 1) * 1024],
                     hrd_sb[:, bi, :], -QMAX - 1.0, QMAX, ALU.max, ALU.min,
                     w=(("ae", p.val("ae")),), inc=("ve", 1))
                if i + 4 < NI:
                    h_load(i + 4)
            VQ = (("ve", p.val("ve")),)
            for t8 in range(8):
                t = half * 8 + t8
                for n in range(2):
                    bank = (t8 * 2 + n) % 4
                    for i in range(NI):
                        ww = VQ if i == 0 else ()
                        p.op("tensor", "matmul", pb[bank][:, :],
                             hq_sb[:, i * 1024 + t8 * 128:i * 1024 + (t8 + 1) * 128],
                             wdt_sb[:, i * 1024 + n * 512:i * 1024 + (n + 1) * 512],
                             start=(i == 0), stop=(i == NI - 1), w=ww,
                             inc=("pe", 1) if i == NI - 1 else None)
                    bt = bank % 2
                    p.op("vector", "scalar_tensor_tensor", dn_tmp[:, bt, :], pb[bank][:, :],
                         scal[:, SCL_SH05:SCL_SH05 + 1], swd_b[:, n * 512:(n + 1) * 512],
                         ALU.mult, ALU.mult, w=(("pe", p.val("pe")),), inc=("ve", 1))
                    p.op("vector", "tensor_add",
                         x_sb[:, t * D + n * 512:t * D + (n + 1) * 512],
                         x_sb[:, t * D + n * 512:t * D + (n + 1) * 512],
                         dn_tmp[:, bt, :], inc=("ve", 1))
                if call == 1:
                    p.dma("sync", "so", out=out_hbm.rearrange("(tt p) d -> p tt d", p=128)[:, t, :],
                          in_=xtile(t), w=(("ve", p.val("ve")),))

    ffn_front(0)
    if N_PHASES <= 2:
        finish_dump_stream(); return nc
    p.flush()
    ffn_down(0)
    if N_PHASES <= 3:
        finish_dump_stream(); return nc
    p.flush()

    ln_stats_tiles(lambda t: xtile(t), LN_EPS, with_minmax=False)
    for t in range(NT):
        p.op("vector", "tensor_scalar", a_tm[:, t * D:(t + 1) * D], xtile(t),
             stats[:, ST_NMU + t:ST_NMU + t + 1], stats[:, ST_RSQ + t:ST_RSQ + t + 1],
             ALU.add, ALU.mult,
             w=(("ae", p.val("ae")), ("ge", p.val("ge"))) if t == 0 else (),
             inc=("ve", 1))
        p.op("vector", "tensor_scalar",
             abf_sb[:, :].rearrange("p (c f) -> p c f", f=TOK)[:, :, t * 128:(t + 1) * 128],
             xtile(t).rearrange("p (c f) -> p c f", f=128),
             stats[:, ST_NMU + t:ST_NMU + t + 1], stats[:, ST_RSQ + t:ST_RSQ + t + 1],
             ALU.add, ALU.mult, inc=("ve", 1))
    p.dma("sync", "so", out=sc_a.transpose([1, 0, 2]),
          in_=a_tm[:, :].rearrange("p (t d) -> p t d", d=D), w=(("ve", p.val("ve")),))
    p.dma("sync", "so", out=sc_x1.transpose([1, 0, 2]),
          in_=x_sb[:, :].rearrange("p (t d) -> p t d", d=D))
    dma_transpose_cmaj(abf_sb, afm, NT * ND)
    if N_PHASES <= 4:
        finish_dump_stream(dbg_src=afm)
        return nc
    p.flush()

    p.dma("gpsimd", "g0", out=wv_sb[:, :, :],
          in_=wv_h.rearrange("(k pp) j -> pp k j", pp=128))

    def load_wpair(w_h, pr, ww=()):
        p.dma("gpsimd", "g1", out=wkp_sb[:, pr % 2, :, :],
              in_=w_h.rearrange("(k pp) (pr j) -> pp k pr j", pp=128, j=128)[:, :, pr, :],
              w=ww)
    load_wpair(wk_h, 0)
    load_wpair(wk_h, 1)
    WT0 = (p.wdma("t0"),)
    for t in range(NT):
        for n in range(2):
            bank = (2 * t + n) % 4
            for k in range(ND):
                ww = (WT0 + (p.wdma("g0"), ("ae", p.val("ae")))) if k == 0 else ()
                p.op("tensor", "matmul", pb[bank][:, :],
                     afm[:, k * TOK + t * 128:k * TOK + (t + 1) * 128],
                     wv_sb[:, k, n * 512:(n + 1) * 512],
                     start=(k == 0), stop=(k == ND - 1), w=ww,
                     inc=("pe", 1) if k == ND - 1 else None)
            p.op("scalar", "copy", vtm[:, t * 1024 + n * 512:t * 1024 + (n + 1) * 512],
                 pb[bank][:, :], w=(("pe", p.val("pe")),), inc=("ae", 1))

    def qk_phi(pr, wsem):
        for n in range(4):
            bank = n % 2
            for k in range(ND):
                ww = ((p.wdma(wsem), ("ae", p.val("ae")), ("ve", p.val("ve")))
                      if k == 0 else ())
                p.op("tensor", "matmul", pb[bank][:, :],
                     wkp_sb[:, pr % 2, k, :],
                     afm[:, k * TOK + n * 512:k * TOK + (n + 1) * 512],
                     start=(k == 0), stop=(k == ND - 1), w=ww,
                     inc=("pe", 1) if k == ND - 1 else None)
            p.op("scalar", "copy", kfmp[:, n * 512:(n + 1) * 512], pb[bank][:, :],
                 w=(("pe", p.val("pe")),), inc=("ae", 1))
        AEK = (("ae", p.val("ae")),)
        for t in range(NT):
            bank = 4 + t % 2
            p.op("tensor", "matmul", pb[bank][:, 0:256],
                 kfmp[:, t * 128:(t + 1) * 128], omgbd[:, :], start=True, stop=True,
                 w=AEK + (("ae", p.val("ae")),), inc=("pe", 1))
            wwx = (("pe", p.val("pe")),)
            if t == 0:
                wwx = wwx + (("ve", p.val("ve")),)
            p.op("scalar", "activation", e_pair[:, t * 256:(t + 1) * 256],
                 pb[bank][:, 0:256], AF.Exp, w=wwx, inc=("ae", 1))
        AEE = (("ae", p.val("ae")),)
        for h01 in range(2):
            p.op("vector", "tensor_reduce", phst[:, h01 * 16:(h01 + 1) * 16],
                 e_pair[:, :].rearrange("p (t f) -> p t f", f=256)
                 [:, :, h01 * 128:(h01 + 1) * 128],
                 AX.X, ALU.max, w=AEE, inc=("ve", 1))
        p.op("scalar", "activation", phst[:, 32:64], phst[:, 0:32], AF.Ln,
             scale=SQM, w=(("ve", p.val("ve")),), inc=("ae", 1))
        p.op("scalar", "activation", phst[:, 32:64], phst[:, 32:64], AF.Exp,
             scale=-1.0, inc=("ae", 1))
        AEF = (("ae", p.val("ae")),)
        for h01 in range(2):
            for t in range(NT):
                wwq = ()
                if h01 == 0 and t == 0:
                    wwq = AEF + (("pe", p.val("pe")), p.wdma("t0"))
                p.op("vector", "tensor_scalar",
                     qktm[:, h01 * TOK + t * 128:h01 * TOK + (t + 1) * 128],
                     e_pair[:, t * 256 + h01 * 128:t * 256 + (h01 + 1) * 128],
                     phst[:, 32 + h01 * 16 + t:33 + h01 * 16 + t], FEAT_EPS,
                     ALU.mult, ALU.add, w=wwq, inc=("ve", 1))

    for pr in range(NP):
        qk_phi(pr, "g1")
        if pr + 2 < NP:
            load_wpair(wk_h, pr + 2, ww=(("pe", p.val("pe")),))
        dma_transpose_cmaj(qktm, kpfm[:, 2 * pr * TOK:(2 * pr + 2) * TOK], 32)
        sbank_free = [("ae", p.val("ae")), ("ae", p.val("ae"))]
        for c in range(NT):
            for h01 in range(2):
                h = 2 * pr + h01
                pbS = pb[6 + h01]
                ww = (sbank_free[h01],) + ((("ve", p.val("ve")),) if c == 0 else ())
                p.op("tensor", "matmul", pbS[:, 0:64],
                     qktm[:, h01 * TOK + c * 128:h01 * TOK + (c + 1) * 128],
                     vtm[:, c * 1024 + h * 64:c * 1024 + (h + 1) * 64],
                     start=(c == 0), stop=False, w=ww)
                p.op("tensor", "matmul", pbS[:, 64:65],
                     qktm[:, h01 * TOK + c * 128:h01 * TOK + (c + 1) * 128],
                     ones_c[:, :], start=False, stop=True, inc=("pe", 1))
            for h01 in range(2):
                h = 2 * pr + h01
                pbS = pb[6 + h01]
                if c < 15:
                    p.op("scalar", "copy", s_sb[:, (h * 15 + c) * 65:(h * 15 + c + 1) * 65],
                         pbS[:, 0:65], w=(("pe", p.val("pe")),), inc=("ae", 1))
                else:
                    p.op("scalar", "activation", ctrb[:, h * 65:(h + 1) * 65],
                         pbS[:, 0:65], AF.Copy, scale=scal[:, SCL_CM:SCL_CM + 1],
                         w=(("pe", p.val("pe")),), inc=("ae", 1))
                sbank_free[h01] = ("ae", p.val("ae"))
    p.dma("sync", "so", out=ccs_in, in_=ctrb[:, :],
          w=(("ae", p.val("ae")), ("cc", p.val("cc"))))
    p.op("gpsimd", "collective_compute", "AllReduce", ALU.add,
         [[0, 1], [2, 3], [4, 5], [6, 7]], [ccs_in], [ccs_out],
         w=(p.wdma("so"), p.wdma("c0")), inc=("cc", 1))
    p.dma("sync", "c0", out=scar[:, :], in_=ccs_out, w=(("cc", p.val("cc")),))
    p.op("vector", "tensor_scalar", scar[:, :], scar[:, :],
         scal[:, SCL_HM:SCL_HM + 1], None, ALU.mult, w=(p.wdma("c0"),), inc=("ve", 1))
    if N_PHASES <= 5:
        finish_dump_stream(dbg_src=s_sb[:, 0:15600] if DEBUG_TAP == "s_sb"
                           else kpfm[:, 0:NT * D])
        return nc
    p.flush()

    load_wpair(wq_h, 0)
    load_wpair(wq_h, 1)
    for pr in range(NP):
        qk_phi(pr, "g1")
        if pr + 2 < NP:
            load_wpair(wq_h, pr + 2, ww=(("pe", p.val("pe")),))
        dma_transpose_cmaj(qktm, qpfm, 32)
        WTQ = (p.wdma("t0"),)
        abank_free = [("ve", p.val("ve")), ("ve", p.val("ve"))]
        nbank_free = [("ve", p.val("ve")), ("ve", p.val("ve"))]
        for c in range(NT):
            for h01 in range(2):
                h = 2 * pr + h01
                p.op("tensor", "matmul", pb[h01][:, 0:128],
                     kpfm[:, h * TOK + c * 128:h * TOK + (c + 1) * 128],
                     qpfm[:, h01 * TOK + c * 128:h01 * TOK + (c + 1) * 128],
                     start=True, stop=True,
                     w=(abank_free[h01],) + (WTQ if c == 0 else ()), inc=("pe", 1))
                p.op("vector", "tensor_tensor", asb[:, h01, :], pb[h01][:, 0:128],
                     tri_f[:, :], ALU.mult, w=(("pe", p.val("pe")),), inc=("ve", 1))
                abank_free[h01] = ("ve", p.val("ve"))
            for h01 in range(2):
                h = 2 * pr + h01
                pbN = pb[2 + h01]
                p.op("tensor", "matmul", pbN[:, 0:64], asb[:, h01, :],
                     vtm[:, c * 1024 + h * 64:c * 1024 + (h + 1) * 64],
                     start=True, stop=False,
                     w=(nbank_free[h01], ("ve", p.val("ve"))))
                p.op("tensor", "matmul", pbN[:, 64:65], asb[:, h01, :], ones_c[:, :],
                     start=False, stop=False)
                if c > 0:
                    p.op("tensor", "matmul", pbN[:, 0:65],
                         qpfm[:, h01 * TOK + c * 128:h01 * TOK + (c + 1) * 128],
                         s_sb[:, (h * 15 + c - 1) * 65:(h * 15 + c) * 65],
                         start=False, stop=False)
                p.op("tensor", "matmul", pbN[:, 0:65],
                     qpfm[:, h01 * TOK + c * 128:h01 * TOK + (c + 1) * 128],
                     scar[:, h * 65:(h + 1) * 65], start=False, stop=True,
                     inc=("pe", 1))
                p.op("scalar", "activation", tiny[:, 14 + h01:15 + h01], pbN[:, 64:65],
                     AF.Ln, bias=feps_pp[:, :], w=(("pe", p.val("pe")),), inc=("ae", 1))
                p.op("scalar", "activation", tiny[:, 14 + h01:15 + h01], tiny[:, 14 + h01:15 + h01],
                     AF.Exp, scale=-1.0, inc=("ae", 1))
                p.op("vector", "tensor_scalar",
                     kpfm[:, 2 * pr * TOK + c * 128 + h01 * 64:
                          2 * pr * TOK + c * 128 + (h01 + 1) * 64],
                     pbN[:, 0:64], tiny[:, 14 + h01:15 + h01], None, ALU.mult,
                     w=(("ae", p.val("ae")),), inc=("ve", 1))
                nbank_free[h01] = ("ve", p.val("ve"))
    if N_PHASES <= 6:
        finish_dump_stream(dbg_src=kpfm[:, 0:NT * D])
        return nc
    p.flush()

    for pr in range(NP):
        dma_transpose_cmaj(kpfm[:, 2 * pr * TOK:(2 * pr + 1) * TOK],
                           afm[:, pr * TOK:(pr + 1) * TOK], 16)
    p.dma("gpsimd", "g0", out=wo_sb[:, :, :],
          in_=wo_h.rearrange("(k pp) j -> pp k j", pp=128))
    WT7 = (p.wdma("t0"), p.wdma("g0"))
    if N_PHASES == 65:
        finish_dump_stream(dbg_src=afm)
        return nc
    for t in range(NT):
        p.dma("sync", "c" + str(t % 2), out=wo_a[:, t % 2, :],
              in_=sc_a[t], w=(("ve", p.val("ve")),))
        for n in range(2):
            bank = (2 * t + n) % 4
            for k in range(ND):
                ww = (WT7 + (("ve", p.val("ve")),)) if k == 0 else ()
                p.op("tensor", "matmul", pb[bank][:, :],
                     afm[:, k * TOK + t * 128:k * TOK + (t + 1) * 128],
                     wo_sb[:, k, n * 512:(n + 1) * 512],
                     start=(k == 0), stop=(k == ND - 1), w=ww,
                     inc=("pe", 1) if k == ND - 1 else None)
            p.op("vector", "tensor_add", t2_sb[:, n * 512:(n + 1) * 512],
                 pb[bank][:, :], wo_a[:, t % 2, n * 512:(n + 1) * 512],
                 w=(("pe", p.val("pe")), p.wdma("c" + str(t % 2))), inc=("ve", 1))
        p.op("vector", "reduce_sum", tiny[:, 4:5], t2_sb[:, :], AX.X, inc=("ve", 1))
        p.op("scalar", "activation", sqd_sb[:, :], t2_sb[:, :], AF.Square,
             accum_out=tiny[:, 5:6], w=(("ve", p.val("ve")),), inc=("ae", 1))
        p.op("gpsimd", "tensor_scalar", tiny[:, 6:7], tiny[:, 4:5], -1.0 / D, None,
             ALU.mult, w=(("ve", p.val("ve")),), inc=("ge", 1))
        p.op("gpsimd", "tensor_scalar", tiny[:, 7:8], tiny[:, 5:6], 1.0 / D, LN1E6,
             ALU.mult, ALU.add, w=(("ae", p.val("ae")),), inc=("ge", 1))
        p.op("scalar", "activation", tiny[:, 8:9], tiny[:, 6:7], AF.Square,
             w=(("ge", p.val("ge")),), inc=("ae", 1))
        p.op("vector", "tensor_sub", tiny[:, 9:10], tiny[:, 7:8], tiny[:, 8:9],
             w=(("ge", p.val("ge")), ("ae", p.val("ae"))), inc=("ve", 1))
        p.op("scalar", "activation", tiny[:, 10:11], tiny[:, 9:10], AF.Ln,
             w=(("ve", p.val("ve")),), inc=("ae", 1))
        p.op("scalar", "activation", tiny[:, 10:11], tiny[:, 10:11], AF.Exp,
             scale=-0.5, inc=("ae", 1))
        p.op("vector", "tensor_scalar", sqd_sb[:, :], t2_sb[:, :], tiny[:, 6:7],
             tiny[:, 10:11], ALU.add, ALU.mult,
             w=(("ge", p.val("ge")), ("ae", p.val("ae"))), inc=("ve", 1))
        p.op("vector", "tensor_tensor", t2_sb[:, :], sqd_sb[:, :], gam_b[:, :],
             ALU.mult, inc=("ve", 1))
        p.op("vector", "tensor_tensor",
             h2tm[:, :].rearrange("p (c f) -> p c f", f=TOK)[:, :, t * 128:(t + 1) * 128],
             t2_sb[:, :].rearrange("p (c f) -> p c f", f=128),
             bet_b[:, :].rearrange("p (c f) -> p c f", f=128), ALU.add, inc=("ve", 1))
    dma_transpose_cmaj(h2tm, h2fm, NT * ND)
    if N_PHASES <= 7:
        finish_dump_stream(dbg_src=afm if DEBUG_TAP == "afm" else h2fm)
        return nc
    p.flush()

    def w1_load(q4, f):
        gs = "g" + str(2 + f % 2)
        p.dma("gpsimd", gs, out=w1c_sb[:, f % 2, :],
              in_=w1_h.rearrange("(k pp) (ff c) -> pp k ff c", pp=128, c=128)
              [:, :, f, :], w=(("pe", p.val("pe")),))

    def w2_load(q4, fh):
        for fj in range(16):
            f = fh * 16 + fj
            p.dma("gpsimd", "g" + str(fh % 2),
                  out=w2h_sb[:, fh % 2, fj, :], in_=w2_h[f * 128:(f + 1) * 128, :],
                  w=(("pe", p.val("pe")),) if fj == 0 else ())

    w2_load(0, 0)
    for q4 in range(4):
        w1_load(q4, 0)
        w1_load(q4, 1)
        for f in range(NF):
            gs = "g" + str(2 + f % 2)
            WL1 = (p.wdma(gs), ("ae", p.val("ae")), p.wdma("t0"), ("ve", p.val("ve")))
            for k in range(ND):
                p.op("tensor", "matmul", pb[f % 4][:, :],
                     w1c_sb[:, f % 2, k * 128:(k + 1) * 128],
                     h2fm[:, k * TOK + q4 * 512:k * TOK + (q4 + 1) * 512],
                     start=(k == 0), stop=(k == ND - 1), w=WL1 if k == 0 else (),
                     inc=("pe", 1) if k == ND - 1 else None)
            if f + 2 < NF:
                w1_load(q4, f + 2)
            p.op("scalar", "activation", rfm[:, f * 512:(f + 1) * 512], pb[f % 4][:, :],
                 AF.Relu, bias=b1_pp[:, f:f + 1], w=(("pe", p.val("pe")),), inc=("ae", 1))
        for fh in range(2):
            if fh == 0:
                w2_load(q4, 1)
            elif q4 < 3:
                w2_load(q4 + 1, 0)
            WL2 = (p.wdma("g" + str(fh % 2)), ("ae", p.val("ae")), ("ve", p.val("ve")))
            for tq4 in range(4):
                for n in range(2):
                    for fj in range(16):
                        f = fh * 16 + fj
                        p.op("tensor", "matmul", pb[tq4 * 2 + n][:, :],
                             rfm[:, f * 512 + tq4 * 128:f * 512 + (tq4 + 1) * 128],
                             w2h_sb[:, fh % 2, fj, n * 512:(n + 1) * 512],
                             start=(fj == 0), stop=(fj == 15), w=WL2 if fj == 0 else (),
                             inc=("pe", 1) if fj == 15 else None)
                    if fh == 0:
                        p.op("vector", "tensor_copy", pacc_sb[:, tq4, n * 512:(n + 1) * 512],
                             pb[tq4 * 2 + n][:, :], w=(("pe", p.val("pe")),),
                             inc=("ve", 1))
        PEQ = (("pe", p.val("pe")),)
        for tq4 in range(4):
            t = 4 * q4 + tq4
            p.dma("sync", "c" + str(t % 2), out=x1t_sb[:, :], in_=sc_x1[t],
                  w=(("ve", p.val("ve")),))
            for n in range(2):
                wwe = PEQ
                if n == 0:
                    wwe = wwe + (p.wdma("c" + str(t % 2)), p.wdma("so"))
                p.op("vector", "tensor_add", x2t_sb[:, n * 512:(n + 1) * 512],
                     pb[tq4 * 2 + n][:, :], pacc_sb[:, tq4, n * 512:(n + 1) * 512],
                     w=wwe, inc=("ve", 1))
            p.op("vector", "tensor_tensor", x2t_sb[:, :], x2t_sb[:, :], b2_b[:, :],
                 ALU.add, inc=("ve", 1))
            p.op("vector", "tensor_add", x2t_sb[:, :], x2t_sb[:, :],
                 x1t_sb[:, :], inc=("ve", 1))
            p.op("vector", "reduce_sum", stats[:, ST_SUM + t:ST_SUM + t + 1],
                 x2t_sb[:, :], AX.X, inc=("ve", 1))
            p.op("scalar", "activation", sqd_sb[:, :], x2t_sb[:, :], AF.Square,
                 accum_out=stats[:, ST_SSQ + t:ST_SSQ + t + 1],
                 w=(("ve", p.val("ve")),), inc=("ae", 1))
            p.dma("sync", "so", out=sc_x2[t], in_=x2t_sb[:, :],
                  w=(("ae", p.val("ae")),))
    W3 = (("ve", p.val("ve")), ("ae", p.val("ae")), ("ge", p.val("ge")))
    sU3, sQ3 = stats[:, ST_SUM:ST_SUM + 16], stats[:, ST_SSQ:ST_SSQ + 16]
    nmu3, rsq3 = stats[:, ST_NMU:ST_NMU + 16], stats[:, ST_RSQ:ST_RSQ + 16]
    t13 = stats[:, ST_SPP:ST_SPP + 16]
    t23 = stats[:, ST_M2:ST_M2 + 16]
    p.op("gpsimd", "tensor_scalar", nmu3, sU3, -1.0 / D, None, ALU.mult, w=W3, inc=("ge", 1))
    p.op("gpsimd", "tensor_scalar", t13, sQ3, 1.0 / D, LN_EPS, ALU.mult, ALU.add, inc=("ge", 1))
    p.op("scalar", "activation", t23, nmu3, AF.Square, w=(("ge", p.val("ge")),), inc=("ae", 1))
    p.op("vector", "tensor_sub", t13, t13, t23,
         w=(("ge", p.val("ge")), ("ae", p.val("ae"))), inc=("ve", 1))
    p.op("scalar", "activation", t23, t13, AF.Ln, w=(("ve", p.val("ve")),), inc=("ae", 1))
    p.op("scalar", "activation", rsq3, t23, AF.Exp, scale=-0.5, inc=("ae", 1))
    for t in range(NT):
        p.dma("sync", "c" + str(t % 2), out=x2t_sb[:, :] if t % 2 == 0 else x1t_sb[:, :],
              in_=sc_x2[t],
              w=(p.wdma("so"), ("ve", p.val("ve"))) if t == 0 else (("ve", p.val("ve")),))
        src = x2t_sb if t % 2 == 0 else x1t_sb
        dstl = xlnt_sb if t % 2 == 0 else sqd_sb
        p.op("vector", "tensor_scalar", dstl[:, :], src[:, :],
             stats[:, ST_NMU + t:ST_NMU + t + 1], stats[:, ST_RSQ + t:ST_RSQ + t + 1],
             ALU.add, ALU.mult,
             w=(p.wdma("c" + str(t % 2)), p.wdma("so")) +
               ((("ae", p.val("ae")), ("ge", p.val("ge"))) if t == 0 else ()),
             inc=("ve", 1))
        p.op("vector", "tensor_reduce", stats[:, t:t + 1], dstl[:, :],
             AX.X, ALU.max, apply_absolute_value=True, inc=("ve", 1))
        p.dma("sync", "so", out=sc_x2ln[t], in_=dstl[:, :],
              w=(("ve", p.val("ve")),))
    p.op("scalar", "copy", stats[:, 32:48], stats[:, 0:16],
         w=(("ve", p.val("ve")),), inc=("ae", 1))
    p.op("vector", "reduce_max", tiny[:, 12:13], stats[:, 32:48], AX.X,
         w=(("ae", p.val("ae")),), inc=("ve", 1))
    global_absmax_chain(tiny[:, 12:13], SCL_SGX, SCL_SX, SCL_RX)
    if N_PHASES <= 8:
        finish_dump_stream(dbg_src=rfm if DEBUG_TAP == "rfm" else h2fm)
        return nc
    p.flush()

    p.dma("sync", "c0", out=x_sb[:, :].rearrange("p (t d) -> p t d", d=D),
          in_=sc_x2.transpose([1, 0, 2]), w=(p.wdma("so"),))
    def xl_load(t):
        p.dma("sync", "h" + str(t % 4), out=xlrd_sb[:, t % 4, :], in_=sc_x2ln[t],
              w=(("ve", p.val("ve")),) if t >= 4 else (p.wdma("so"),))
    for t in range(4):
        xl_load(t)
    for t in range(NT):
        bi = t % 4
        WRD = (p.wdma("h" + str(t % 4)),)
        buf = t % 4
        p.op("vector", "tensor_scalar", u1_sb[:, buf, :], xlrd_sb[:, bi, :],
             scal[:, SCL_RX:SCL_RX + 1], None, ALU.mult,
             w=WRD + ((("ae", p.val("ae")), ("ge", p.val("ge"))) if t == 0 else ()),
             inc=("ve", 1))
        p.op("scalar", "add", u1_sb[:, buf, :], u1_sb[:, buf, :], pmag_pp[:, :],
             w=(("ve", p.val("ve")),), inc=("ae", 1))
        p.op("vector", "tensor_scalar", u1_sb[:, buf, :], u1_sb[:, buf, :],
             -MAGIC, -QMAX - 1.0, ALU.add, ALU.max, w=(("ae", p.val("ae")),),
             inc=("ve", 1))
        p.op("vector", "tensor_scalar",
             xqtm[:, :].rearrange("p (c f) -> p c f", f=TOK)[:, :, t * 128:(t + 1) * 128],
             u1_sb[:, buf, :].rearrange("p (c f) -> p c f", f=128),
             QMAX, None, ALU.min, inc=("ve", 1))
        if t + 4 < NT:
            xl_load(t + 4)
    dma_transpose_cmaj(xqtm, xqfm, NT * ND)
    if N_PHASES <= 9:
        finish_dump_stream(dbg_src=xqfm)
        return nc
    p.flush()

    ffn_front(1)
    if N_PHASES <= 10:
        finish_dump_stream(); return nc
    p.flush()
    ffn_down(1)
    p.flush()

    p.op("sync", "nop", w=(p.wdma("so"),))
    p.flush()
    return nc


_CACHE = {}


def kernel(**inputs):
    key = N_PHASES
    if key not in _CACHE:
        _CACHE[key] = build_nc()
    nc = _CACHE[key]

    f32 = lambda a: np.ascontiguousarray(np.asarray(a, dtype=np.float32))
    shared = {
        "Wq": f32(inputs["L0_attn_Wq"]), "Wk": f32(inputs["L0_attn_Wk"]),
        "Wv": f32(inputs["L0_attn_Wv"]), "Wo": f32(inputs["L0_attn_Wo"]),
        "omega": f32(inputs["L0_attn_omega"]),
        "W1": f32(inputs["L0_attn_W1"]),
        "b1": f32(inputs["L0_attn_b1"]).reshape(32, 128),
        "W2": f32(inputs["L0_attn_W2"]),
        "b2": f32(inputs["L0_attn_b2"]).reshape(1, D),
        "gamma": f32(inputs["L0_attn_gamma"]).reshape(1, D),
        "beta": f32(inputs["L0_attn_beta"]).reshape(1, D),
        "clip_x": f32(inputs["L0_ffn_clip_x"]).reshape(1, 1),
        "clip_down": f32(inputs["L0_ffn_clip_down"]).reshape(1, 1),
        "w_gate": f32(inputs["L0_ffn_w_gate"]),
        "w_up": f32(inputs["L0_ffn_w_up"]),
        "w_down": f32(inputs["L0_ffn_w_down"]),
    }
    xb = f32(inputs["L0_attn_batch"])
    in_maps = []
    for c in range(8):
        b, half = c // 2, c % 2
        m = dict(shared)
        m["x0"] = np.ascontiguousarray(xb[b, half * TOK:(half + 1) * TOK, :])
        m["hmask"] = np.full((1, 1), float(half == 1), np.float32)
        m["cmask"] = np.full((1, 1), float(half == 0), np.float32)
        in_maps.append(m)

    res = run_bass_kernel_spmd(nc, in_maps, core_ids=list(range(8)))
    kernel._LAST = res
    out = np.empty((B, S, D), np.float32)
    for c in range(8):
        b, half = c // 2, c % 2
        out[b, half * TOK:(half + 1) * TOK, :] = res.results[c]["out"]
    return out
